# version 08
# baseline (speedup 1.0000x reference)
import math
from contextlib import ExitStack

import numpy as np
import ml_dtypes
import concourse.bass as bass
import concourse.mybir as mybir
from concourse.bass_utils import run_bass_kernel_spmd

F32 = mybir.dt.float32
BF16 = mybir.dt.bfloat16
I32 = mybir.dt.int32
AF = mybir.ActivationFunctionType
ALU = mybir.AluOpType
AX = mybir.AxisListType

D = 4096
S = 8192
NCORES = 8
H = 32
KVH = 8
DFF = 11008
FT = DFF // 128
TOK = 1024
NCH = D // 128
ISQ = 1.0 / math.sqrt(128.0)
BF16NP = ml_dtypes.bfloat16
ENGS = ["sync", "gpsimd", "tensor", "scalar", "vector"]



def attn_pairs():
    return [(0, b) for b in range(4)] + [(1, b) for b in range(8)]


def attn_groups_of_pair(g, b):
    jlo = b if g == 0 else max(b, 4)
    jhi = 4 * g + 3
    N = (jhi - jlo + 1) * 128
    tpg = {512: 2, 384: 2, 256: 4, 128: 8}[N]
    slot = 512 if N >= 384 else N
    return [dict(tiles=list(range(s0, s0 + tpg)), slot=slot,
                 span=(tpg - 1) * slot + N, N=N, jlo=jlo, jhi=jhi)
            for s0 in range(0, 8, tpg)]


class Rec:

    def __init__(self, C):
        self.ops = {e: [] for e in ENGS}
        self.C = C

    def emit(self, eng, fn, waits=(), inc=None, inc_by=1):
        ws = [(s, v) for (s, v) in waits if v > 0]
        self.ops[eng].append((fn, ws, inc, inc_by))
        if inc is not None:
            self.C[inc] += inc_by
        return None if inc is None else (inc, self.C[inc])

    def realize(self, nc, sems):
        with nc.Block() as block:
            for ename in ENGS:
                if not self.ops[ename]:
                    continue

                def mk(ename):
                    def body(eng):
                        for fn, ws, inc, inc_by in self.ops[ename]:
                            for (s, v) in ws:
                                eng.wait_ge(sems[s], v)
                            ins = fn(eng)
                            if inc is not None:
                                ins.then_inc(sems[inc], inc_by)
                    return body
                getattr(block, ename)(mk(ename))


def build_program():
    nc = bass.Bass("TRN2", target_bir_lowering=False)

    xqT = nc.dram_tensor("xqT", [D, TOK], F32, kind="ExternalInput").ap()
    xq = nc.dram_tensor("xq", [TOK, D], F32, kind="ExternalInput").ap()
    xkvT = nc.dram_tensor("xkvT", [D, TOK], F32, kind="ExternalInput").ap()
    wqT = nc.dram_tensor("wqT", [D, D], F32, kind="ExternalInput").ap()
    wkT = nc.dram_tensor("wkT", [D, KVH * 128], F32, kind="ExternalInput").ap()
    wvT = nc.dram_tensor("wvT", [D, KVH * 128], F32, kind="ExternalInput").ap()
    woT = nc.dram_tensor("woT", [D, D], F32, kind="ExternalInput").ap()
    gwT = nc.dram_tensor("gwT", [D, DFF], I32, kind="ExternalInput").ap()
    uwT = nc.dram_tensor("uwT", [D, DFF], I32, kind="ExternalInput").ap()
    dwT = nc.dram_tensor("dwT", [DFF, D], I32, kind="ExternalInput").ap()
    gs_in = nc.dram_tensor("gs", [128, FT], F32, kind="ExternalInput").ap()
    us_in = nc.dram_tensor("us", [128, FT], F32, kind="ExternalInput").ap()
    ds_in = nc.dram_tensor("ds", [128, NCH], F32, kind="ExternalInput").ap()
    cosq_in = nc.dram_tensor("cosq", [128, TOK], F32, kind="ExternalInput").ap()
    sinq_in = nc.dram_tensor("sinq", [128, TOK], F32, kind="ExternalInput").ap()
    coskv_in = nc.dram_tensor("coskv", [128, TOK], F32, kind="ExternalInput").ap()
    sinkv_in = nc.dram_tensor("sinkv", [128, TOK], F32, kind="ExternalInput").ap()
    rotm_in = nc.dram_tensor("rotm", [128, 128], BF16, kind="ExternalInput").ap()
    masks_in = nc.dram_tensor("masks", [128, 1024], BF16, kind="ExternalInput").ap()
    exp8_in = nc.dram_tensor("exp8", [8, 32], F32, kind="ExternalInput").ap()
    ident_in = nc.dram_tensor("ident", [128, 128], BF16, kind="ExternalInput").ap()
    identf_in = nc.dram_tensor("identf", [128, 128], F32, kind="ExternalInput").ap()
    ones_in = nc.dram_tensor("onesi", [128, 128], F32, kind="ExternalInput").ap()
    out = nc.dram_tensor("out", [TOK, D], F32, kind="ExternalOutput").ap()

    kc = nc.dram_tensor("kc", [1032, 1024], BF16)
    vc = nc.dram_tensor("vc", [1024, 1024], BF16)
    kg = nc.dram_tensor("kg", [NCORES * 1032, 1024], BF16)
    vg = nc.dram_tensor("vg", [NCORES * 1024, 1024], BF16)
    atnd = nc.dram_tensor("atnd", [D, TOK], BF16)
    r2d = nc.dram_tensor("r2d", [TOK, D], F32)
    xt2d = nc.dram_tensor("xt2d", [D, TOK], BF16)
    midd = nc.dram_tensor("midd", [DFF, TOK], BF16)

    glob = ExitStack()

    def sb(name, shape, dtype):
        return glob.enter_context(nc.sbuf_tensor(name, shape, dtype))

    ident = sb("identt", [128, 128], BF16)
    rotmb = sb("rotmbt", [128, 128], BF16)
    identf = sb("identft", [128, 128], F32)
    onesf = sb("onesft", [128, 128], F32)
    onesb = sb("onesbt", [128, 1], BF16)
    masks = sb("maskst", [128, 1024], BF16)
    gsb = sb("gsbt", [128, FT], F32)
    usb = sb("usbt", [128, FT], F32)
    dsb = sb("dsbt", [128, NCH], F32)
    btbl = sb("btblt", [128, 32], F32)
    exp8b = sb("exp8bt", [8, 32], F32)
    statrow = sb("statrowt", [1, 8192], BF16)
    nqrow = sb("nqrowt", [1, 1024], F32)
    q2row = sb("q2rowt", [1, 32], F32)
    k2row = sb("k2rowt", [1, 128], F32)
    srow = sb("srowt", [1, 32], F32)
    srow2 = sb("srow2t", [1, 32], F32)
    epsA = sb("epsAt", [128, 1], F32)

    mkr = sb("mkrt", [1, 16], F32)

    sems = {}
    for name in ["dG", "dG0", "dG1", "dS", "dS0", "dS1", "dS2", "dS3", "dO",
                 "cc", "ccd", "sPs", "sPv", "sA", "sV", "sF1", "sT", "sF2"]:
        sems[name] = glob.enter_context(nc.semaphore(name))
    C = {k: 0 for k in sems}

    def snap(sem):
        return (sem, C[sem])

    def ln_chain(r, stat, B, a_list, sqscr, first_waits=()):
        SA = stat[:, 0 * B:1 * B]
        SB = stat[:, 1 * B:2 * B]
        SC = stat[:, 2 * B:3 * B]
        SD = stat[:, 3 * B:4 * B]
        SE = stat[:, 4 * B:5 * B]
        SR = stat[:, 5 * B:6 * B]
        SF = stat[:, 6 * B:7 * B]

        def v1(v):
            ins = None
            for t, a in enumerate(a_list):
                v.reduce_sum(SA[:, t:t + 1], a, axis=AX.X)
                ins = v.scalar_tensor_tensor(
                    sqscr[:, :], a, 1.0, a, ALU.mult, ALU.mult,
                    accum_out=SB[:, t:t + 1])
            return ins
        r.emit("vector", v1, waits=list(first_waits), inc="sV")
        r.emit("scalar", lambda sc: sc.activation(
            SC, SA, AF.Square, bias=0.0, scale=1.0 / 4096.0),
            waits=[snap("sV")], inc="sA")
        r.emit("vector", lambda v: v.scalar_tensor_tensor(
            SD, SB, 1.0 / 4096.0, SC, ALU.mult, ALU.subtract),
            waits=[snap("sA")], inc="sV")
        r.emit("scalar", lambda sc: sc.activation(
            SE, SD, AF.Ln, bias=epsA[:, 0:1], scale=1.0),
            waits=[snap("sV")], inc="sA")
        r.emit("scalar", lambda sc: sc.activation(
            SR, SE, AF.Exp, bias=0.0, scale=-0.5),
            waits=[snap("sA")], inc="sA")
        r.emit("vector", lambda v: v.scalar_tensor_tensor(
            SF, SA, -1.0 / 4096.0, SR, ALU.mult, ALU.mult),
            waits=[snap("sA")], inc="sV")
        return SR, SF, snap("sV")

    r = Rec(C)
    for dst, src in [(ident, ident_in), (identf, identf_in), (onesf, ones_in),
                     (masks, masks_in), (gsb, gs_in), (usb, us_in), (dsb, ds_in),
                     (rotmb, rotm_in), (exp8b, exp8_in)]:
        r.emit("gpsimd", lambda g, d=dst, s=src: g.dma_start(out=d[:, :], in_=s),
               inc="dG", inc_by=16)
    r.emit("gpsimd", lambda g: g.wait_ge(sems["dG"], C["dG"]))
    r.emit("vector", lambda v: v.memset(onesb[:, :], 1.0))
    r.emit("vector", lambda v: v.memset(epsA[:, :], 1e-5))
    r.realize(nc, sems)

    stA = ExitStack()
    xkvb = stA.enter_context(nc.sbuf_tensor("xkvb", [128, NCH * 1024], BF16))
    ckv = stA.enter_context(nc.sbuf_tensor("ckv", [128, 2 * TOK], F32))
    qraw = stA.enter_context(nc.sbuf_tensor("qraw", [128, 4 * 1024], BF16))
    rtmp = stA.enter_context(nc.sbuf_tensor("rtmp", [128, 1024], F32))
    kfull = stA.enter_context(nc.sbuf_tensor("kfull", [128, KVH * 1024], BF16))
    vfull = stA.enter_context(nc.sbuf_tensor("vfull", [128, KVH * 1024], BF16))
    vtb = stA.enter_context(nc.sbuf_tensor("vtb", [128, 8 * 1024], BF16))
    wbufA = stA.enter_context(nc.sbuf_tensor("wbufA", [128, 2 * 512], BF16))
    sqbA = stA.enter_context(nc.sbuf_tensor("sqbA", [128, 1024], BF16))

    r = Rec(C)
    r.emit("gpsimd", lambda g: g.dma_start(
        out=xkvb[:, :].rearrange("p (c t) -> p c t", t=TOK),
        in_=xkvT.rearrange("(c p) t -> p c t", p=128)), inc="dG", inc_by=16)
    r.emit("gpsimd", lambda g: g.dma_start(out=ckv[:, 0:TOK], in_=coskv_in),
           inc="dG", inc_by=16)
    r.emit("gpsimd", lambda g: g.dma_start(out=ckv[:, TOK:2 * TOK], in_=sinkv_in),
           inc="dG", inc_by=16)
    r.emit("gpsimd", lambda g: g.wait_ge(sems["dG"], C["dG"]))
    r.realize(nc, sems)

    def proj_block(which, og, wT, dst, cos_sin, xb):
        stP = ExitStack()
        kps = [stP.enter_context(nc.psum_tensor(f"ps{which}{og}_{i}", [128, 1024], F32))
               for i in range(4)]
        r = Rec(C)
        guard = {}
        xload = snap("dG")
        prevdone = [snap("sV"), snap("sA")]
        for c in range(NCH):
            w = [xload] + guard.pop(("w", c % 2), [])
            wl = "dG0" if c % 2 == 0 else "dG1"
            r.emit("gpsimd", lambda g, c=c: g.dma_start(
                out=wbufA[:, (c % 2) * 512:(c % 2) * 512 + 512],
                in_=wT[c * 128:(c + 1) * 128, og * 512:(og + 1) * 512]),
                waits=w, inc=wl, inc_by=16)
            wdone = snap(wl)

            def mmgrp(t, c=c):
                ins = None
                for hh in range(4):
                    lt = wbufA[:, (c % 2) * 512 + hh * 128:(c % 2) * 512 + hh * 128 + 128]
                    for half in range(2):
                        ins = t.matmul(
                            kps[hh][:, half * 512:half * 512 + 512], lt,
                            xb[:, c * 1024 + half * 512:c * 1024 + half * 512 + 512],
                            start=(c == 0), stop=(c == NCH - 1))
                return ins
            r.emit("tensor", mmgrp,
                   waits=[wdone] + (prevdone if c == 0 else []), inc="sPs")
            guard[("w", c % 2)] = [snap("sPs")]
        mmdone = snap("sPs")
        if which in ("k", "q"):
            cos = cos_sin[:, 0:TOK]
            sin = cos_sin[:, TOK:2 * TOK]
            for hh in range(4):
                r.emit("scalar", lambda sc, hh=hh: sc.copy(
                    qraw[:, hh * 1024:(hh + 1) * 1024], kps[hh][:, :]),
                    waits=[mmdone] if hh == 0 else (), inc="sA")
            evdone = snap("sA")
            for hh in range(4):
                dcol = (og * 4 + hh) * 1024

                def rotmm(t, hh=hh):
                    ins = None
                    for half in range(2):
                        ins = t.matmul(kps[hh][:, half * 512:half * 512 + 512],
                                       rotmb[:, :],
                                       qraw[:, hh * 1024 + half * 512:
                                            hh * 1024 + half * 512 + 512],
                                       start=True, stop=True)
                    return ins
                r.emit("tensor", rotmm, waits=[evdone] if hh == 0 else (), inc="sPs")
                rotdone = snap("sPs")

                def rope(v, hh=hh, dcol=dcol):
                    v.tensor_mul(rtmp[:, :], qraw[:, hh * 1024:(hh + 1) * 1024], cos)
                    v.tensor_mul(sqbA[:, :], kps[hh][:, :], sin)
                    return v.tensor_add(dst[:, dcol:dcol + 1024], rtmp[:, :], sqbA[:, :])
                r.emit("vector", rope, waits=[rotdone], inc="sV")
        else:
            for hh in range(4):
                dcol = (og * 4 + hh) * 1024
                r.emit("scalar",
                       lambda sc, hh=hh, dcol=dcol: sc.copy(
                           dst[:, dcol:dcol + 1024], kps[hh][:, :]),
                       waits=[mmdone] if hh == 0 else (), inc="sA")
        r.realize(nc, sems)
        stP.close()

    for og in range(2):
        proj_block("k", og, wkT, kfull, ckv, xkvb)
    for og in range(2):
        proj_block("v", og, wvT, vfull, ckv, xkvb)

    stP = ExitStack()
    vtp = [stP.enter_context(nc.psum_tensor(f"vtp{i}", [128, 128], BF16)) for i in range(2)]
    stat_ps = stP.enter_context(nc.psum_tensor("statps", [1, 1024], F32))
    r = Rec(C)
    guard = {}
    for n in range(64):
        h, tt = divmod(n, 8)
        r.emit("tensor", lambda t, h=h, tt=tt, n=n: t.transpose(
            vtp[n % 2][:, :], vfull[:, h * 1024 + tt * 128:h * 1024 + tt * 128 + 128],
            ident[:, :]),
            waits=guard.pop(("vtp", n % 2), []) + ([snap("sA")] if n == 0 else []),
            inc="sPs")
        w = snap("sPs")
        r.emit("vector", lambda v, h=h, tt=tt, n=n: v.tensor_copy(
            vtb[:, tt * 1024 + h * 128:tt * 1024 + h * 128 + 128], vtp[n % 2][:, :]),
            waits=[w], inc="sV")
        guard[("vtp", n % 2)] = [snap("sV")]
    for h in range(KVH):
        r.emit("vector", lambda v, h=h: v.tensor_mul(
            sqbA[:, :], kfull[:, h * 1024:(h + 1) * 1024], kfull[:, h * 1024:(h + 1) * 1024]),
            waits=guard.pop("sqfree", []), inc="sV")
        w = snap("sV")
        last = None
        for half in range(2):
            last = r.emit("tensor", lambda t, half=half: t.matmul(
                stat_ps[0:1, half * 512:half * 512 + 512], onesb[:, 0:1],
                sqbA[:, half * 512:half * 512 + 512], start=True, stop=True),
                waits=[w] + (guard.pop("statps", []) if half == 0 else []), inc="sPs")
        guard["sqfree"] = [snap("sPs")]
        r.emit("scalar", lambda sc, h=h: sc.copy(
            statrow[0:1, h * 1024:(h + 1) * 1024], stat_ps[0:1, :]),
            waits=[snap("sPs")], inc="sA")
        guard["statps"] = [snap("sA")]
    r.realize(nc, sems)
    stP.close()

    r = Rec(C)
    r.emit("sync", lambda s: s.dma_start(
        out=kc.ap()[0:1024, :].rearrange("(h p) t -> p h t", p=128),
        in_=kfull[:, :].rearrange("p (h t) -> p h t", t=1024)),
        waits=[snap("sV"), snap("sA")], inc="dS", inc_by=16)
    r.emit("sync", lambda s: s.dma_start(
        out=kc.ap()[1024:1032, :].rearrange("(o h) t -> o h t", o=1),
        in_=statrow[0:1, :].rearrange("o (h t) -> o h t", t=1024)), inc="dS", inc_by=16)
    r.emit("sync", lambda s: s.dma_start(
        out=vc.ap()[:, :].rearrange("(tt p) o -> p tt o", p=128),
        in_=vtb[:, :].rearrange("p (tt o) -> p tt o", o=1024)), inc="dS", inc_by=16)
    r.emit("sync", lambda s: s.wait_ge(sems["dS"], C["dS"]))
    r.realize(nc, sems)
    stA.close()

    stQA = ExitStack()
    qfull = stQA.enter_context(nc.sbuf_tensor("qfull", [128, H * 1024], BF16))
    stQ = ExitStack()
    xqb = stQ.enter_context(nc.sbuf_tensor("xqb", [128, NCH * 1024], BF16))
    cq = stQ.enter_context(nc.sbuf_tensor("cq", [128, 2 * TOK], F32))
    qrawQ = stQ.enter_context(nc.sbuf_tensor("qrawQ", [128, 4 * 1024], BF16))
    rtmpQ = stQ.enter_context(nc.sbuf_tensor("rtmpQ", [128, 1024], F32))
    sq2 = stQ.enter_context(nc.sbuf_tensor("sq2", [128, 1024], BF16))
    wbufQ = stQ.enter_context(nc.sbuf_tensor("wbufQ", [128, 2 * 512], BF16))
    wstgQ = stQ.enter_context(nc.sbuf_tensor("wstgQ", [128, 2 * 512], F32))

    r = Rec(C)
    r.emit("gpsimd", lambda g: g.dma_start(
        out=xqb[:, :].rearrange("p (c t) -> p c t", t=TOK),
        in_=xqT.rearrange("(c p) t -> p c t", p=128)), inc="dG", inc_by=16)
    r.emit("gpsimd", lambda g: g.dma_start(out=cq[:, 0:TOK], in_=cosq_in),
           inc="dG", inc_by=16)
    r.emit("gpsimd", lambda g: g.dma_start(out=cq[:, TOK:2 * TOK], in_=sinq_in),
           inc="dG", inc_by=16)
    r.realize(nc, sems)

    r = Rec(C)
    r.emit("gpsimd", lambda g: g.wait_ge(sems["dS"], C["dS"]))
    r.emit("gpsimd", lambda g: g.collective_compute(
        "AllGather", ALU.bypass, replica_groups=[list(range(NCORES))],
        ins=[kc.ap().opt()], outs=[kg.ap().opt()]), inc="cc")
    r.emit("gpsimd", lambda g: g.collective_compute(
        "AllGather", ALU.bypass, replica_groups=[list(range(NCORES))],
        ins=[vc.ap().opt()], outs=[vg.ap().opt()]), inc="cc")
    r.emit("gpsimd", lambda g: g.dma_start(out=mkr[0:1, :], in_=onesf[0:1, 0:16]),
           inc="ccd", inc_by=16)
    r.realize(nc, sems)

    qguard = {}

    def q_block(og):
        stP = ExitStack()
        kps = [stP.enter_context(nc.psum_tensor(f"qps{og}_{i}", [128, 1024], F32))
               for i in range(4)]
        r = Rec(C)
        guard = qguard
        xload = snap("dG")
        prevdone = [snap("sV"), snap("sA")]
        for c in range(NCH):
            w = guard.pop(("ws", c % 2), [])
            wl = "dS0" if c % 2 == 0 else "dS1"
            r.emit("sync", lambda s, c=c: s.dma_start(
                out=wstgQ[:, (c % 2) * 512:(c % 2) * 512 + 512],
                in_=wqT[c * 128:(c + 1) * 128, og * 512:(og + 1) * 512]),
                waits=w, inc=wl, inc_by=16)
            r.emit("vector", lambda v, c=c: v.tensor_copy(
                wbufQ[:, (c % 2) * 512:(c % 2) * 512 + 512],
                wstgQ[:, (c % 2) * 512:(c % 2) * 512 + 512]),
                waits=[snap(wl)] + guard.pop(("w", c % 2), []), inc="sV")
            guard[("ws", c % 2)] = [snap("sV")]
            wdone = snap("sV")

            def mmgrp(t, c=c):
                ins = None
                for hh in range(4):
                    lt = wbufQ[:, (c % 2) * 512 + hh * 128:(c % 2) * 512 + hh * 128 + 128]
                    for half in range(2):
                        ins = t.matmul(
                            kps[hh][:, half * 512:half * 512 + 512], lt,
                            xqb[:, c * 1024 + half * 512:c * 1024 + half * 512 + 512],
                            start=(c == 0), stop=(c == NCH - 1))
                return ins
            r.emit("tensor", mmgrp,
                   waits=[wdone] + ([xload] if c == 0 else []) + (prevdone if c == 0 else []),
                   inc="sPs")
            guard[("w", c % 2)] = [snap("sPs")]
        mmdone = snap("sPs")
        cos = cq[:, 0:TOK]
        sin = cq[:, TOK:2 * TOK]
        for hh in range(4):
            r.emit("scalar", lambda sc, hh=hh: sc.copy(
                qrawQ[:, hh * 1024:(hh + 1) * 1024], kps[hh][:, :]),
                waits=[mmdone] if hh == 0 else (), inc="sA")
        evdone = snap("sA")
        for hh in range(4):
            dcol = (og * 4 + hh) * 1024

            def rotmm(t, hh=hh):
                ins = None
                for half in range(2):
                    ins = t.matmul(kps[hh][:, half * 512:half * 512 + 512],
                                   rotmb[:, :],
                                   qrawQ[:, hh * 1024 + half * 512:
                                        hh * 1024 + half * 512 + 512],
                                   start=True, stop=True)
                return ins
            r.emit("tensor", rotmm, waits=[evdone] if hh == 0 else (), inc="sPs")
            rotdone = snap("sPs")

            def rope(v, hh=hh, dcol=dcol):
                v.tensor_mul(rtmpQ[:, :], qrawQ[:, hh * 1024:(hh + 1) * 1024], cos)
                v.tensor_mul(sq2[:, :], kps[hh][:, :], sin)
                return v.tensor_add(qfull[:, dcol:dcol + 1024], rtmpQ[:, :], sq2[:, :])
            r.emit("vector", rope, waits=[rotdone], inc="sV")
        r.realize(nc, sems)
        stP.close()

    for og in range(8):
        q_block(og)

    stP = ExitStack()
    statq = stP.enter_context(nc.psum_tensor("statq", [1, 1024], F32))
    r = Rec(C)
    guard = {}
    for h in range(H):
        r.emit("vector", lambda v, h=h: v.tensor_mul(
            sq2[:, :], qfull[:, h * 1024:(h + 1) * 1024], qfull[:, h * 1024:(h + 1) * 1024]),
            waits=guard.pop("sqfree", []), inc="sV")
        w = snap("sV")
        for half in range(2):
            r.emit("tensor", lambda t, half=half: t.matmul(
                statq[0:1, half * 512:half * 512 + 512], onesb[:, 0:1],
                sq2[:, half * 512:half * 512 + 512], start=True, stop=True),
                waits=[w] + (guard.pop("statps", []) if half == 0 else []), inc="sPs")
        guard["sqfree"] = [snap("sPs")]
        r.emit("scalar", lambda sc: sc.copy(nqrow[0:1, :], statq[0:1, :]),
               waits=[snap("sPs")] + guard.pop("nqfree", []), inc="sA")
        guard["statps"] = [snap("sA")]
        r.emit("vector", lambda v, h=h: v.reduce_max(
            q2row[0:1, h:h + 1], nqrow[0:1, :], axis=AX.X), waits=[snap("sA")], inc="sV")
        guard["nqfree"] = [snap("sV")]
    r.realize(nc, sems)
    stP.close()
    stQ.close()

    stB = ExitStack()
    krows = stB.enter_context(nc.sbuf_tensor("krows", [8, 8 * 1024], BF16))
    k2pad = stB.enter_context(nc.sbuf_tensor("k2pad", [8, 1], F32))
    k2eps = stB.enter_context(nc.psum_tensor("k2eps", [1, 32], F32))
    btp = stB.enter_context(nc.psum_tensor("btp", [128, 32], F32))
    r = Rec(C)
    r.emit("gpsimd", lambda g: g.wait_ge(sems["cc"], 2))
    for rr in range(NCORES):
        r.emit("gpsimd", lambda g, rr=rr: g.dma_start(
            out=krows[0:8, rr * 1024:(rr + 1) * 1024],
            in_=kg.ap()[rr * 1032 + 1024:rr * 1032 + 1032, :]), inc="dG", inc_by=16)
    wd = snap("dG")
    r.emit("vector", lambda v: v.reduce_max(k2pad[0:8, 0:1], krows[0:8, :], axis=AX.X),
           waits=[wd], inc="sV")
    r.emit("tensor", lambda t: t.matmul(k2eps[0:1, :], k2pad[0:8, 0:1],
                                        exp8b[0:8, 0:32], start=True, stop=True),
           waits=[snap("sV")], inc="sPs")
    r.emit("vector", lambda v: v.tensor_mul(srow[0:1, :], q2row[0:1, :],
                                            k2eps[0:1, :]),
           waits=[snap("sPs")], inc="sV")

    r.emit("scalar", lambda sc: sc.activation(
        srow2[0:1, :], srow[0:1, :], AF.Ln, bias=epsA[0:1, 0:1],
        scale=1.0 / 128.0), waits=[snap("sV")], inc="sA")
    r.emit("scalar", lambda sc: sc.activation(
        srow[0:1, :], srow2[0:1, :], AF.Exp, bias=0.0, scale=0.5),
        waits=[snap("sA")], inc="sA")
    r.emit("vector", lambda v: v.tensor_scalar_mul(srow2[0:1, :], srow[0:1, :], -1.0),
           waits=[snap("sA")], inc="sV")
    r.emit("tensor", lambda t: t.matmul(btp[:, :], onesf[0:1, :], srow2[0:1, 0:32],
                                        start=True, stop=True),
           waits=[snap("sV")], inc="sPs")
    r.emit("scalar", lambda sc: sc.copy(btbl[:, :], btp[:, :]),
           waits=[snap("sPs")], inc="sA")
    r.realize(nc, sems)
    stB.close()

    stT = ExitStack()
    kbuf = stT.enter_context(nc.sbuf_tensor("kbuf", [128, 2 * 1024], BF16))
    vbuf = stT.enter_context(nc.sbuf_tensor("vbuf", [128, 2 * 1024], BF16))
    ptb = stT.enter_context(nc.sbuf_tensor("ptb", [128, 2 * 1024], BF16))
    ascr = stT.enter_context(nc.sbuf_tensor("ascr", [128, 1024], F32))
    dacc = stT.enter_context(nc.sbuf_tensor("dacc", [128, 1024], F32))
    atile = stT.enter_context(nc.sbuf_tensor("atile", [128, 2 * 1024], BF16))
    drow = stT.enter_context(nc.sbuf_tensor("drow", [1, 1024], F32))
    drow2 = stT.enter_context(nc.sbuf_tensor("drow2", [128, 1024], F32))
    sps = [stT.enter_context(nc.psum_tensor(f"sps{i}", [128, 1024], F32)) for i in range(2)]
    outps = stT.enter_context(nc.psum_tensor("outps", [128, 1024], F32))
    denps = stT.enter_context(nc.psum_tensor("denps", [128, 1024], F32))

    r = Rec(C)
    guard = {}
    r.emit("sync", lambda s: s.wait_ge(sems["ccd"], 16))
    eg = 0
    pair = 0
    pending = None

    def emit_pv(pg):
        waits = [pg["wA"]] + ([pg["wV"]] if pg["diag"] else []) + pg["whead"]

        def pv(t, pg=pg):
            ins = None
            for ti, tt in enumerate(pg["tiles"]):
                rhs = ptb[:, (pg["eg"] % 2) * 1024 + ti * pg["slot"]:
                          (pg["eg"] % 2) * 1024 + ti * pg["slot"] + pg["N"]]
                st = pg["first"] and ti == 0
                ins = t.matmul(
                    outps[:, pg["jlo"] * 128:pg["jlo"] * 128 + pg["N"]],
                    vbuf[:, pg["pp"] * 1024 + tt * 128:pg["pp"] * 1024 + tt * 128 + 128],
                    rhs, start=st, stop=False, skip_group_check=True)
            return ins
        r.emit("tensor", pv, waits=waits, inc="sPv")
        guard[("ptb", pg["eg"] % 2)] = [snap("sPv"), pg["dsum"]]
        if pg["tiles"][-1] == 7:
            guard[("kv", pg["pp"])] = [snap("sPv")]

    headguard = [snap("sA")]
    for h in range(H):
        first_pv_of_head = True
        for (g, b) in attn_pairs():
            pp = pair % 2
            pw = guard.pop(("kv", pp), [])
            kv = h // 4
            wl = "dS0" if pp == 0 else "dS1"
            r.emit("sync", lambda s, b=b, kv=kv, pp=pp: s.dma_start(
                out=kbuf[:, pp * 1024:pp * 1024 + 1024],
                in_=kg.ap()[b * 1032 + kv * 128:b * 1032 + (kv + 1) * 128, :]),
                waits=pw, inc=wl, inc_by=16)
            r.emit("sync", lambda s, b=b, kv=kv, pp=pp: s.dma_start(
                out=vbuf[:, pp * 1024:(pp + 1) * 1024]
                    .rearrange("p (t f) -> p t f", f=128),
                in_=vg.ap()[b * 1024:(b + 1) * 1024, kv * 128:(kv + 1) * 128]
                    .rearrange("(t p) f -> p t f", p=128)), inc=wl, inc_by=16)
            kvdone = snap(wl)
            diag = (g == 0) or (b >= 4)
            for gi, gr in enumerate(attn_groups_of_pair(g, b)):
                tiles, slot, span, N, jlo, jhi = (gr["tiles"], gr["slot"], gr["span"],
                                                  gr["N"], gr["jlo"], gr["jhi"])
                sw = ([kvdone] + guard.pop(("sps", eg % 2), [])
                      + ([snap("sA")] if eg == 0 else []))

                def sgrp(t, tiles=tiles, slot=slot, N=N, jlo=jlo, h=h, pp=pp, eg=eg):
                    ins = None
                    for ti, tt in enumerate(tiles):
                        ins = t.matmul(
                            sps[eg % 2][:, ti * slot:ti * slot + N],
                            kbuf[:, pp * 1024 + tt * 128:pp * 1024 + tt * 128 + 128],
                            qfull[:, h * 1024 + jlo * 128:h * 1024 + jlo * 128 + N],
                            start=True, stop=True)
                    return ins
                r.emit("tensor", sgrp, waits=sw, inc="sPs")
                sdone = snap("sPs")
                ew = [sdone] + guard.pop(("ptb", eg % 2), [])
                r.emit("scalar", lambda sc, eg=eg, span=span, h=h: sc.activation(
                    ptb[:, (eg % 2) * 1024:(eg % 2) * 1024 + span],
                    sps[eg % 2][:, 0:span], AF.Exp,
                    bias=btbl[:, h:h + 1], scale=ISQ), waits=ew, inc="sA")
                guard[("sps", eg % 2)] = [snap("sA")]
                wA = snap("sA")
                wV = None
                if diag:
                    def mgrp(v, tiles=tiles, slot=slot, eg=eg):
                        ins = None
                        for ti, tt in enumerate(tiles):
                            sl = ptb[:, (eg % 2) * 1024 + ti * slot:
                                     (eg % 2) * 1024 + ti * slot + 128]
                            ins = v.tensor_mul(sl, sl,
                                               masks[:, tt * 128:(tt + 1) * 128])
                        return ins
                    r.emit("vector", mgrp, waits=[wA], inc="sV")
                    wV = snap("sV")
                dw = [wA] + ([wV] if diag else []) + guard.pop("dacc", [])

                def dadd(p, tiles=tiles, slot=slot, N=N, jlo=jlo, eg=eg,
                         first=(b == 0 and gi == 0)):
                    ins = None
                    dst = dacc[:, jlo * 128:jlo * 128 + N]
                    for ti in range(len(tiles)):
                        sl = ptb[:, (eg % 2) * 1024 + ti * slot:
                                 (eg % 2) * 1024 + ti * slot + N]
                        if first and ti == 0:
                            ins = p.tensor_copy(dst, sl)
                        else:
                            ins = p.tensor_add(dst, dst, sl)
                    return ins
                r.emit("gpsimd", dadd, waits=dw, inc="sF1")
                dsum = snap("sF1")
                if pending is not None:
                    emit_pv(pending)
                pending = dict(tiles=tiles, slot=slot, N=N, jlo=jlo,
                               diag=diag, pp=pp, eg=eg, wA=wA, wV=wV,
                               first=(b == 0 and gi == 0), dsum=dsum,
                               whead=headguard if first_pv_of_head else [])
                first_pv_of_head = False
                eg += 1
            pair += 1
        emit_pv(pending)
        pending = None
        pvall = snap("sPv")
        dnw = [snap("sF1")] + guard.pop("denps", [])

        def denmm(t):
            ins = None
            for half in range(2):
                ins = t.matmul(denps[0:1, half * 512:half * 512 + 512],
                               onesf[:, 0:1],
                               dacc[:, half * 512:half * 512 + 512],
                               start=True, stop=True, skip_group_check=True)
            return ins
        r.emit("tensor", denmm, waits=dnw, inc="sPs")
        guard["dacc"] = [snap("sPs")]
        dmmdone = snap("sPs")
        r.emit("scalar", lambda sc: sc.copy(ascr[:, :], outps[:, :]),
               waits=[pvall] + guard.pop("ascr", []), inc="sA")
        outfree = snap("sA")
        r.emit("scalar", lambda sc: sc.copy(drow[0:1, :], denps[0:1, :]),
               waits=[dmmdone], inc="sA")
        rw = snap("sA")

        def bcast(t):
            ins = None
            for half in range(2):
                ins = t.matmul(denps[:, half * 512:half * 512 + 512],
                               onesf[0:1, 0:128],
                               drow[0:1, half * 512:half * 512 + 512],
                               start=True, stop=True, skip_group_check=True)
            return ins
        r.emit("tensor", bcast, waits=[rw], inc="sPs")
        r.emit("vector", lambda v: v.reciprocal(drow2[:, :], denps[:, :]),
               waits=[snap("sPs")] + guard.pop("drow2", []), inc="sV")
        rcpw = snap("sV")
        fw = ([rcpw, outfree] + guard.pop(("atile", h % 2), []))
        r.emit("vector", lambda v, h=h: v.tensor_mul(
            atile[:, (h % 2) * 1024:(h % 2) * 1024 + 1024],
            ascr[:, :], drow2[:, :]), waits=fw, inc="sV")
        denfree = snap("sV")
        guard["drow2"] = [denfree]
        guard["ascr"] = [denfree]
        wl = "dG0" if h % 2 == 0 else "dG1"
        r.emit("gpsimd", lambda g, h=h: g.dma_start(
            out=atnd.ap()[h * 128:(h + 1) * 128, :],
            in_=atile[:, (h % 2) * 1024:(h % 2) * 1024 + 1024]),
            waits=[denfree], inc=wl, inc_by=16)
        guard[("atile", h % 2)] = [snap(wl)]
        guard["denps"] = [denfree]
        headguard = [outfree]
    g0, g1 = snap("dG0"), snap("dG1")
    r.emit("gpsimd", lambda g: g.wait_ge(sems["dG0"], g0[1]))
    r.emit("gpsimd", lambda g: g.wait_ge(sems["dG1"], g1[1]))
    r.realize(nc, sems)
    stT.close()
    stQA.close()

    stL1 = ExitStack()
    ln1full = stL1.enter_context(nc.sbuf_tensor("ln1full", [128, 8 * 4096], BF16))
    stO = ExitStack()
    oacc = stO.enter_context(nc.sbuf_tensor("oacc", [128, 4 * 4096], F32))
    atb = stO.enter_context(nc.sbuf_tensor("atb", [128, 2 * 512], BF16))
    wob = stO.enter_context(nc.sbuf_tensor("wob", [128, 2 * 1024], BF16))
    xqq = stO.enter_context(nc.sbuf_tensor("xqq", [128, 2 * 1024], F32))
    sqf = stO.enter_context(nc.sbuf_tensor("sqf", [128, 4096], F32))
    ln1f = stO.enter_context(nc.sbuf_tensor("ln1f", [128, 4096], F32))
    ostat = stO.enter_context(nc.sbuf_tensor("ostat", [128, 28], F32))
    psO = [stO.enter_context(nc.psum_tensor(f"psO{i}", [128, 1024], F32)) for i in range(4)]

    r = Rec(C)
    guard = {}
    n = 0
    for ttg in range(2):
        for hq in range(4):
            for oc in range(NCH):
                wl = "dG0" if n % 2 == 0 else "dG1"
                r.emit("gpsimd", lambda g, hq=hq, oc=oc, n=n: g.dma_start(
                    out=wob[:, (n % 2) * 1024:(n % 2) * 1024 + 1024],
                    in_=woT[oc * 128:(oc + 1) * 128, hq * 1024:(hq + 1) * 1024]),
                    waits=guard.pop(("wob", n % 2), []), inc=wl, inc_by=16)
                wG = snap(wl)
                sl = "dS0" if n % 2 == 0 else "dS1"
                r.emit("sync", lambda s, ttg=ttg, oc=oc, n=n: s.dma_start(
                    out=atb[:, (n % 2) * 512:(n % 2) * 512 + 512],
                    in_=atnd.ap()[oc * 128:(oc + 1) * 128, ttg * 512:(ttg + 1) * 512]),
                    waits=guard.pop(("atb", n % 2), []), inc=sl, inc_by=16)
                wS = snap(sl)
                mmw = [wG, wS] + (guard.pop("psO", []) if oc == 0 else [])

                def mmo(t, oc=oc, n=n):
                    ins = None
                    for tt in range(4):
                        for half in range(2):
                            ins = t.matmul(
                                psO[tt][:, half * 512:half * 512 + 512],
                                atb[:, (n % 2) * 512 + tt * 128:(n % 2) * 512 + tt * 128 + 128],
                                wob[:, (n % 2) * 1024 + half * 512:
                                    (n % 2) * 1024 + half * 512 + 512],
                                start=(oc == 0), stop=(oc == NCH - 1))
                    return ins
                r.emit("tensor", mmo, waits=mmw, inc="sPs")
                guard[("wob", n % 2)] = [snap("sPs")]
                guard[("atb", n % 2)] = [snap("sPs")]
                n += 1
            mmdone = snap("sPs")
            addsnap = None
            for tt in range(4):
                m = (ttg * 4 + hq) * 4 + tt
                sl = "dS2" if m % 2 == 0 else "dS3"
                r.emit("sync", lambda s, ttg=ttg, tt=tt, hq=hq, m=m: s.dma_start(
                    out=xqq[:, (m % 2) * 1024:(m % 2) * 1024 + 1024],
                    in_=xq[(ttg * 4 + tt) * 128:(ttg * 4 + tt + 1) * 128,
                           hq * 1024:(hq + 1) * 1024]),
                    waits=guard.pop(("xqq", m % 2), []), inc=sl, inc_by=16)
                wS = snap(sl)
                r.emit("vector", lambda v, tt=tt, hq=hq, m=m: v.tensor_add(
                    oacc[:, tt * 4096 + hq * 1024:tt * 4096 + hq * 1024 + 1024],
                    psO[tt][:, :], xqq[:, (m % 2) * 1024:(m % 2) * 1024 + 1024]),
                    waits=[mmdone, wS], inc="sV")
                guard[("xqq", m % 2)] = [snap("sV")]
            guard["psO"] = [snap("sV")]
        SR, SF, wln = ln_chain(r, ostat, 4,
                               [oacc[:, tt * 4096:(tt + 1) * 4096]
                                for tt in range(4)], sqf)
        for tt in range(4):
            k = ttg * 4 + tt
            a = oacc[:, tt * 4096:(tt + 1) * 4096]
            r.emit("vector", lambda v, a=a, tt=tt, SR=SR, SF=SF: v.tensor_scalar(
                ln1f[:, :], a, SR[:, tt:tt + 1], SF[:, tt:tt + 1],
                ALU.mult, ALU.add),
                waits=[wln] + guard.pop("ln1f", []), inc="sV")
            nrm = snap("sV")
            r.emit("sync", lambda s, k=k: s.dma_start(
                out=r2d.ap()[k * 128:(k + 1) * 128, :],
                in_=ln1f[:, :]),
                waits=[nrm], inc="dO", inc_by=16)
            r.emit("vector", lambda v, k=k: v.tensor_copy(
                ln1full[:, k * 4096:(k + 1) * 4096],
                ln1f[:, :]), waits=[nrm], inc="sV")
            guard["ln1f"] = [snap("dO"), snap("sV")]
    r.emit("sync", lambda s: s.wait_ge(sems["dO"], C["dO"]))
    r.realize(nc, sems)
    stO.close()

    stL = ExitStack()
    xt2st = stL.enter_context(nc.sbuf_tensor("xt2st", [128, 2 * 4096], BF16))
    ptL = [stL.enter_context(nc.psum_tensor(f"ptL{i}", [128, 128], BF16)) for i in range(2)]
    r = Rec(C)
    guard = {}
    for tt in range(8):
        for c in range(NCH):
            nn = tt * NCH + c
            r.emit("tensor", lambda t, tt=tt, c=c, nn=nn: t.transpose(
                ptL[nn % 2][:, :],
                ln1full[:, tt * 4096 + c * 128:tt * 4096 + c * 128 + 128], ident[:, :]),
                waits=guard.pop(("ptL", nn % 2), []) + ([snap("sV")] if nn == 0 else []),
                inc="sT")
            r.emit("vector", lambda v, tt=tt, c=c, nn=nn: v.tensor_copy(
                xt2st[:, (tt % 2) * 4096 + c * 128:(tt % 2) * 4096 + c * 128 + 128],
                ptL[nn % 2][:, :]),
                waits=[snap("sT")] + (guard.pop(("xt2st", tt % 2), []) if c == 0 else []),
                inc="sV")
            guard[("ptL", nn % 2)] = [snap("sV")]
        sl = "dS2" if tt % 2 == 0 else "dS3"
        r.emit("sync", lambda s, tt=tt: s.dma_start(
            out=xt2d.ap().rearrange("(c p) t -> p c t", p=128)[:, :, tt * 128:(tt + 1) * 128],
            in_=xt2st[:, (tt % 2) * 4096:(tt % 2) * 4096 + 4096]
                .rearrange("p (c f) -> p c f", f=128)),
            waits=[snap("sV")], inc=sl, inc_by=16)
        guard[("xt2st", tt % 2)] = [snap(sl)]
    r.realize(nc, sems)
    stL.close()
    stL1.close()

    stF = ExitStack()
    xt2b = stF.enter_context(nc.sbuf_tensor("xt2b", [128, NCH * 1024], BF16))
    gwb = stF.enter_context(nc.sbuf_tensor("gwb", [128, 2 * 4096], BF16))
    uwb = stF.enter_context(nc.sbuf_tensor("uwb", [128, 2 * 4096], BF16))
    gsil = stF.enter_context(nc.sbuf_tensor("gsil", [128, 2 * 1024], F32))
    mid1 = stF.enter_context(nc.sbuf_tensor("mid1", [128, 1024], F32))
    midb2 = stF.enter_context(nc.sbuf_tensor("midb2", [128, 2 * 1024], BF16))
    gps = [stF.enter_context(nc.psum_tensor(f"gps{i}", [128, 1024], F32)) for i in range(2)]
    ups = [stF.enter_context(nc.psum_tensor(f"ups{i}", [128, 1024], F32)) for i in range(2)]

    r = Rec(C)
    guard = {}
    r.emit("gpsimd", lambda g: g.dma_start(
        out=xt2b[:, :].rearrange("p (c t) -> p c t", t=TOK),
        in_=xt2d.ap().rearrange("(c p) t -> p c t", p=128)),
        waits=[snap("dS2"), snap("dS3")], inc="dG", inc_by=16)
    xload = snap("dG")
    for ft in range(FT):
        gw = guard.pop(("wf", ft % 2), []) + ([xload] if ft < 2 else [])
        wl = "dG0" if ft % 2 == 0 else "dG1"
        r.emit("gpsimd", lambda g, ft=ft: g.dma_start(
            out=gwb[:, (ft % 2) * 4096:(ft % 2) * 4096 + 4096]
                .rearrange("p (c f) -> p c f", f=128),
            in_=gwT.rearrange("(c p) f -> p c f", p=128)[:, :, ft * 128:(ft + 1) * 128]),
            waits=gw, inc=wl, inc_by=16)
        r.emit("gpsimd", lambda g, ft=ft: g.dma_start(
            out=uwb[:, (ft % 2) * 4096:(ft % 2) * 4096 + 4096]
                .rearrange("p (c f) -> p c f", f=128),
            in_=uwT.rearrange("(c p) f -> p c f", p=128)[:, :, ft * 128:(ft + 1) * 128]),
            inc=wl, inc_by=16)
        wG = snap(wl)
        mmw = [wG] + guard.pop(("gps", ft % 2), []) + guard.pop(("ups", ft % 2), [])

        def mmf(t, ft=ft):
            ins = None
            for c in range(NCH):
                for (ps, wb) in ((gps, gwb), (ups, uwb)):
                    lt = wb[:, (ft % 2) * 4096 + c * 128:(ft % 2) * 4096 + c * 128 + 128]
                    for half in range(2):
                        ins = t.matmul(
                            ps[ft % 2][:, half * 512:half * 512 + 512], lt,
                            xt2b[:, c * 1024 + half * 512:c * 1024 + half * 512 + 512],
                            start=(c == 0), stop=(c == NCH - 1))
            return ins
        r.emit("tensor", mmf, waits=mmw, inc="sPs")
        guard[("wf", ft % 2)] = [snap("sPs")]
        mmdone = snap("sPs")
        r.emit("scalar", lambda sc, ft=ft: sc.activation(
            gsil[:, (ft % 2) * 1024:(ft % 2) * 1024 + 1024], gps[ft % 2][:, :], AF.Silu,
            bias=0.0, scale=gsb[:, ft:ft + 1]),
            waits=[mmdone] + guard.pop(("gsil", ft % 2), []), inc="sA")
        guard[("gps", ft % 2)] = [snap("sA")]
        wA = snap("sA")

        def midop(v, ft=ft):
            v.tensor_mul(mid1[:, :], ups[ft % 2][:, :],
                         gsil[:, (ft % 2) * 1024:(ft % 2) * 1024 + 1024])
            return v.tensor_scalar(midb2[:, (ft % 2) * 1024:(ft % 2) * 1024 + 1024],
                                   mid1[:, :], usb[:, ft:ft + 1], None, ALU.mult)
        r.emit("vector", midop, waits=[wA] + guard.pop(("midb", ft % 2), []), inc="sV")
        guard[("ups", ft % 2)] = [snap("sV")]
        guard[("gsil", ft % 2)] = [snap("sV")]
        sl = "dS0" if ft % 2 == 0 else "dS1"
        r.emit("sync", lambda s, ft=ft: s.dma_start(
            out=midd.ap()[ft * 128:(ft + 1) * 128, :],
            in_=midb2[:, (ft % 2) * 1024:(ft % 2) * 1024 + 1024]),
            waits=[snap("sV")], inc=sl, inc_by=16)
        guard[("midb", ft % 2)] = [snap(sl)]
    r.emit("sync", lambda s: s.wait_ge(sems["dS0"], C["dS0"]))
    r.emit("sync", lambda s: s.wait_ge(sems["dS1"], C["dS1"]))
    r.realize(nc, sems)
    stF.close()

    stD = ExitStack()
    dstage = stD.enter_context(nc.sbuf_tensor("dstage", [128, NCH * 1024], BF16))
    dwb = stD.enter_context(nc.sbuf_tensor("dwb", [128, 2 * 512], BF16))
    midb = stD.enter_context(nc.sbuf_tensor("midb", [128, 2 * 1024], BF16))
    stDp = ExitStack()
    dps = [stDp.enter_context(nc.psum_tensor(f"dps{i}", [128, 1024], F32)) for i in range(4)]

    r = Rec(C)
    guard = {}
    for dg in range(8):
        for fc in range(FT):
            nn = dg * FT + fc
            wl = "dG0" if nn % 2 == 0 else "dG1"
            r.emit("gpsimd", lambda g, dg=dg, fc=fc, nn=nn: g.dma_start(
                out=dwb[:, (nn % 2) * 512:(nn % 2) * 512 + 512],
                in_=dwT[fc * 128:(fc + 1) * 128, dg * 512:(dg + 1) * 512]),
                waits=guard.pop(("dwb", nn % 2), []), inc=wl, inc_by=16)
            wG = snap(wl)
            sl = "dS2" if nn % 2 == 0 else "dS3"
            r.emit("sync", lambda s, fc=fc, nn=nn: s.dma_start(
                out=midb[:, (nn % 2) * 1024:(nn % 2) * 1024 + 1024],
                in_=midd.ap()[fc * 128:(fc + 1) * 128, :]),
                waits=guard.pop(("midb", nn % 2), []), inc=sl, inc_by=16)
            wS = snap(sl)
            mmw = [wG, wS] + (guard.pop("dps", []) if fc == 0 else [])

            def mmd(t, fc=fc, nn=nn):
                ins = None
                for dt in range(4):
                    lt = dwb[:, (nn % 2) * 512 + dt * 128:(nn % 2) * 512 + dt * 128 + 128]
                    for half in range(2):
                        ins = t.matmul(
                            dps[dt][:, half * 512:half * 512 + 512], lt,
                            midb[:, (nn % 2) * 1024 + half * 512:(nn % 2) * 1024 + half * 512 + 512],
                            start=(fc == 0), stop=(fc == FT - 1))
                return ins
            r.emit("tensor", mmd, waits=mmw, inc="sPs")
            guard[("dwb", nn % 2)] = [snap("sPs")]
            guard[("midb", nn % 2)] = [snap("sPs")]
        mmdone = snap("sPs")
        last = None
        for dt in range(4):
            last = r.emit("scalar", lambda sc, dg=dg, dt=dt: sc.activation(
                dstage[:, (dg * 4 + dt) * 1024:(dg * 4 + dt) * 1024 + 1024],
                dps[dt][:, :], AF.Copy, bias=0.0, scale=dsb[:, dg * 4 + dt:dg * 4 + dt + 1]),
                waits=[mmdone] if dt == 0 else (), inc="sA")
        guard["dps"] = [snap("sA")]
    r.realize(nc, sems)
    stDp.close()

    stZ = ExitStack()
    ftok = stZ.enter_context(nc.sbuf_tensor("ftok", [128, 2 * 4096], F32))
    r2b = stZ.enter_context(nc.sbuf_tensor("r2b", [128, 2 * 4096], F32))
    fadd = stZ.enter_context(nc.sbuf_tensor("fadd", [128, 2 * 4096], F32))
    sqz = stZ.enter_context(nc.sbuf_tensor("sqz", [128, 4096], F32))
    zstat = stZ.enter_context(nc.sbuf_tensor("zstat", [128, 8], F32))
    ptZ = [stZ.enter_context(nc.psum_tensor(f"ptZ{i}", [128, 128], BF16)) for i in range(2)]

    r = Rec(C)
    guard = {}
    for tt in range(8):
        sl = "dS0" if tt % 2 == 0 else "dS1"
        r.emit("sync", lambda s, tt=tt: s.dma_start(
            out=r2b[:, (tt % 2) * 4096:(tt % 2) * 4096 + 4096],
            in_=r2d.ap()[tt * 128:(tt + 1) * 128, :]),
            waits=guard.pop(("r2b", tt % 2), []), inc=sl, inc_by=16)
        wS = snap(sl)
        for c in range(NCH):
            nn = tt * NCH + c
            r.emit("tensor", lambda t, tt=tt, c=c, nn=nn: t.transpose(
                ptZ[nn % 2][:, :],
                dstage[:, c * 1024 + tt * 128:c * 1024 + tt * 128 + 128], ident[:, :]),
                waits=guard.pop(("ptZ", nn % 2), []) + ([snap("sA")] if nn == 0 else []),
                inc="sT")
            r.emit("scalar", lambda sc, tt=tt, c=c, nn=nn: sc.copy(
                ftok[:, (tt % 2) * 4096 + c * 128:(tt % 2) * 4096 + c * 128 + 128],
                ptZ[nn % 2][:, :]),
                waits=[snap("sT")] + (guard.pop(("ftok", tt % 2), []) if c == 0 else []),
                inc="sA")
            guard[("ptZ", nn % 2)] = [snap("sA")]
        evdone = snap("sA")
        fa = fadd[:, (tt % 2) * 4096:(tt % 2) * 4096 + 4096]
        r.emit("vector", lambda v, tt=tt, fa=fa: v.tensor_add(
            fa, ftok[:, (tt % 2) * 4096:(tt % 2) * 4096 + 4096],
            r2b[:, (tt % 2) * 4096:(tt % 2) * 4096 + 4096]),
            waits=[wS, evdone] + guard.pop(("fadd", tt % 2), []), inc="sV")
        guard[("r2b", tt % 2)] = [snap("sV")]
        SR, SF, wln = ln_chain(r, zstat, 1, [fa], sqz,
                               first_waits=[snap("sV")])
        r.emit("vector", lambda v, tt=tt, fa=fa, SR=SR, SF=SF: v.tensor_scalar(
            ftok[:, (tt % 2) * 4096:(tt % 2) * 4096 + 4096], fa,
            SR[:, 0:1], SF[:, 0:1], ALU.mult, ALU.add),
            waits=[wln], inc="sV")
        guard[("fadd", tt % 2)] = [snap("sV")]
        so = "dS2" if tt % 2 == 0 else "dS3"
        r.emit("sync", lambda s, tt=tt: s.dma_start(
            out=out[tt * 128:(tt + 1) * 128, :],
            in_=ftok[:, (tt % 2) * 4096:(tt % 2) * 4096 + 4096]),
            waits=[snap("sV")], inc=so, inc_by=16)
        guard[("ftok", tt % 2)] = [snap(so)]
    r.emit("sync", lambda s: s.wait_ge(sems["dS2"], C["dS2"]))
    r.emit("sync", lambda s: s.wait_ge(sems["dS3"], C["dS3"]))
    r.realize(nc, sems)
    stZ.close()
    stD.close()
    glob.close()
    return nc



_ROPE_THETA = 10000.0


def _rope_tables(pos):
    freqs = np.exp(-np.arange(0, 128, 2, dtype=np.float32)
                   * np.float32(math.log(_ROPE_THETA) / 128.0))
    ang = pos.astype(np.float32)[None, :] * freqs[:, None]
    cos = np.cos(ang).astype(np.float32)
    sin = np.sin(ang).astype(np.float32)
    return (np.ascontiguousarray(np.vstack([cos, cos])),
            np.ascontiguousarray(np.vstack([sin, sin])))


def _rot_matrix():
    m = np.zeros((128, 128), dtype=BF16NP)
    for d in range(64):
        m[64 + d, d] = -1
        m[d, 64 + d] = 1
    return m


def _stage_inputs(inputs):
    x = np.asarray(inputs["L0_attn_hidden_states"], np.float32)[0]
    Wq = np.asarray(inputs["L0_attn_Wq"], np.float32)
    Wk = np.asarray(inputs["L0_attn_Wk"], np.float32)
    Wv = np.asarray(inputs["L0_attn_Wv"], np.float32)
    Wo = np.asarray(inputs["L0_attn_Wo"], np.float32)
    gw = np.asarray(inputs["L0_ffn_gate_w"], np.int32)
    uw = np.asarray(inputs["L0_ffn_up_w"], np.int32)
    dw = np.asarray(inputs["L0_ffn_down_w"], np.int32)
    gs = np.asarray(inputs["L0_ffn_gate_s"], np.float32)
    us = np.asarray(inputs["L0_ffn_up_s"], np.float32)
    ds = np.asarray(inputs["L0_ffn_down_s"], np.float32)

    shared = {
        "wqT": np.ascontiguousarray(Wq.T),
        "wkT": np.ascontiguousarray(Wk.T),
        "wvT": np.ascontiguousarray(Wv.T),
        "woT": np.ascontiguousarray(Wo.T),
        "gwT": np.ascontiguousarray(gw.T),
        "uwT": np.ascontiguousarray(uw.T),
        "dwT": np.ascontiguousarray(dw.T),
        "gs": np.ascontiguousarray(gs.reshape(FT, 128).T),
        "us": np.ascontiguousarray(us.reshape(FT, 128).T),
        "ds": np.ascontiguousarray(ds.reshape(NCH, 128).T),
        "ident": np.eye(128, dtype=BF16NP),
        "identf": np.eye(128, dtype=np.float32),
        "onesi": np.ones((128, 128), np.float32),
        "rotm": _rot_matrix(),
        "exp8": np.ascontiguousarray(
            (np.arange(32)[None, :] // 4 == np.arange(8)[:, None]).astype(np.float32)),
    }
    xr = x.reshape(64, 128, D)
    in_maps = []
    for i in range(NCORES):
        xqi = np.ascontiguousarray(xr[i::8].reshape(TOK, D))
        xkvi = x[i * TOK:(i + 1) * TOK]
        pos_q = (np.arange(8)[:, None] * 1024 + i * 128 + np.arange(128)[None, :]).reshape(-1)
        pos_kv = np.arange(i * TOK, (i + 1) * TOK)
        cosq, sinq = _rope_tables(pos_q)
        coskv, sinkv = _rope_tables(pos_kv)
        m = np.zeros((128, 1024), dtype=BF16NP)
        for t in range(8):
            if t < i:
                m[:, t * 128:(t + 1) * 128] = 1
            elif t == i:
                m[:, t * 128:(t + 1) * 128] = np.triu(np.ones((128, 128))).astype(BF16NP)
        d = dict(shared)
        d.update({
            "xq": xqi,
            "xqT": np.ascontiguousarray(xqi.T),
            "xkvT": np.ascontiguousarray(xkvi.T),
            "cosq": cosq, "sinq": sinq, "coskv": coskv, "sinkv": sinkv,
            "masks": m,
        })
        in_maps.append(d)
    return in_maps


_NC_CACHE = None


def kernel(**inputs):
    global _NC_CACHE
    in_maps = _stage_inputs(inputs)
    if _NC_CACHE is None:
        _NC_CACHE = build_program()
    res = run_bass_kernel_spmd(_NC_CACHE, in_maps, core_ids=list(range(NCORES)))
    arr = np.stack([res.results[i]["out"] for i in range(NCORES)])
    arr = arr.reshape(NCORES, 8, 128, D).transpose(1, 0, 2, 3).reshape(1, S, D)
    return np.ascontiguousarray(arr)


if __name__ == "__main__":
    import time
    t0 = time.time()
    nc = build_program()
    print(f"build ok in {time.time() - t0:.1f}s; instructions: {nc.n_instructions()}")


# version 09
# speedup vs baseline: 1.0532x; 1.0532x over previous
import math
from contextlib import ExitStack

import numpy as np
import ml_dtypes
import concourse.bass as bass
import concourse.mybir as mybir
from concourse.bass_utils import run_bass_kernel_spmd

F32 = mybir.dt.float32
BF16 = mybir.dt.bfloat16
I32 = mybir.dt.int32
AF = mybir.ActivationFunctionType
ALU = mybir.AluOpType
AX = mybir.AxisListType

D = 4096
S = 8192
NCORES = 8
H = 32
KVH = 8
DFF = 11008
FT = DFF // 128
TOK = 1024
NCH = D // 128
ISQ = 1.0 / math.sqrt(128.0)
BF16NP = ml_dtypes.bfloat16
ENGS = ["sync", "gpsimd", "tensor", "scalar", "vector"]



def attn_pairs():
    return [(0, b) for b in range(4)] + [(1, b) for b in range(8)]


def attn_groups_of_pair(g, b):
    jlo = b if g == 0 else max(b, 4)
    jhi = 4 * g + 3
    N = (jhi - jlo + 1) * 128
    tpg = {512: 2, 384: 2, 256: 4, 128: 8}[N]
    slot = 512 if N >= 384 else N
    return [dict(tiles=list(range(s0, s0 + tpg)), slot=slot,
                 span=(tpg - 1) * slot + N, N=N, jlo=jlo, jhi=jhi)
            for s0 in range(0, 8, tpg)]


class Rec:

    def __init__(self, C):
        self.ops = {e: [] for e in ENGS}
        self.C = C

    def emit(self, eng, fn, waits=(), inc=None, inc_by=1):
        ws = [(s, v) for (s, v) in waits if v > 0]
        self.ops[eng].append((fn, ws, inc, inc_by))
        if inc is not None:
            self.C[inc] += inc_by
        return None if inc is None else (inc, self.C[inc])

    def realize(self, nc, sems):
        with nc.Block() as block:
            for ename in ENGS:
                if not self.ops[ename]:
                    continue

                def mk(ename):
                    def body(eng):
                        for fn, ws, inc, inc_by in self.ops[ename]:
                            for (s, v) in ws:
                                eng.wait_ge(sems[s], v)
                            ins = fn(eng)
                            if inc is not None:
                                ins.then_inc(sems[inc], inc_by)
                    return body
                getattr(block, ename)(mk(ename))


def build_program():
    nc = bass.Bass("TRN2", target_bir_lowering=False)

    xqT = nc.dram_tensor("xqT", [D, TOK], F32, kind="ExternalInput").ap()
    xq = nc.dram_tensor("xq", [TOK, D], F32, kind="ExternalInput").ap()
    xkvT = nc.dram_tensor("xkvT", [D, TOK], F32, kind="ExternalInput").ap()
    wqT = nc.dram_tensor("wqT", [D, D], F32, kind="ExternalInput").ap()
    wkT = nc.dram_tensor("wkT", [D, KVH * 128], F32, kind="ExternalInput").ap()
    wvT = nc.dram_tensor("wvT", [D, KVH * 128], F32, kind="ExternalInput").ap()
    woT = nc.dram_tensor("woT", [D, D], F32, kind="ExternalInput").ap()
    gwT = nc.dram_tensor("gwT", [D, DFF], I32, kind="ExternalInput").ap()
    uwT = nc.dram_tensor("uwT", [D, DFF], I32, kind="ExternalInput").ap()
    dwT = nc.dram_tensor("dwT", [DFF, D], I32, kind="ExternalInput").ap()
    gs_in = nc.dram_tensor("gs", [128, FT], F32, kind="ExternalInput").ap()
    us_in = nc.dram_tensor("us", [128, FT], F32, kind="ExternalInput").ap()
    ds_in = nc.dram_tensor("ds", [128, NCH], F32, kind="ExternalInput").ap()
    cosq_in = nc.dram_tensor("cosq", [128, TOK], F32, kind="ExternalInput").ap()
    sinq_in = nc.dram_tensor("sinq", [128, TOK], F32, kind="ExternalInput").ap()
    coskv_in = nc.dram_tensor("coskv", [128, TOK], F32, kind="ExternalInput").ap()
    sinkv_in = nc.dram_tensor("sinkv", [128, TOK], F32, kind="ExternalInput").ap()
    rotm_in = nc.dram_tensor("rotm", [128, 128], BF16, kind="ExternalInput").ap()
    masks_in = nc.dram_tensor("masks", [128, 1024], BF16, kind="ExternalInput").ap()
    exp8_in = nc.dram_tensor("exp8", [8, 32], F32, kind="ExternalInput").ap()
    ident_in = nc.dram_tensor("ident", [128, 128], BF16, kind="ExternalInput").ap()
    identf_in = nc.dram_tensor("identf", [128, 128], F32, kind="ExternalInput").ap()
    ones_in = nc.dram_tensor("onesi", [128, 128], F32, kind="ExternalInput").ap()
    out = nc.dram_tensor("out", [TOK, D], F32, kind="ExternalOutput").ap()

    kc = nc.dram_tensor("kc", [1032, 1024], BF16)
    vc = nc.dram_tensor("vc", [1024, 1024], BF16)
    kg = nc.dram_tensor("kg", [NCORES * 1032, 1024], BF16)
    vg = nc.dram_tensor("vg", [NCORES * 1024, 1024], BF16)
    atnd = nc.dram_tensor("atnd", [D, TOK], BF16)
    r2d = nc.dram_tensor("r2d", [TOK, D], F32)
    xt2d = nc.dram_tensor("xt2d", [D, TOK], BF16)
    midd = nc.dram_tensor("midd", [DFF, TOK], BF16)

    glob = ExitStack()

    def sb(name, shape, dtype):
        return glob.enter_context(nc.sbuf_tensor(name, shape, dtype))

    ident = sb("identt", [128, 128], BF16)
    rotmb = sb("rotmbt", [128, 128], BF16)
    identf = sb("identft", [128, 128], F32)
    onesf = sb("onesft", [128, 128], F32)
    onesb = sb("onesbt", [128, 1], BF16)
    masks = sb("maskst", [128, 1024], BF16)
    gsb = sb("gsbt", [128, FT], F32)
    usb = sb("usbt", [128, FT], F32)
    dsb = sb("dsbt", [128, NCH], F32)
    btbl = sb("btblt", [128, 32], F32)
    exp8b = sb("exp8bt", [8, 32], F32)
    statrow = sb("statrowt", [1, 8192], BF16)
    nqrow = sb("nqrowt", [1, 1024], F32)
    q2row = sb("q2rowt", [1, 32], F32)
    k2row = sb("k2rowt", [1, 128], F32)
    srow = sb("srowt", [1, 32], F32)
    srow2 = sb("srow2t", [1, 32], F32)
    epsA = sb("epsAt", [128, 1], F32)

    mkr = sb("mkrt", [1, 16], F32)

    sems = {}
    for name in ["dG", "dG0", "dG1", "dS", "dS0", "dS1", "dS2", "dS3", "dO",
                 "cc", "ccd", "sPs", "sPv", "sA", "sV", "sF1", "sT", "sF2"]:
        sems[name] = glob.enter_context(nc.semaphore(name))
    C = {k: 0 for k in sems}

    def snap(sem):
        return (sem, C[sem])

    def ln_chain(r, stat, B, a_list, sqscr, first_waits=()):
        SA = stat[:, 0 * B:1 * B]
        SB = stat[:, 1 * B:2 * B]
        SC = stat[:, 2 * B:3 * B]
        SD = stat[:, 3 * B:4 * B]
        SE = stat[:, 4 * B:5 * B]
        SR = stat[:, 5 * B:6 * B]
        SF = stat[:, 6 * B:7 * B]

        def v1(v):
            ins = None
            for t, a in enumerate(a_list):
                v.reduce_sum(SA[:, t:t + 1], a, axis=AX.X)
                ins = v.scalar_tensor_tensor(
                    sqscr[:, :], a, 1.0, a, ALU.mult, ALU.mult,
                    accum_out=SB[:, t:t + 1])
            return ins
        r.emit("vector", v1, waits=list(first_waits), inc="sV")
        r.emit("scalar", lambda sc: sc.activation(
            SC, SA, AF.Square, bias=0.0, scale=1.0 / 4096.0),
            waits=[snap("sV")], inc="sA")
        r.emit("vector", lambda v: v.scalar_tensor_tensor(
            SD, SB, 1.0 / 4096.0, SC, ALU.mult, ALU.subtract),
            waits=[snap("sA")], inc="sV")
        r.emit("scalar", lambda sc: sc.activation(
            SE, SD, AF.Ln, bias=epsA[:, 0:1], scale=1.0),
            waits=[snap("sV")], inc="sA")
        r.emit("scalar", lambda sc: sc.activation(
            SR, SE, AF.Exp, bias=0.0, scale=-0.5),
            waits=[snap("sA")], inc="sA")
        r.emit("vector", lambda v: v.scalar_tensor_tensor(
            SF, SA, -1.0 / 4096.0, SR, ALU.mult, ALU.mult),
            waits=[snap("sA")], inc="sV")
        return SR, SF, snap("sV")

    r = Rec(C)
    for dst, src in [(ident, ident_in), (identf, identf_in), (onesf, ones_in),
                     (masks, masks_in), (gsb, gs_in), (usb, us_in), (dsb, ds_in),
                     (rotmb, rotm_in), (exp8b, exp8_in)]:
        r.emit("gpsimd", lambda g, d=dst, s=src: g.dma_start(out=d[:, :], in_=s),
               inc="dG", inc_by=16)
    r.emit("gpsimd", lambda g: g.wait_ge(sems["dG"], C["dG"]))
    r.emit("vector", lambda v: v.memset(onesb[:, :], 1.0))
    r.emit("vector", lambda v: v.memset(epsA[:, :], 1e-5))
    r.realize(nc, sems)

    stA = ExitStack()
    xkvb = stA.enter_context(nc.sbuf_tensor("xkvb", [128, NCH * 1024], BF16))
    ckv = stA.enter_context(nc.sbuf_tensor("ckv", [128, 2 * TOK], F32))
    qraw = stA.enter_context(nc.sbuf_tensor("qraw", [128, 4 * 1024], BF16))
    rtmp = stA.enter_context(nc.sbuf_tensor("rtmp", [128, 1024], F32))
    kfull = stA.enter_context(nc.sbuf_tensor("kfull", [128, KVH * 1024], BF16))
    vfull = stA.enter_context(nc.sbuf_tensor("vfull", [128, KVH * 1024], BF16))
    vtb = stA.enter_context(nc.sbuf_tensor("vtb", [128, 8 * 1024], BF16))
    wbufA = stA.enter_context(nc.sbuf_tensor("wbufA", [128, 2 * 512], BF16))
    sqbA = stA.enter_context(nc.sbuf_tensor("sqbA", [128, 1024], BF16))

    r = Rec(C)
    r.emit("gpsimd", lambda g: g.dma_start(
        out=xkvb[:, :].rearrange("p (c t) -> p c t", t=TOK),
        in_=xkvT.rearrange("(c p) t -> p c t", p=128)), inc="dG", inc_by=16)
    r.emit("gpsimd", lambda g: g.dma_start(out=ckv[:, 0:TOK], in_=coskv_in),
           inc="dG", inc_by=16)
    r.emit("gpsimd", lambda g: g.dma_start(out=ckv[:, TOK:2 * TOK], in_=sinkv_in),
           inc="dG", inc_by=16)
    r.emit("gpsimd", lambda g: g.wait_ge(sems["dG"], C["dG"]))
    r.realize(nc, sems)

    def proj_block(which, og, wT, dst, cos_sin, xb):
        stP = ExitStack()
        kps = [stP.enter_context(nc.psum_tensor(f"ps{which}{og}_{i}", [128, 1024], F32))
               for i in range(4)]
        r = Rec(C)
        guard = {}
        xload = snap("dG")
        prevdone = [snap("sV"), snap("sA")]
        for c in range(NCH):
            w = [xload] + guard.pop(("w", c % 2), [])
            wl = "dG0" if c % 2 == 0 else "dG1"
            r.emit("gpsimd", lambda g, c=c: g.dma_start(
                out=wbufA[:, (c % 2) * 512:(c % 2) * 512 + 512],
                in_=wT[c * 128:(c + 1) * 128, og * 512:(og + 1) * 512]),
                waits=w, inc=wl, inc_by=16)
            wdone = snap(wl)

            def mmgrp(t, c=c):
                ins = None
                for hh in range(4):
                    lt = wbufA[:, (c % 2) * 512 + hh * 128:(c % 2) * 512 + hh * 128 + 128]
                    for half in range(2):
                        ins = t.matmul(
                            kps[hh][:, half * 512:half * 512 + 512], lt,
                            xb[:, c * 1024 + half * 512:c * 1024 + half * 512 + 512],
                            start=(c == 0), stop=(c == NCH - 1))
                return ins
            r.emit("tensor", mmgrp,
                   waits=[wdone] + (prevdone if c == 0 else []), inc="sPs")
            guard[("w", c % 2)] = [snap("sPs")]
        mmdone = snap("sPs")
        if which in ("k", "q"):
            cos = cos_sin[:, 0:TOK]
            sin = cos_sin[:, TOK:2 * TOK]
            for hh in range(4):
                r.emit("scalar", lambda sc, hh=hh: sc.copy(
                    qraw[:, hh * 1024:(hh + 1) * 1024], kps[hh][:, :]),
                    waits=[mmdone] if hh == 0 else (), inc="sA")
            evdone = snap("sA")
            for hh in range(4):
                dcol = (og * 4 + hh) * 1024

                def rotmm(t, hh=hh):
                    ins = None
                    for half in range(2):
                        ins = t.matmul(kps[hh][:, half * 512:half * 512 + 512],
                                       rotmb[:, :],
                                       qraw[:, hh * 1024 + half * 512:
                                            hh * 1024 + half * 512 + 512],
                                       start=True, stop=True)
                    return ins
                r.emit("tensor", rotmm, waits=[evdone] if hh == 0 else (), inc="sPs")
                rotdone = snap("sPs")

                def rope(v, hh=hh, dcol=dcol):
                    v.tensor_mul(rtmp[:, :], qraw[:, hh * 1024:(hh + 1) * 1024], cos)
                    v.tensor_mul(sqbA[:, :], kps[hh][:, :], sin)
                    return v.tensor_add(dst[:, dcol:dcol + 1024], rtmp[:, :], sqbA[:, :])
                r.emit("vector", rope, waits=[rotdone], inc="sV")
        else:
            for hh in range(4):
                dcol = (og * 4 + hh) * 1024
                r.emit("scalar",
                       lambda sc, hh=hh, dcol=dcol: sc.copy(
                           dst[:, dcol:dcol + 1024], kps[hh][:, :]),
                       waits=[mmdone] if hh == 0 else (), inc="sA")
        r.realize(nc, sems)
        stP.close()

    for og in range(2):
        proj_block("k", og, wkT, kfull, ckv, xkvb)
    for og in range(2):
        proj_block("v", og, wvT, vfull, ckv, xkvb)

    stP = ExitStack()
    vtp = [stP.enter_context(nc.psum_tensor(f"vtp{i}", [128, 128], BF16)) for i in range(2)]
    stat_ps = stP.enter_context(nc.psum_tensor("statps", [1, 1024], F32))
    r = Rec(C)
    guard = {}
    for n in range(64):
        h, tt = divmod(n, 8)
        r.emit("tensor", lambda t, h=h, tt=tt, n=n: t.transpose(
            vtp[n % 2][:, :], vfull[:, h * 1024 + tt * 128:h * 1024 + tt * 128 + 128],
            ident[:, :]),
            waits=guard.pop(("vtp", n % 2), []) + ([snap("sA")] if n == 0 else []),
            inc="sPs")
        w = snap("sPs")
        r.emit("vector", lambda v, h=h, tt=tt, n=n: v.tensor_copy(
            vtb[:, tt * 1024 + h * 128:tt * 1024 + h * 128 + 128], vtp[n % 2][:, :]),
            waits=[w], inc="sV")
        guard[("vtp", n % 2)] = [snap("sV")]
    for h in range(KVH):
        r.emit("vector", lambda v, h=h: v.tensor_mul(
            sqbA[:, :], kfull[:, h * 1024:(h + 1) * 1024], kfull[:, h * 1024:(h + 1) * 1024]),
            waits=guard.pop("sqfree", []), inc="sV")
        w = snap("sV")
        last = None
        for half in range(2):
            last = r.emit("tensor", lambda t, half=half: t.matmul(
                stat_ps[0:1, half * 512:half * 512 + 512], onesb[:, 0:1],
                sqbA[:, half * 512:half * 512 + 512], start=True, stop=True),
                waits=[w] + (guard.pop("statps", []) if half == 0 else []), inc="sPs")
        guard["sqfree"] = [snap("sPs")]
        r.emit("scalar", lambda sc, h=h: sc.copy(
            statrow[0:1, h * 1024:(h + 1) * 1024], stat_ps[0:1, :]),
            waits=[snap("sPs")], inc="sA")
        guard["statps"] = [snap("sA")]
    r.realize(nc, sems)
    stP.close()

    r = Rec(C)
    r.emit("sync", lambda s: s.dma_start(
        out=kc.ap()[0:1024, :].rearrange("(h p) t -> p h t", p=128),
        in_=kfull[:, :].rearrange("p (h t) -> p h t", t=1024)),
        waits=[snap("sV"), snap("sA")], inc="dS", inc_by=16)
    r.emit("sync", lambda s: s.dma_start(
        out=kc.ap()[1024:1032, :].rearrange("(o h) t -> o h t", o=1),
        in_=statrow[0:1, :].rearrange("o (h t) -> o h t", t=1024)), inc="dS", inc_by=16)
    r.emit("sync", lambda s: s.dma_start(
        out=vc.ap()[:, :].rearrange("(tt p) o -> p tt o", p=128),
        in_=vtb[:, :].rearrange("p (tt o) -> p tt o", o=1024)), inc="dS", inc_by=16)
    r.emit("sync", lambda s: s.wait_ge(sems["dS"], C["dS"]))
    r.realize(nc, sems)
    stA.close()

    stQA = ExitStack()
    qfull = stQA.enter_context(nc.sbuf_tensor("qfull", [128, H * 1024], BF16))
    stQ = ExitStack()
    xqb = stQ.enter_context(nc.sbuf_tensor("xqb", [128, NCH * 1024], BF16))
    cq = stQ.enter_context(nc.sbuf_tensor("cq", [128, 2 * TOK], F32))
    qrawQ = stQ.enter_context(nc.sbuf_tensor("qrawQ", [128, 4 * 1024], BF16))
    rtmpQ = stQ.enter_context(nc.sbuf_tensor("rtmpQ", [128, 1024], F32))
    sq2 = stQ.enter_context(nc.sbuf_tensor("sq2", [128, 1024], BF16))
    wbufQ = stQ.enter_context(nc.sbuf_tensor("wbufQ", [128, 2 * 512], BF16))
    wstgQ = stQ.enter_context(nc.sbuf_tensor("wstgQ", [128, 2 * 512], F32))

    r = Rec(C)
    r.emit("gpsimd", lambda g: g.dma_start(
        out=xqb[:, :].rearrange("p (c t) -> p c t", t=TOK),
        in_=xqT.rearrange("(c p) t -> p c t", p=128)), inc="dG", inc_by=16)
    r.emit("gpsimd", lambda g: g.dma_start(out=cq[:, 0:TOK], in_=cosq_in),
           inc="dG", inc_by=16)
    r.emit("gpsimd", lambda g: g.dma_start(out=cq[:, TOK:2 * TOK], in_=sinq_in),
           inc="dG", inc_by=16)
    r.realize(nc, sems)

    r = Rec(C)
    r.emit("gpsimd", lambda g: g.wait_ge(sems["dS"], C["dS"]))
    r.emit("gpsimd", lambda g: g.collective_compute(
        "AllGather", ALU.bypass, replica_groups=[list(range(NCORES))],
        ins=[kc.ap().opt()], outs=[kg.ap().opt()]), inc="cc")
    r.emit("gpsimd", lambda g: g.collective_compute(
        "AllGather", ALU.bypass, replica_groups=[list(range(NCORES))],
        ins=[vc.ap().opt()], outs=[vg.ap().opt()]), inc="cc")
    r.emit("gpsimd", lambda g: g.dma_start(out=mkr[0:1, :], in_=onesf[0:1, 0:16]),
           inc="ccd", inc_by=16)
    r.realize(nc, sems)

    qguard = {}

    def q_block(og):
        stP = ExitStack()
        kps = [stP.enter_context(nc.psum_tensor(f"qps{og}_{i}", [128, 1024], F32))
               for i in range(4)]
        r = Rec(C)
        guard = qguard
        xload = snap("dG")
        prevdone = [snap("sV"), snap("sA")]
        for c in range(NCH):
            w = guard.pop(("ws", c % 2), [])
            wl = "dS0" if c % 2 == 0 else "dS1"
            r.emit("sync", lambda s, c=c: s.dma_start(
                out=wstgQ[:, (c % 2) * 512:(c % 2) * 512 + 512],
                in_=wqT[c * 128:(c + 1) * 128, og * 512:(og + 1) * 512]),
                waits=w, inc=wl, inc_by=16)
            r.emit("vector", lambda v, c=c: v.tensor_copy(
                wbufQ[:, (c % 2) * 512:(c % 2) * 512 + 512],
                wstgQ[:, (c % 2) * 512:(c % 2) * 512 + 512]),
                waits=[snap(wl)] + guard.pop(("w", c % 2), []), inc="sV")
            guard[("ws", c % 2)] = [snap("sV")]
            wdone = snap("sV")

            def mmgrp(t, c=c):
                ins = None
                for hh in range(4):
                    lt = wbufQ[:, (c % 2) * 512 + hh * 128:(c % 2) * 512 + hh * 128 + 128]
                    for half in range(2):
                        ins = t.matmul(
                            kps[hh][:, half * 512:half * 512 + 512], lt,
                            xqb[:, c * 1024 + half * 512:c * 1024 + half * 512 + 512],
                            start=(c == 0), stop=(c == NCH - 1))
                return ins
            r.emit("tensor", mmgrp,
                   waits=[wdone] + ([xload] if c == 0 else []) + (prevdone if c == 0 else []),
                   inc="sPs")
            guard[("w", c % 2)] = [snap("sPs")]
        mmdone = snap("sPs")
        cos = cq[:, 0:TOK]
        sin = cq[:, TOK:2 * TOK]
        for hh in range(4):
            r.emit("scalar", lambda sc, hh=hh: sc.copy(
                qrawQ[:, hh * 1024:(hh + 1) * 1024], kps[hh][:, :]),
                waits=[mmdone] if hh == 0 else (), inc="sA")
        evdone = snap("sA")
        for hh in range(4):
            dcol = (og * 4 + hh) * 1024

            def rotmm(t, hh=hh):
                ins = None
                for half in range(2):
                    ins = t.matmul(kps[hh][:, half * 512:half * 512 + 512],
                                   rotmb[:, :],
                                   qrawQ[:, hh * 1024 + half * 512:
                                        hh * 1024 + half * 512 + 512],
                                   start=True, stop=True)
                return ins
            r.emit("tensor", rotmm, waits=[evdone] if hh == 0 else (), inc="sPs")
            rotdone = snap("sPs")

            def rope(v, hh=hh, dcol=dcol):
                v.tensor_mul(rtmpQ[:, :], qrawQ[:, hh * 1024:(hh + 1) * 1024], cos)
                v.tensor_mul(sq2[:, :], kps[hh][:, :], sin)
                return v.tensor_add(qfull[:, dcol:dcol + 1024], rtmpQ[:, :], sq2[:, :])
            r.emit("vector", rope, waits=[rotdone], inc="sV")
        r.realize(nc, sems)
        stP.close()

    for og in range(8):
        q_block(og)

    stP = ExitStack()
    statq = stP.enter_context(nc.psum_tensor("statq", [1, 1024], F32))
    r = Rec(C)
    guard = {}
    for h in range(H):
        r.emit("vector", lambda v, h=h: v.tensor_mul(
            sq2[:, :], qfull[:, h * 1024:(h + 1) * 1024], qfull[:, h * 1024:(h + 1) * 1024]),
            waits=guard.pop("sqfree", []), inc="sV")
        w = snap("sV")
        for half in range(2):
            r.emit("tensor", lambda t, half=half: t.matmul(
                statq[0:1, half * 512:half * 512 + 512], onesb[:, 0:1],
                sq2[:, half * 512:half * 512 + 512], start=True, stop=True),
                waits=[w] + (guard.pop("statps", []) if half == 0 else []), inc="sPs")
        guard["sqfree"] = [snap("sPs")]
        r.emit("scalar", lambda sc: sc.copy(nqrow[0:1, :], statq[0:1, :]),
               waits=[snap("sPs")] + guard.pop("nqfree", []), inc="sA")
        guard["statps"] = [snap("sA")]
        r.emit("vector", lambda v, h=h: v.reduce_max(
            q2row[0:1, h:h + 1], nqrow[0:1, :], axis=AX.X), waits=[snap("sA")], inc="sV")
        guard["nqfree"] = [snap("sV")]
    r.realize(nc, sems)
    stP.close()
    stQ.close()

    stB = ExitStack()
    krows = stB.enter_context(nc.sbuf_tensor("krows", [8, 8 * 1024], BF16))
    k2pad = stB.enter_context(nc.sbuf_tensor("k2pad", [8, 1], F32))
    k2eps = stB.enter_context(nc.psum_tensor("k2eps", [1, 32], F32))
    btp = stB.enter_context(nc.psum_tensor("btp", [128, 32], F32))
    r = Rec(C)
    r.emit("gpsimd", lambda g: g.wait_ge(sems["cc"], 2))
    for rr in range(NCORES):
        r.emit("gpsimd", lambda g, rr=rr: g.dma_start(
            out=krows[0:8, rr * 1024:(rr + 1) * 1024],
            in_=kg.ap()[rr * 1032 + 1024:rr * 1032 + 1032, :]), inc="dG", inc_by=16)
    wd = snap("dG")
    r.emit("vector", lambda v: v.reduce_max(k2pad[0:8, 0:1], krows[0:8, :], axis=AX.X),
           waits=[wd], inc="sV")
    r.emit("tensor", lambda t: t.matmul(k2eps[0:1, :], k2pad[0:8, 0:1],
                                        exp8b[0:8, 0:32], start=True, stop=True),
           waits=[snap("sV")], inc="sPs")
    r.emit("vector", lambda v: v.tensor_mul(srow[0:1, :], q2row[0:1, :],
                                            k2eps[0:1, :]),
           waits=[snap("sPs")], inc="sV")

    r.emit("scalar", lambda sc: sc.activation(
        srow2[0:1, :], srow[0:1, :], AF.Ln, bias=epsA[0:1, 0:1],
        scale=1.0 / 128.0), waits=[snap("sV")], inc="sA")
    r.emit("scalar", lambda sc: sc.activation(
        srow[0:1, :], srow2[0:1, :], AF.Exp, bias=0.0, scale=0.5),
        waits=[snap("sA")], inc="sA")
    r.emit("vector", lambda v: v.tensor_scalar_mul(srow2[0:1, :], srow[0:1, :], -1.0),
           waits=[snap("sA")], inc="sV")
    r.emit("tensor", lambda t: t.matmul(btp[:, :], onesf[0:1, :], srow2[0:1, 0:32],
                                        start=True, stop=True),
           waits=[snap("sV")], inc="sPs")
    r.emit("scalar", lambda sc: sc.copy(btbl[:, :], btp[:, :]),
           waits=[snap("sPs")], inc="sA")
    r.realize(nc, sems)
    stB.close()

    stT = ExitStack()
    kbuf = stT.enter_context(nc.sbuf_tensor("kbuf", [128, 2 * 1024], BF16))
    vbuf = stT.enter_context(nc.sbuf_tensor("vbuf", [128, 2 * 1024], BF16))
    ptb = stT.enter_context(nc.sbuf_tensor("ptb", [128, 2 * 1024], BF16))
    ascr = stT.enter_context(nc.sbuf_tensor("ascr", [128, 1024], F32))
    dacc = stT.enter_context(nc.sbuf_tensor("dacc", [128, 2 * 1024], F32))
    atile = stT.enter_context(nc.sbuf_tensor("atile", [128, 2 * 1024], BF16))
    drow = stT.enter_context(nc.sbuf_tensor("drow", [1, 1024], F32))
    drow2 = stT.enter_context(nc.sbuf_tensor("drow2", [128, 1024], F32))
    sps = [stT.enter_context(nc.psum_tensor(f"sps{i}", [128, 1024], F32)) for i in range(2)]
    outps = stT.enter_context(nc.psum_tensor("outps", [128, 1024], F32))
    denps = stT.enter_context(nc.psum_tensor("denps", [128, 1024], F32))

    r = Rec(C)
    guard = {}
    r.emit("sync", lambda s: s.wait_ge(sems["ccd"], 16))
    eg = 0
    pair = 0
    pending = None

    def emit_pv(pg):
        waits = [pg["wA"]] + ([pg["wV"]] if pg["diag"] else []) + pg["whead"]

        def pv(t, pg=pg):
            ins = None
            for ti, tt in enumerate(pg["tiles"]):
                rhs = ptb[:, (pg["eg"] % 2) * 1024 + ti * pg["slot"]:
                          (pg["eg"] % 2) * 1024 + ti * pg["slot"] + pg["N"]]
                st = pg["first"] and ti == 0
                ins = t.matmul(
                    outps[:, pg["jlo"] * 128:pg["jlo"] * 128 + pg["N"]],
                    vbuf[:, pg["pp"] * 1024 + tt * 128:pg["pp"] * 1024 + tt * 128 + 128],
                    rhs, start=st, stop=False, skip_group_check=True)
            return ins
        r.emit("tensor", pv, waits=waits, inc="sPv")
        guard[("ptb", pg["eg"] % 2)] = [snap("sPv"), pg["dsum"]]
        if pg["tiles"][-1] == 7:
            guard[("kv", pg["pp"])] = [snap("sPv")]

    headguard = [snap("sA")]
    pending_store = None

    def emit_store(ps):
        hs, dfree = ps
        wl = "dG0" if hs % 2 == 0 else "dG1"
        r.emit("gpsimd", lambda g, hs=hs: g.dma_start(
            out=atnd.ap()[hs * 128:(hs + 1) * 128, :],
            in_=atile[:, (hs % 2) * 1024:(hs % 2) * 1024 + 1024]),
            waits=[dfree], inc=wl, inc_by=16)
        guard[("atile", hs % 2)] = [snap(wl)]

    for h in range(H):
        first_pv_of_head = True
        for (g, b) in attn_pairs():
            pp = pair % 2
            pw = guard.pop(("kv", pp), [])
            kv = h // 4
            wl = "dS0" if pp == 0 else "dS1"
            r.emit("sync", lambda s, b=b, kv=kv, pp=pp: s.dma_start(
                out=kbuf[:, pp * 1024:pp * 1024 + 1024],
                in_=kg.ap()[b * 1032 + kv * 128:b * 1032 + (kv + 1) * 128, :]),
                waits=pw, inc=wl, inc_by=16)
            r.emit("sync", lambda s, b=b, kv=kv, pp=pp: s.dma_start(
                out=vbuf[:, pp * 1024:(pp + 1) * 1024]
                    .rearrange("p (t f) -> p t f", f=128),
                in_=vg.ap()[b * 1024:(b + 1) * 1024, kv * 128:(kv + 1) * 128]
                    .rearrange("(t p) f -> p t f", p=128)), inc=wl, inc_by=16)
            kvdone = snap(wl)
            diag = (g == 0) or (b >= 4)
            for gi, gr in enumerate(attn_groups_of_pair(g, b)):
                tiles, slot, span, N, jlo, jhi = (gr["tiles"], gr["slot"], gr["span"],
                                                  gr["N"], gr["jlo"], gr["jhi"])
                sw = ([kvdone] + guard.pop(("sps", eg % 2), [])
                      + ([snap("sA")] if eg == 0 else []))

                def sgrp(t, tiles=tiles, slot=slot, N=N, jlo=jlo, h=h, pp=pp, eg=eg):
                    ins = None
                    for ti, tt in enumerate(tiles):
                        ins = t.matmul(
                            sps[eg % 2][:, ti * slot:ti * slot + N],
                            kbuf[:, pp * 1024 + tt * 128:pp * 1024 + tt * 128 + 128],
                            qfull[:, h * 1024 + jlo * 128:h * 1024 + jlo * 128 + N],
                            start=True, stop=True)
                    return ins
                r.emit("tensor", sgrp, waits=sw, inc="sPs")
                sdone = snap("sPs")
                ew = [sdone] + guard.pop(("ptb", eg % 2), [])
                r.emit("scalar", lambda sc, eg=eg, span=span, h=h: sc.activation(
                    ptb[:, (eg % 2) * 1024:(eg % 2) * 1024 + span],
                    sps[eg % 2][:, 0:span], AF.Exp,
                    bias=btbl[:, h:h + 1], scale=ISQ), waits=ew, inc="sA")
                guard[("sps", eg % 2)] = [snap("sA")]
                wA = snap("sA")
                wV = None
                if diag:
                    def mgrp(p, tiles=tiles, slot=slot, eg=eg):
                        ins = None
                        for ti, tt in enumerate(tiles):
                            sl = ptb[:, (eg % 2) * 1024 + ti * slot:
                                     (eg % 2) * 1024 + ti * slot + 128]
                            ins = p.tensor_mul(sl, sl,
                                               masks[:, tt * 128:(tt + 1) * 128])
                        return ins
                    r.emit("gpsimd", mgrp, waits=[wA], inc="sF2")
                    wV = snap("sF2")
                    if pending_store is not None:
                        emit_store(pending_store)
                        pending_store = None
                dw = [wA] + ([wV] if diag else []) + guard.pop(("dacc", h % 2), [])

                def dadd(v, tiles=tiles, slot=slot, N=N, jlo=jlo, eg=eg, h=h,
                         first=(b == 0 and gi == 0)):
                    ins = None
                    dst = dacc[:, (h % 2) * 1024 + jlo * 128:
                               (h % 2) * 1024 + jlo * 128 + N]
                    for ti in range(len(tiles)):
                        sl = ptb[:, (eg % 2) * 1024 + ti * slot:
                                 (eg % 2) * 1024 + ti * slot + N]
                        if first and ti == 0:
                            ins = v.tensor_copy(dst, sl)
                        else:
                            ins = v.tensor_add(dst, dst, sl)
                    return ins
                r.emit("vector", dadd, waits=dw, inc="sV")
                dsum = snap("sV")
                if pending is not None:
                    emit_pv(pending)
                pending = dict(tiles=tiles, slot=slot, N=N, jlo=jlo,
                               diag=diag, pp=pp, eg=eg, wA=wA, wV=wV,
                               first=(b == 0 and gi == 0), dsum=dsum,
                               whead=headguard if first_pv_of_head else [])
                first_pv_of_head = False
                eg += 1
            pair += 1
        emit_pv(pending)
        pending = None
        pvall = snap("sPv")
        dnw = [snap("sV")] + guard.pop("denps", [])

        def denmm(t, h=h):
            ins = None
            for half in range(2):
                ins = t.matmul(denps[0:1, half * 512:half * 512 + 512],
                               onesf[:, 0:1],
                               dacc[:, (h % 2) * 1024 + half * 512:
                                    (h % 2) * 1024 + half * 512 + 512],
                               start=True, stop=True, skip_group_check=True)
            return ins
        r.emit("tensor", denmm, waits=dnw, inc="sPs")
        guard[("dacc", h % 2)] = [snap("sPs")]
        dmmdone = snap("sPs")
        r.emit("scalar", lambda sc: sc.copy(ascr[:, :], outps[:, :]),
               waits=[pvall] + guard.pop("ascr", []), inc="sA")
        outfree = snap("sA")
        r.emit("scalar", lambda sc: sc.copy(drow[0:1, :], denps[0:1, :]),
               waits=[dmmdone], inc="sA")
        rw = snap("sA")

        def bcast(t):
            ins = None
            for half in range(2):
                ins = t.matmul(denps[:, half * 512:half * 512 + 512],
                               onesf[0:1, 0:128],
                               drow[0:1, half * 512:half * 512 + 512],
                               start=True, stop=True, skip_group_check=True)
            return ins
        r.emit("tensor", bcast, waits=[rw], inc="sPs")
        r.emit("vector", lambda v: v.reciprocal(drow2[:, :], denps[:, :]),
               waits=[snap("sPs")] + guard.pop("drow2", []), inc="sV")
        rcpw = snap("sV")
        fw = ([rcpw, outfree] + guard.pop(("atile", h % 2), []))
        r.emit("vector", lambda v, h=h: v.tensor_mul(
            atile[:, (h % 2) * 1024:(h % 2) * 1024 + 1024],
            ascr[:, :], drow2[:, :]), waits=fw, inc="sV")
        denfree = snap("sV")
        guard["drow2"] = [denfree]
        guard["ascr"] = [denfree]
        pending_store = (h, denfree)
        guard["denps"] = [denfree]
        headguard = [outfree]
    emit_store(pending_store)
    pending_store = None
    g0, g1 = snap("dG0"), snap("dG1")
    r.emit("gpsimd", lambda g: g.wait_ge(sems["dG0"], g0[1]))
    r.emit("gpsimd", lambda g: g.wait_ge(sems["dG1"], g1[1]))
    r.realize(nc, sems)
    stT.close()
    stQA.close()

    stL1 = ExitStack()
    ln1full = stL1.enter_context(nc.sbuf_tensor("ln1full", [128, 8 * 4096], BF16))
    stO = ExitStack()
    oacc = stO.enter_context(nc.sbuf_tensor("oacc", [128, 4 * 4096], F32))
    atb = stO.enter_context(nc.sbuf_tensor("atb", [128, 2 * 512], BF16))
    wob = stO.enter_context(nc.sbuf_tensor("wob", [128, 2 * 1024], BF16))
    xqq = stO.enter_context(nc.sbuf_tensor("xqq", [128, 2 * 1024], F32))
    sqf = stO.enter_context(nc.sbuf_tensor("sqf", [128, 4096], F32))
    ln1f = stO.enter_context(nc.sbuf_tensor("ln1f", [128, 4096], F32))
    ostat = stO.enter_context(nc.sbuf_tensor("ostat", [128, 28], F32))
    psO = [stO.enter_context(nc.psum_tensor(f"psO{i}", [128, 1024], F32)) for i in range(4)]

    r = Rec(C)
    guard = {}
    n = 0
    for ttg in range(2):
        for hq in range(4):
            for oc in range(NCH):
                wl = "dG0" if n % 2 == 0 else "dG1"
                r.emit("gpsimd", lambda g, hq=hq, oc=oc, n=n: g.dma_start(
                    out=wob[:, (n % 2) * 1024:(n % 2) * 1024 + 1024],
                    in_=woT[oc * 128:(oc + 1) * 128, hq * 1024:(hq + 1) * 1024]),
                    waits=guard.pop(("wob", n % 2), []), inc=wl, inc_by=16)
                wG = snap(wl)
                sl = "dS0" if n % 2 == 0 else "dS1"
                r.emit("sync", lambda s, ttg=ttg, oc=oc, n=n: s.dma_start(
                    out=atb[:, (n % 2) * 512:(n % 2) * 512 + 512],
                    in_=atnd.ap()[oc * 128:(oc + 1) * 128, ttg * 512:(ttg + 1) * 512]),
                    waits=guard.pop(("atb", n % 2), []), inc=sl, inc_by=16)
                wS = snap(sl)
                mmw = [wG, wS] + (guard.pop("psO", []) if oc == 0 else [])

                def mmo(t, oc=oc, n=n):
                    ins = None
                    for tt in range(4):
                        for half in range(2):
                            ins = t.matmul(
                                psO[tt][:, half * 512:half * 512 + 512],
                                atb[:, (n % 2) * 512 + tt * 128:(n % 2) * 512 + tt * 128 + 128],
                                wob[:, (n % 2) * 1024 + half * 512:
                                    (n % 2) * 1024 + half * 512 + 512],
                                start=(oc == 0), stop=(oc == NCH - 1))
                    return ins
                r.emit("tensor", mmo, waits=mmw, inc="sPs")
                guard[("wob", n % 2)] = [snap("sPs")]
                guard[("atb", n % 2)] = [snap("sPs")]
                n += 1
            mmdone = snap("sPs")
            addsnap = None
            for tt in range(4):
                m = (ttg * 4 + hq) * 4 + tt
                sl = "dS2" if m % 2 == 0 else "dS3"
                r.emit("sync", lambda s, ttg=ttg, tt=tt, hq=hq, m=m: s.dma_start(
                    out=xqq[:, (m % 2) * 1024:(m % 2) * 1024 + 1024],
                    in_=xq[(ttg * 4 + tt) * 128:(ttg * 4 + tt + 1) * 128,
                           hq * 1024:(hq + 1) * 1024]),
                    waits=guard.pop(("xqq", m % 2), []), inc=sl, inc_by=16)
                wS = snap(sl)
                r.emit("vector", lambda v, tt=tt, hq=hq, m=m: v.tensor_add(
                    oacc[:, tt * 4096 + hq * 1024:tt * 4096 + hq * 1024 + 1024],
                    psO[tt][:, :], xqq[:, (m % 2) * 1024:(m % 2) * 1024 + 1024]),
                    waits=[mmdone, wS], inc="sV")
                guard[("xqq", m % 2)] = [snap("sV")]
            guard["psO"] = [snap("sV")]
        SR, SF, wln = ln_chain(r, ostat, 4,
                               [oacc[:, tt * 4096:(tt + 1) * 4096]
                                for tt in range(4)], sqf)
        for tt in range(4):
            k = ttg * 4 + tt
            a = oacc[:, tt * 4096:(tt + 1) * 4096]
            r.emit("vector", lambda v, a=a, tt=tt, SR=SR, SF=SF: v.tensor_scalar(
                ln1f[:, :], a, SR[:, tt:tt + 1], SF[:, tt:tt + 1],
                ALU.mult, ALU.add),
                waits=[wln] + guard.pop("ln1f", []), inc="sV")
            nrm = snap("sV")
            r.emit("sync", lambda s, k=k: s.dma_start(
                out=r2d.ap()[k * 128:(k + 1) * 128, :],
                in_=ln1f[:, :]),
                waits=[nrm], inc="dO", inc_by=16)
            r.emit("vector", lambda v, k=k: v.tensor_copy(
                ln1full[:, k * 4096:(k + 1) * 4096],
                ln1f[:, :]), waits=[nrm], inc="sV")
            guard["ln1f"] = [snap("dO"), snap("sV")]
    r.emit("sync", lambda s: s.wait_ge(sems["dO"], C["dO"]))
    r.realize(nc, sems)
    stO.close()

    stL = ExitStack()
    xt2st = stL.enter_context(nc.sbuf_tensor("xt2st", [128, 2 * 4096], BF16))
    ptL = [stL.enter_context(nc.psum_tensor(f"ptL{i}", [128, 128], BF16)) for i in range(2)]
    r = Rec(C)
    guard = {}
    for tt in range(8):
        for c in range(NCH):
            nn = tt * NCH + c
            r.emit("tensor", lambda t, tt=tt, c=c, nn=nn: t.transpose(
                ptL[nn % 2][:, :],
                ln1full[:, tt * 4096 + c * 128:tt * 4096 + c * 128 + 128], ident[:, :]),
                waits=guard.pop(("ptL", nn % 2), []) + ([snap("sV")] if nn == 0 else []),
                inc="sT")
            r.emit("vector", lambda v, tt=tt, c=c, nn=nn: v.tensor_copy(
                xt2st[:, (tt % 2) * 4096 + c * 128:(tt % 2) * 4096 + c * 128 + 128],
                ptL[nn % 2][:, :]),
                waits=[snap("sT")] + (guard.pop(("xt2st", tt % 2), []) if c == 0 else []),
                inc="sV")
            guard[("ptL", nn % 2)] = [snap("sV")]
        sl = "dS2" if tt % 2 == 0 else "dS3"
        r.emit("sync", lambda s, tt=tt: s.dma_start(
            out=xt2d.ap().rearrange("(c p) t -> p c t", p=128)[:, :, tt * 128:(tt + 1) * 128],
            in_=xt2st[:, (tt % 2) * 4096:(tt % 2) * 4096 + 4096]
                .rearrange("p (c f) -> p c f", f=128)),
            waits=[snap("sV")], inc=sl, inc_by=16)
        guard[("xt2st", tt % 2)] = [snap(sl)]
    r.realize(nc, sems)
    stL.close()
    stL1.close()

    stF = ExitStack()
    xt2b = stF.enter_context(nc.sbuf_tensor("xt2b", [128, NCH * 1024], BF16))
    gwb = stF.enter_context(nc.sbuf_tensor("gwb", [128, 2 * 4096], BF16))
    uwb = stF.enter_context(nc.sbuf_tensor("uwb", [128, 2 * 4096], BF16))
    gsil = stF.enter_context(nc.sbuf_tensor("gsil", [128, 2 * 1024], F32))
    mid1 = stF.enter_context(nc.sbuf_tensor("mid1", [128, 1024], F32))
    midb2 = stF.enter_context(nc.sbuf_tensor("midb2", [128, 2 * 1024], BF16))
    gps = [stF.enter_context(nc.psum_tensor(f"gps{i}", [128, 1024], F32)) for i in range(2)]
    ups = [stF.enter_context(nc.psum_tensor(f"ups{i}", [128, 1024], F32)) for i in range(2)]

    r = Rec(C)
    guard = {}
    r.emit("gpsimd", lambda g: g.dma_start(
        out=xt2b[:, :].rearrange("p (c t) -> p c t", t=TOK),
        in_=xt2d.ap().rearrange("(c p) t -> p c t", p=128)),
        waits=[snap("dS2"), snap("dS3")], inc="dG", inc_by=16)
    xload = snap("dG")
    for ft in range(FT):
        gw = guard.pop(("wf", ft % 2), []) + ([xload] if ft < 2 else [])
        wl = "dG0" if ft % 2 == 0 else "dG1"
        r.emit("gpsimd", lambda g, ft=ft: g.dma_start(
            out=gwb[:, (ft % 2) * 4096:(ft % 2) * 4096 + 4096]
                .rearrange("p (c f) -> p c f", f=128),
            in_=gwT.rearrange("(c p) f -> p c f", p=128)[:, :, ft * 128:(ft + 1) * 128]),
            waits=gw, inc=wl, inc_by=16)
        r.emit("gpsimd", lambda g, ft=ft: g.dma_start(
            out=uwb[:, (ft % 2) * 4096:(ft % 2) * 4096 + 4096]
                .rearrange("p (c f) -> p c f", f=128),
            in_=uwT.rearrange("(c p) f -> p c f", p=128)[:, :, ft * 128:(ft + 1) * 128]),
            inc=wl, inc_by=16)
        wG = snap(wl)
        mmw = [wG] + guard.pop(("gps", ft % 2), []) + guard.pop(("ups", ft % 2), [])

        def mmf(t, ft=ft):
            ins = None
            for c in range(NCH):
                for (ps, wb) in ((gps, gwb), (ups, uwb)):
                    lt = wb[:, (ft % 2) * 4096 + c * 128:(ft % 2) * 4096 + c * 128 + 128]
                    for half in range(2):
                        ins = t.matmul(
                            ps[ft % 2][:, half * 512:half * 512 + 512], lt,
                            xt2b[:, c * 1024 + half * 512:c * 1024 + half * 512 + 512],
                            start=(c == 0), stop=(c == NCH - 1))
            return ins
        r.emit("tensor", mmf, waits=mmw, inc="sPs")
        guard[("wf", ft % 2)] = [snap("sPs")]
        mmdone = snap("sPs")
        r.emit("scalar", lambda sc, ft=ft: sc.activation(
            gsil[:, (ft % 2) * 1024:(ft % 2) * 1024 + 1024], gps[ft % 2][:, :], AF.Silu,
            bias=0.0, scale=gsb[:, ft:ft + 1]),
            waits=[mmdone] + guard.pop(("gsil", ft % 2), []), inc="sA")
        guard[("gps", ft % 2)] = [snap("sA")]
        wA = snap("sA")

        def midop(v, ft=ft):
            v.tensor_mul(mid1[:, :], ups[ft % 2][:, :],
                         gsil[:, (ft % 2) * 1024:(ft % 2) * 1024 + 1024])
            return v.tensor_scalar(midb2[:, (ft % 2) * 1024:(ft % 2) * 1024 + 1024],
                                   mid1[:, :], usb[:, ft:ft + 1], None, ALU.mult)
        r.emit("vector", midop, waits=[wA] + guard.pop(("midb", ft % 2), []), inc="sV")
        guard[("ups", ft % 2)] = [snap("sV")]
        guard[("gsil", ft % 2)] = [snap("sV")]
        sl = "dS0" if ft % 2 == 0 else "dS1"
        r.emit("sync", lambda s, ft=ft: s.dma_start(
            out=midd.ap()[ft * 128:(ft + 1) * 128, :],
            in_=midb2[:, (ft % 2) * 1024:(ft % 2) * 1024 + 1024]),
            waits=[snap("sV")], inc=sl, inc_by=16)
        guard[("midb", ft % 2)] = [snap(sl)]
    r.emit("sync", lambda s: s.wait_ge(sems["dS0"], C["dS0"]))
    r.emit("sync", lambda s: s.wait_ge(sems["dS1"], C["dS1"]))
    r.realize(nc, sems)
    stF.close()

    stD = ExitStack()
    dstage = stD.enter_context(nc.sbuf_tensor("dstage", [128, NCH * 1024], BF16))
    dwb = stD.enter_context(nc.sbuf_tensor("dwb", [128, 2 * 512], BF16))
    midb = stD.enter_context(nc.sbuf_tensor("midb", [128, 2 * 1024], BF16))
    stDp = ExitStack()
    dps = [stDp.enter_context(nc.psum_tensor(f"dps{i}", [128, 1024], F32)) for i in range(4)]

    r = Rec(C)
    guard = {}
    for dg in range(8):
        for fc in range(FT):
            nn = dg * FT + fc
            wl = "dG0" if nn % 2 == 0 else "dG1"
            r.emit("gpsimd", lambda g, dg=dg, fc=fc, nn=nn: g.dma_start(
                out=dwb[:, (nn % 2) * 512:(nn % 2) * 512 + 512],
                in_=dwT[fc * 128:(fc + 1) * 128, dg * 512:(dg + 1) * 512]),
                waits=guard.pop(("dwb", nn % 2), []), inc=wl, inc_by=16)
            wG = snap(wl)
            sl = "dS2" if nn % 2 == 0 else "dS3"
            r.emit("sync", lambda s, fc=fc, nn=nn: s.dma_start(
                out=midb[:, (nn % 2) * 1024:(nn % 2) * 1024 + 1024],
                in_=midd.ap()[fc * 128:(fc + 1) * 128, :]),
                waits=guard.pop(("midb", nn % 2), []), inc=sl, inc_by=16)
            wS = snap(sl)
            mmw = [wG, wS] + (guard.pop("dps", []) if fc == 0 else [])

            def mmd(t, fc=fc, nn=nn):
                ins = None
                for dt in range(4):
                    lt = dwb[:, (nn % 2) * 512 + dt * 128:(nn % 2) * 512 + dt * 128 + 128]
                    for half in range(2):
                        ins = t.matmul(
                            dps[dt][:, half * 512:half * 512 + 512], lt,
                            midb[:, (nn % 2) * 1024 + half * 512:(nn % 2) * 1024 + half * 512 + 512],
                            start=(fc == 0), stop=(fc == FT - 1))
                return ins
            r.emit("tensor", mmd, waits=mmw, inc="sPs")
            guard[("dwb", nn % 2)] = [snap("sPs")]
            guard[("midb", nn % 2)] = [snap("sPs")]
        mmdone = snap("sPs")
        last = None
        for dt in range(4):
            last = r.emit("scalar", lambda sc, dg=dg, dt=dt: sc.activation(
                dstage[:, (dg * 4 + dt) * 1024:(dg * 4 + dt) * 1024 + 1024],
                dps[dt][:, :], AF.Copy, bias=0.0, scale=dsb[:, dg * 4 + dt:dg * 4 + dt + 1]),
                waits=[mmdone] if dt == 0 else (), inc="sA")
        guard["dps"] = [snap("sA")]
    r.realize(nc, sems)
    stDp.close()

    stZ = ExitStack()
    ftok = stZ.enter_context(nc.sbuf_tensor("ftok", [128, 2 * 4096], F32))
    r2b = stZ.enter_context(nc.sbuf_tensor("r2b", [128, 2 * 4096], F32))
    fadd = stZ.enter_context(nc.sbuf_tensor("fadd", [128, 2 * 4096], F32))
    sqz = stZ.enter_context(nc.sbuf_tensor("sqz", [128, 4096], F32))
    zstat = stZ.enter_context(nc.sbuf_tensor("zstat", [128, 8], F32))
    ptZ = [stZ.enter_context(nc.psum_tensor(f"ptZ{i}", [128, 128], BF16)) for i in range(2)]

    r = Rec(C)
    guard = {}
    for tt in range(8):
        sl = "dS0" if tt % 2 == 0 else "dS1"
        r.emit("sync", lambda s, tt=tt: s.dma_start(
            out=r2b[:, (tt % 2) * 4096:(tt % 2) * 4096 + 4096],
            in_=r2d.ap()[tt * 128:(tt + 1) * 128, :]),
            waits=guard.pop(("r2b", tt % 2), []), inc=sl, inc_by=16)
        wS = snap(sl)
        for c in range(NCH):
            nn = tt * NCH + c
            r.emit("tensor", lambda t, tt=tt, c=c, nn=nn: t.transpose(
                ptZ[nn % 2][:, :],
                dstage[:, c * 1024 + tt * 128:c * 1024 + tt * 128 + 128], ident[:, :]),
                waits=guard.pop(("ptZ", nn % 2), []) + ([snap("sA")] if nn == 0 else []),
                inc="sT")
            r.emit("scalar", lambda sc, tt=tt, c=c, nn=nn: sc.copy(
                ftok[:, (tt % 2) * 4096 + c * 128:(tt % 2) * 4096 + c * 128 + 128],
                ptZ[nn % 2][:, :]),
                waits=[snap("sT")] + (guard.pop(("ftok", tt % 2), []) if c == 0 else []),
                inc="sA")
            guard[("ptZ", nn % 2)] = [snap("sA")]
        evdone = snap("sA")
        fa = fadd[:, (tt % 2) * 4096:(tt % 2) * 4096 + 4096]
        r.emit("vector", lambda v, tt=tt, fa=fa: v.tensor_add(
            fa, ftok[:, (tt % 2) * 4096:(tt % 2) * 4096 + 4096],
            r2b[:, (tt % 2) * 4096:(tt % 2) * 4096 + 4096]),
            waits=[wS, evdone] + guard.pop(("fadd", tt % 2), []), inc="sV")
        guard[("r2b", tt % 2)] = [snap("sV")]
        SR, SF, wln = ln_chain(r, zstat, 1, [fa], sqz,
                               first_waits=[snap("sV")])
        r.emit("vector", lambda v, tt=tt, fa=fa, SR=SR, SF=SF: v.tensor_scalar(
            ftok[:, (tt % 2) * 4096:(tt % 2) * 4096 + 4096], fa,
            SR[:, 0:1], SF[:, 0:1], ALU.mult, ALU.add),
            waits=[wln], inc="sV")
        guard[("fadd", tt % 2)] = [snap("sV")]
        so = "dS2" if tt % 2 == 0 else "dS3"
        r.emit("sync", lambda s, tt=tt: s.dma_start(
            out=out[tt * 128:(tt + 1) * 128, :],
            in_=ftok[:, (tt % 2) * 4096:(tt % 2) * 4096 + 4096]),
            waits=[snap("sV")], inc=so, inc_by=16)
        guard[("ftok", tt % 2)] = [snap(so)]
    r.emit("sync", lambda s: s.wait_ge(sems["dS2"], C["dS2"]))
    r.emit("sync", lambda s: s.wait_ge(sems["dS3"], C["dS3"]))
    r.realize(nc, sems)
    stZ.close()
    stD.close()
    glob.close()
    return nc



_ROPE_THETA = 10000.0


def _rope_tables(pos):
    freqs = np.exp(-np.arange(0, 128, 2, dtype=np.float32)
                   * np.float32(math.log(_ROPE_THETA) / 128.0))
    ang = pos.astype(np.float32)[None, :] * freqs[:, None]
    cos = np.cos(ang).astype(np.float32)
    sin = np.sin(ang).astype(np.float32)
    return (np.ascontiguousarray(np.vstack([cos, cos])),
            np.ascontiguousarray(np.vstack([sin, sin])))


def _rot_matrix():
    m = np.zeros((128, 128), dtype=BF16NP)
    for d in range(64):
        m[64 + d, d] = -1
        m[d, 64 + d] = 1
    return m


def _stage_inputs(inputs):
    x = np.asarray(inputs["L0_attn_hidden_states"], np.float32)[0]
    Wq = np.asarray(inputs["L0_attn_Wq"], np.float32)
    Wk = np.asarray(inputs["L0_attn_Wk"], np.float32)
    Wv = np.asarray(inputs["L0_attn_Wv"], np.float32)
    Wo = np.asarray(inputs["L0_attn_Wo"], np.float32)
    gw = np.asarray(inputs["L0_ffn_gate_w"], np.int32)
    uw = np.asarray(inputs["L0_ffn_up_w"], np.int32)
    dw = np.asarray(inputs["L0_ffn_down_w"], np.int32)
    gs = np.asarray(inputs["L0_ffn_gate_s"], np.float32)
    us = np.asarray(inputs["L0_ffn_up_s"], np.float32)
    ds = np.asarray(inputs["L0_ffn_down_s"], np.float32)

    shared = {
        "wqT": np.ascontiguousarray(Wq.T),
        "wkT": np.ascontiguousarray(Wk.T),
        "wvT": np.ascontiguousarray(Wv.T),
        "woT": np.ascontiguousarray(Wo.T),
        "gwT": np.ascontiguousarray(gw.T),
        "uwT": np.ascontiguousarray(uw.T),
        "dwT": np.ascontiguousarray(dw.T),
        "gs": np.ascontiguousarray(gs.reshape(FT, 128).T),
        "us": np.ascontiguousarray(us.reshape(FT, 128).T),
        "ds": np.ascontiguousarray(ds.reshape(NCH, 128).T),
        "ident": np.eye(128, dtype=BF16NP),
        "identf": np.eye(128, dtype=np.float32),
        "onesi": np.ones((128, 128), np.float32),
        "rotm": _rot_matrix(),
        "exp8": np.ascontiguousarray(
            (np.arange(32)[None, :] // 4 == np.arange(8)[:, None]).astype(np.float32)),
    }
    xr = x.reshape(64, 128, D)
    in_maps = []
    for i in range(NCORES):
        xqi = np.ascontiguousarray(xr[i::8].reshape(TOK, D))
        xkvi = x[i * TOK:(i + 1) * TOK]
        pos_q = (np.arange(8)[:, None] * 1024 + i * 128 + np.arange(128)[None, :]).reshape(-1)
        pos_kv = np.arange(i * TOK, (i + 1) * TOK)
        cosq, sinq = _rope_tables(pos_q)
        coskv, sinkv = _rope_tables(pos_kv)
        m = np.zeros((128, 1024), dtype=BF16NP)
        for t in range(8):
            if t < i:
                m[:, t * 128:(t + 1) * 128] = 1
            elif t == i:
                m[:, t * 128:(t + 1) * 128] = np.triu(np.ones((128, 128))).astype(BF16NP)
        d = dict(shared)
        d.update({
            "xq": xqi,
            "xqT": np.ascontiguousarray(xqi.T),
            "xkvT": np.ascontiguousarray(xkvi.T),
            "cosq": cosq, "sinq": sinq, "coskv": coskv, "sinkv": sinkv,
            "masks": m,
        })
        in_maps.append(d)
    return in_maps


_NC_CACHE = None


def kernel(**inputs):
    global _NC_CACHE
    in_maps = _stage_inputs(inputs)
    if _NC_CACHE is None:
        _NC_CACHE = build_program()
    res = run_bass_kernel_spmd(_NC_CACHE, in_maps, core_ids=list(range(NCORES)))
    arr = np.stack([res.results[i]["out"] for i in range(NCORES)])
    arr = arr.reshape(NCORES, 8, 128, D).transpose(1, 0, 2, 3).reshape(1, S, D)
    return np.ascontiguousarray(arr)


if __name__ == "__main__":
    import time
    t0 = time.time()
    nc = build_program()
    print(f"build ok in {time.time() - t0:.1f}s; instructions: {nc.n_instructions()}")


# version 10
# speedup vs baseline: 1.0583x; 1.0048x over previous
import math
from contextlib import ExitStack

import numpy as np
import ml_dtypes
import concourse.bass as bass
import concourse.mybir as mybir
from concourse.bass_utils import run_bass_kernel_spmd

F32 = mybir.dt.float32
BF16 = mybir.dt.bfloat16
I32 = mybir.dt.int32
AF = mybir.ActivationFunctionType
ALU = mybir.AluOpType
AX = mybir.AxisListType

D = 4096
S = 8192
NCORES = 8
H = 32
KVH = 8
DFF = 11008
FT = DFF // 128
TOK = 1024
NCH = D // 128
ISQ = 1.0 / math.sqrt(128.0)
BF16NP = ml_dtypes.bfloat16
ENGS = ["sync", "gpsimd", "tensor", "scalar", "vector"]



def attn_pairs():
    return [(0, b) for b in range(4)] + [(1, b) for b in range(8)]


def attn_groups_of_pair(g, b):
    jlo = b if g == 0 else max(b, 4)
    jhi = 4 * g + 3
    N = (jhi - jlo + 1) * 128
    tpg = {512: 2, 384: 2, 256: 4, 128: 8}[N]
    slot = 512 if N >= 384 else N
    return [dict(tiles=list(range(s0, s0 + tpg)), slot=slot,
                 span=(tpg - 1) * slot + N, N=N, jlo=jlo, jhi=jhi)
            for s0 in range(0, 8, tpg)]


class Rec:

    def __init__(self, C):
        self.ops = {e: [] for e in ENGS}
        self.C = C

    def emit(self, eng, fn, waits=(), inc=None, inc_by=1):
        ws = [(s, v) for (s, v) in waits if v > 0]
        self.ops[eng].append((fn, ws, inc, inc_by))
        if inc is not None:
            self.C[inc] += inc_by
        return None if inc is None else (inc, self.C[inc])

    def realize(self, nc, sems):
        with nc.Block() as block:
            for ename in ENGS:
                if not self.ops[ename]:
                    continue

                def mk(ename):
                    def body(eng):
                        for fn, ws, inc, inc_by in self.ops[ename]:
                            for (s, v) in ws:
                                eng.wait_ge(sems[s], v)
                            ins = fn(eng)
                            if inc is not None:
                                ins.then_inc(sems[inc], inc_by)
                    return body
                getattr(block, ename)(mk(ename))


def build_program():
    nc = bass.Bass("TRN2", target_bir_lowering=False)

    xqT = nc.dram_tensor("xqT", [D, TOK], F32, kind="ExternalInput").ap()
    xq = nc.dram_tensor("xq", [TOK, D], F32, kind="ExternalInput").ap()
    xkvT = nc.dram_tensor("xkvT", [D, TOK], F32, kind="ExternalInput").ap()
    wqT = nc.dram_tensor("wqT", [D, D], F32, kind="ExternalInput").ap()
    wkT = nc.dram_tensor("wkT", [D, KVH * 128], F32, kind="ExternalInput").ap()
    wvT = nc.dram_tensor("wvT", [D, KVH * 128], F32, kind="ExternalInput").ap()
    woT = nc.dram_tensor("woT", [D, D], F32, kind="ExternalInput").ap()
    gwT = nc.dram_tensor("gwT", [D, DFF], I32, kind="ExternalInput").ap()
    uwT = nc.dram_tensor("uwT", [D, DFF], I32, kind="ExternalInput").ap()
    dwT = nc.dram_tensor("dwT", [DFF, D], I32, kind="ExternalInput").ap()
    gs_in = nc.dram_tensor("gs", [128, FT], F32, kind="ExternalInput").ap()
    us_in = nc.dram_tensor("us", [128, FT], F32, kind="ExternalInput").ap()
    ds_in = nc.dram_tensor("ds", [128, NCH], F32, kind="ExternalInput").ap()
    cosq_in = nc.dram_tensor("cosq", [128, TOK], F32, kind="ExternalInput").ap()
    sinq_in = nc.dram_tensor("sinq", [128, TOK], F32, kind="ExternalInput").ap()
    coskv_in = nc.dram_tensor("coskv", [128, TOK], F32, kind="ExternalInput").ap()
    sinkv_in = nc.dram_tensor("sinkv", [128, TOK], F32, kind="ExternalInput").ap()
    rotm_in = nc.dram_tensor("rotm", [128, 128], BF16, kind="ExternalInput").ap()
    masks_in = nc.dram_tensor("masks", [128, 1024], BF16, kind="ExternalInput").ap()
    exp8_in = nc.dram_tensor("exp8", [8, 32], F32, kind="ExternalInput").ap()
    ident_in = nc.dram_tensor("ident", [128, 128], BF16, kind="ExternalInput").ap()
    identf_in = nc.dram_tensor("identf", [128, 128], F32, kind="ExternalInput").ap()
    ones_in = nc.dram_tensor("onesi", [128, 128], F32, kind="ExternalInput").ap()
    out = nc.dram_tensor("out", [TOK, D], F32, kind="ExternalOutput").ap()

    kc = nc.dram_tensor("kc", [1032, 1024], BF16)
    vc = nc.dram_tensor("vc", [1024, 1024], BF16)
    kg = nc.dram_tensor("kg", [NCORES * 1032, 1024], BF16)
    vg = nc.dram_tensor("vg", [NCORES * 1024, 1024], BF16)
    atnd = nc.dram_tensor("atnd", [D, TOK], BF16)
    r2d = nc.dram_tensor("r2d", [TOK, D], F32)
    xt2d = nc.dram_tensor("xt2d", [D, TOK], BF16)
    midd = nc.dram_tensor("midd", [DFF, 512], BF16)

    glob = ExitStack()

    def sb(name, shape, dtype):
        return glob.enter_context(nc.sbuf_tensor(name, shape, dtype))

    ident = sb("identt", [128, 128], BF16)
    rotmb = sb("rotmbt", [128, 128], BF16)
    identf = sb("identft", [128, 128], F32)
    onesf = sb("onesft", [128, 128], F32)
    onesb = sb("onesbt", [128, 1], BF16)
    masks = sb("maskst", [128, 1024], BF16)
    gsb = sb("gsbt", [128, FT], F32)
    usb = sb("usbt", [128, FT], F32)
    dsb = sb("dsbt", [128, NCH], F32)
    btbl = sb("btblt", [128, 32], F32)
    exp8b = sb("exp8bt", [8, 32], F32)
    nqrow = sb("nqrowt", [1, 1024], F32)
    q2row = sb("q2rowt", [1, 32], F32)
    k2row = sb("k2rowt", [1, 128], F32)
    srow = sb("srowt", [1, 32], F32)
    srow2 = sb("srow2t", [1, 32], F32)
    epsA = sb("epsAt", [128, 1], F32)

    mkr = sb("mkrt", [1, 16], F32)

    sems = {}
    for name in ["dG", "dG0", "dG1", "dS", "dS0", "dS1", "dS2", "dS3", "dO",
                 "cc", "ccd", "sPs", "sPv", "sA", "sV", "sF1", "sT", "sF2"]:
        sems[name] = glob.enter_context(nc.semaphore(name))
    C = {k: 0 for k in sems}

    def snap(sem):
        return (sem, C[sem])

    def ln_chain(r, stat, B, a_list, sqscr, first_waits=()):
        SA = stat[:, 0 * B:1 * B]
        SB = stat[:, 1 * B:2 * B]
        SC = stat[:, 2 * B:3 * B]
        SD = stat[:, 3 * B:4 * B]
        SE = stat[:, 4 * B:5 * B]
        SR = stat[:, 5 * B:6 * B]
        SF = stat[:, 6 * B:7 * B]

        def v1(v):
            ins = None
            for t, a in enumerate(a_list):
                v.reduce_sum(SA[:, t:t + 1], a, axis=AX.X)
                ins = v.scalar_tensor_tensor(
                    sqscr[:, :], a, 1.0, a, ALU.mult, ALU.mult,
                    accum_out=SB[:, t:t + 1])
            return ins
        r.emit("vector", v1, waits=list(first_waits), inc="sV")
        r.emit("scalar", lambda sc: sc.activation(
            SC, SA, AF.Square, bias=0.0, scale=1.0 / 4096.0),
            waits=[snap("sV")], inc="sA")
        r.emit("vector", lambda v: v.scalar_tensor_tensor(
            SD, SB, 1.0 / 4096.0, SC, ALU.mult, ALU.subtract),
            waits=[snap("sA")], inc="sV")
        r.emit("scalar", lambda sc: sc.activation(
            SE, SD, AF.Ln, bias=epsA[:, 0:1], scale=1.0),
            waits=[snap("sV")], inc="sA")
        r.emit("scalar", lambda sc: sc.activation(
            SR, SE, AF.Exp, bias=0.0, scale=-0.5),
            waits=[snap("sA")], inc="sA")
        r.emit("vector", lambda v: v.scalar_tensor_tensor(
            SF, SA, -1.0 / 4096.0, SR, ALU.mult, ALU.mult),
            waits=[snap("sA")], inc="sV")
        return SR, SF, snap("sV")

    r = Rec(C)
    for dst, src in [(ident, ident_in), (identf, identf_in), (onesf, ones_in),
                     (masks, masks_in), (gsb, gs_in), (usb, us_in), (dsb, ds_in),
                     (rotmb, rotm_in), (exp8b, exp8_in)]:
        r.emit("gpsimd", lambda g, d=dst, s=src: g.dma_start(out=d[:, :], in_=s),
               inc="dG", inc_by=16)
    r.emit("gpsimd", lambda g: g.wait_ge(sems["dG"], C["dG"]))
    r.emit("vector", lambda v: v.memset(onesb[:, :], 1.0))
    r.emit("vector", lambda v: v.memset(epsA[:, :], 1e-5))
    r.realize(nc, sems)

    stA = ExitStack()
    statrow = stA.enter_context(nc.sbuf_tensor("statrowt", [1, 8192], BF16))
    xkvb = stA.enter_context(nc.sbuf_tensor("xkvb", [128, NCH * 1024], BF16))
    ckv = stA.enter_context(nc.sbuf_tensor("ckv", [128, 2 * TOK], F32))
    qraw = stA.enter_context(nc.sbuf_tensor("qraw", [128, 4 * 1024], BF16))
    rtmp = stA.enter_context(nc.sbuf_tensor("rtmp", [128, 1024], F32))
    kfull = stA.enter_context(nc.sbuf_tensor("kfull", [128, KVH * 1024], BF16))
    vfull = stA.enter_context(nc.sbuf_tensor("vfull", [128, KVH * 1024], BF16))
    vtb = stA.enter_context(nc.sbuf_tensor("vtb", [128, 8 * 1024], BF16))
    wbufA = stA.enter_context(nc.sbuf_tensor("wbufA", [128, 2 * 512], BF16))
    sqbA = stA.enter_context(nc.sbuf_tensor("sqbA", [128, 1024], BF16))

    r = Rec(C)
    r.emit("gpsimd", lambda g: g.dma_start(
        out=xkvb[:, :].rearrange("p (c t) -> p c t", t=TOK),
        in_=xkvT.rearrange("(c p) t -> p c t", p=128)), inc="dG", inc_by=16)
    r.emit("gpsimd", lambda g: g.dma_start(out=ckv[:, 0:TOK], in_=coskv_in),
           inc="dG", inc_by=16)
    r.emit("gpsimd", lambda g: g.dma_start(out=ckv[:, TOK:2 * TOK], in_=sinkv_in),
           inc="dG", inc_by=16)
    r.emit("gpsimd", lambda g: g.wait_ge(sems["dG"], C["dG"]))
    r.realize(nc, sems)

    def proj_block(which, og, wT, dst, cos_sin, xb):
        stP = ExitStack()
        kps = [stP.enter_context(nc.psum_tensor(f"ps{which}{og}_{i}", [128, 1024], F32))
               for i in range(4)]
        r = Rec(C)
        guard = {}
        xload = snap("dG")
        prevdone = [snap("sV"), snap("sA")]
        for c in range(NCH):
            w = [xload] + guard.pop(("w", c % 2), [])
            wl = "dG0" if c % 2 == 0 else "dG1"
            r.emit("gpsimd", lambda g, c=c: g.dma_start(
                out=wbufA[:, (c % 2) * 512:(c % 2) * 512 + 512],
                in_=wT[c * 128:(c + 1) * 128, og * 512:(og + 1) * 512]),
                waits=w, inc=wl, inc_by=16)
            wdone = snap(wl)

            def mmgrp(t, c=c):
                ins = None
                for hh in range(4):
                    lt = wbufA[:, (c % 2) * 512 + hh * 128:(c % 2) * 512 + hh * 128 + 128]
                    for half in range(2):
                        ins = t.matmul(
                            kps[hh][:, half * 512:half * 512 + 512], lt,
                            xb[:, c * 1024 + half * 512:c * 1024 + half * 512 + 512],
                            start=(c == 0), stop=(c == NCH - 1))
                return ins
            r.emit("tensor", mmgrp,
                   waits=[wdone] + (prevdone if c == 0 else []), inc="sPs")
            guard[("w", c % 2)] = [snap("sPs")]
        mmdone = snap("sPs")
        if which in ("k", "q"):
            cos = cos_sin[:, 0:TOK]
            sin = cos_sin[:, TOK:2 * TOK]
            for hh in range(4):
                r.emit("scalar", lambda sc, hh=hh: sc.copy(
                    qraw[:, hh * 1024:(hh + 1) * 1024], kps[hh][:, :]),
                    waits=[mmdone] if hh == 0 else (), inc="sA")
            evdone = snap("sA")
            for hh in range(4):
                dcol = (og * 4 + hh) * 1024

                def rotmm(t, hh=hh):
                    ins = None
                    for half in range(2):
                        ins = t.matmul(kps[hh][:, half * 512:half * 512 + 512],
                                       rotmb[:, :],
                                       qraw[:, hh * 1024 + half * 512:
                                            hh * 1024 + half * 512 + 512],
                                       start=True, stop=True)
                    return ins
                r.emit("tensor", rotmm, waits=[evdone] if hh == 0 else (), inc="sPs")
                rotdone = snap("sPs")

                def rope(v, hh=hh, dcol=dcol):
                    v.tensor_mul(rtmp[:, :], qraw[:, hh * 1024:(hh + 1) * 1024], cos)
                    v.tensor_mul(sqbA[:, :], kps[hh][:, :], sin)
                    return v.tensor_add(dst[:, dcol:dcol + 1024], rtmp[:, :], sqbA[:, :])
                r.emit("vector", rope, waits=[rotdone], inc="sV")
        else:
            for hh in range(4):
                dcol = (og * 4 + hh) * 1024
                r.emit("scalar",
                       lambda sc, hh=hh, dcol=dcol: sc.copy(
                           dst[:, dcol:dcol + 1024], kps[hh][:, :]),
                       waits=[mmdone] if hh == 0 else (), inc="sA")
        r.realize(nc, sems)
        stP.close()

    for og in range(2):
        proj_block("k", og, wkT, kfull, ckv, xkvb)
    for og in range(2):
        proj_block("v", og, wvT, vfull, ckv, xkvb)

    stP = ExitStack()
    vtp = [stP.enter_context(nc.psum_tensor(f"vtp{i}", [128, 128], BF16)) for i in range(2)]
    stat_ps = stP.enter_context(nc.psum_tensor("statps", [1, 1024], F32))
    r = Rec(C)
    guard = {}
    for n in range(64):
        h, tt = divmod(n, 8)
        r.emit("tensor", lambda t, h=h, tt=tt, n=n: t.transpose(
            vtp[n % 2][:, :], vfull[:, h * 1024 + tt * 128:h * 1024 + tt * 128 + 128],
            ident[:, :]),
            waits=guard.pop(("vtp", n % 2), []) + ([snap("sA")] if n == 0 else []),
            inc="sPs")
        w = snap("sPs")
        r.emit("vector", lambda v, h=h, tt=tt, n=n: v.tensor_copy(
            vtb[:, tt * 1024 + h * 128:tt * 1024 + h * 128 + 128], vtp[n % 2][:, :]),
            waits=[w], inc="sV")
        guard[("vtp", n % 2)] = [snap("sV")]
    for h in range(KVH):
        r.emit("vector", lambda v, h=h: v.tensor_mul(
            sqbA[:, :], kfull[:, h * 1024:(h + 1) * 1024], kfull[:, h * 1024:(h + 1) * 1024]),
            waits=guard.pop("sqfree", []), inc="sV")
        w = snap("sV")
        last = None
        for half in range(2):
            last = r.emit("tensor", lambda t, half=half: t.matmul(
                stat_ps[0:1, half * 512:half * 512 + 512], onesb[:, 0:1],
                sqbA[:, half * 512:half * 512 + 512], start=True, stop=True),
                waits=[w] + (guard.pop("statps", []) if half == 0 else []), inc="sPs")
        guard["sqfree"] = [snap("sPs")]
        r.emit("scalar", lambda sc, h=h: sc.copy(
            statrow[0:1, h * 1024:(h + 1) * 1024], stat_ps[0:1, :]),
            waits=[snap("sPs")], inc="sA")
        guard["statps"] = [snap("sA")]
    r.realize(nc, sems)
    stP.close()

    r = Rec(C)
    r.emit("sync", lambda s: s.dma_start(
        out=kc.ap()[0:1024, :].rearrange("(h p) t -> p h t", p=128),
        in_=kfull[:, :].rearrange("p (h t) -> p h t", t=1024)),
        waits=[snap("sV"), snap("sA")], inc="dS", inc_by=16)
    r.emit("sync", lambda s: s.dma_start(
        out=kc.ap()[1024:1032, :].rearrange("(o h) t -> o h t", o=1),
        in_=statrow[0:1, :].rearrange("o (h t) -> o h t", t=1024)), inc="dS", inc_by=16)
    r.emit("sync", lambda s: s.dma_start(
        out=vc.ap()[:, :].rearrange("(tt p) o -> p tt o", p=128),
        in_=vtb[:, :].rearrange("p (tt o) -> p tt o", o=1024)), inc="dS", inc_by=16)
    r.emit("sync", lambda s: s.wait_ge(sems["dS"], C["dS"]))
    r.realize(nc, sems)
    stA.close()

    stQA = ExitStack()
    qfull = stQA.enter_context(nc.sbuf_tensor("qfull", [128, H * 1024], BF16))
    stQ = ExitStack()
    xqb = stQ.enter_context(nc.sbuf_tensor("xqb", [128, NCH * 1024], BF16))
    cq = stQ.enter_context(nc.sbuf_tensor("cq", [128, 2 * TOK], F32))
    qrawQ = stQ.enter_context(nc.sbuf_tensor("qrawQ", [128, 4 * 1024], BF16))
    rtmpQ = stQ.enter_context(nc.sbuf_tensor("rtmpQ", [128, 1024], F32))
    sq2 = stQ.enter_context(nc.sbuf_tensor("sq2", [128, 1024], BF16))
    wbufQ = stQ.enter_context(nc.sbuf_tensor("wbufQ", [128, 2 * 512], BF16))
    wstgQ = stQ.enter_context(nc.sbuf_tensor("wstgQ", [128, 2 * 512], F32))

    r = Rec(C)
    r.emit("gpsimd", lambda g: g.dma_start(
        out=xqb[:, :].rearrange("p (c t) -> p c t", t=TOK),
        in_=xqT.rearrange("(c p) t -> p c t", p=128)), inc="dG", inc_by=16)
    r.emit("gpsimd", lambda g: g.dma_start(out=cq[:, 0:TOK], in_=cosq_in),
           inc="dG", inc_by=16)
    r.emit("gpsimd", lambda g: g.dma_start(out=cq[:, TOK:2 * TOK], in_=sinq_in),
           inc="dG", inc_by=16)
    r.realize(nc, sems)

    r = Rec(C)
    r.emit("gpsimd", lambda g: g.wait_ge(sems["dS"], C["dS"]))
    r.emit("gpsimd", lambda g: g.collective_compute(
        "AllGather", ALU.bypass, replica_groups=[list(range(NCORES))],
        ins=[kc.ap().opt()], outs=[kg.ap().opt()]), inc="cc")
    r.emit("gpsimd", lambda g: g.collective_compute(
        "AllGather", ALU.bypass, replica_groups=[list(range(NCORES))],
        ins=[vc.ap().opt()], outs=[vg.ap().opt()]), inc="cc")
    r.emit("gpsimd", lambda g: g.dma_start(out=mkr[0:1, :], in_=onesf[0:1, 0:16]),
           inc="ccd", inc_by=16)
    r.realize(nc, sems)

    qguard = {}

    def q_block(og):
        stP = ExitStack()
        kps = [stP.enter_context(nc.psum_tensor(f"qps{og}_{i}", [128, 1024], F32))
               for i in range(4)]
        r = Rec(C)
        guard = qguard
        xload = snap("dG")
        prevdone = [snap("sV"), snap("sA")]
        for c in range(NCH):
            w = guard.pop(("ws", c % 2), [])
            wl = "dS0" if c % 2 == 0 else "dS1"
            r.emit("sync", lambda s, c=c: s.dma_start(
                out=wstgQ[:, (c % 2) * 512:(c % 2) * 512 + 512],
                in_=wqT[c * 128:(c + 1) * 128, og * 512:(og + 1) * 512]),
                waits=w, inc=wl, inc_by=16)
            r.emit("vector", lambda v, c=c: v.tensor_copy(
                wbufQ[:, (c % 2) * 512:(c % 2) * 512 + 512],
                wstgQ[:, (c % 2) * 512:(c % 2) * 512 + 512]),
                waits=[snap(wl)] + guard.pop(("w", c % 2), []), inc="sV")
            guard[("ws", c % 2)] = [snap("sV")]
            wdone = snap("sV")

            def mmgrp(t, c=c):
                ins = None
                for hh in range(4):
                    lt = wbufQ[:, (c % 2) * 512 + hh * 128:(c % 2) * 512 + hh * 128 + 128]
                    for half in range(2):
                        ins = t.matmul(
                            kps[hh][:, half * 512:half * 512 + 512], lt,
                            xqb[:, c * 1024 + half * 512:c * 1024 + half * 512 + 512],
                            start=(c == 0), stop=(c == NCH - 1))
                return ins
            r.emit("tensor", mmgrp,
                   waits=[wdone] + ([xload] if c == 0 else []) + (prevdone if c == 0 else []),
                   inc="sPs")
            guard[("w", c % 2)] = [snap("sPs")]
        mmdone = snap("sPs")
        cos = cq[:, 0:TOK]
        sin = cq[:, TOK:2 * TOK]
        for hh in range(4):
            r.emit("scalar", lambda sc, hh=hh: sc.copy(
                qrawQ[:, hh * 1024:(hh + 1) * 1024], kps[hh][:, :]),
                waits=[mmdone] if hh == 0 else (), inc="sA")
        evdone = snap("sA")
        for hh in range(4):
            dcol = (og * 4 + hh) * 1024

            def rotmm(t, hh=hh):
                ins = None
                for half in range(2):
                    ins = t.matmul(kps[hh][:, half * 512:half * 512 + 512],
                                   rotmb[:, :],
                                   qrawQ[:, hh * 1024 + half * 512:
                                        hh * 1024 + half * 512 + 512],
                                   start=True, stop=True)
                return ins
            r.emit("tensor", rotmm, waits=[evdone] if hh == 0 else (), inc="sPs")
            rotdone = snap("sPs")

            def rope(v, hh=hh, dcol=dcol):
                v.tensor_mul(rtmpQ[:, :], qrawQ[:, hh * 1024:(hh + 1) * 1024], cos)
                v.tensor_mul(sq2[:, :], kps[hh][:, :], sin)
                return v.tensor_add(qfull[:, dcol:dcol + 1024], rtmpQ[:, :], sq2[:, :])
            r.emit("vector", rope, waits=[rotdone], inc="sV")
        r.realize(nc, sems)
        stP.close()

    for og in range(8):
        q_block(og)

    stP = ExitStack()
    statq = stP.enter_context(nc.psum_tensor("statq", [1, 1024], F32))
    r = Rec(C)
    guard = {}
    for h in range(H):
        r.emit("vector", lambda v, h=h: v.tensor_mul(
            sq2[:, :], qfull[:, h * 1024:(h + 1) * 1024], qfull[:, h * 1024:(h + 1) * 1024]),
            waits=guard.pop("sqfree", []), inc="sV")
        w = snap("sV")
        for half in range(2):
            r.emit("tensor", lambda t, half=half: t.matmul(
                statq[0:1, half * 512:half * 512 + 512], onesb[:, 0:1],
                sq2[:, half * 512:half * 512 + 512], start=True, stop=True),
                waits=[w] + (guard.pop("statps", []) if half == 0 else []), inc="sPs")
        guard["sqfree"] = [snap("sPs")]
        r.emit("scalar", lambda sc: sc.copy(nqrow[0:1, :], statq[0:1, :]),
               waits=[snap("sPs")] + guard.pop("nqfree", []), inc="sA")
        guard["statps"] = [snap("sA")]
        r.emit("vector", lambda v, h=h: v.reduce_max(
            q2row[0:1, h:h + 1], nqrow[0:1, :], axis=AX.X), waits=[snap("sA")], inc="sV")
        guard["nqfree"] = [snap("sV")]
    r.realize(nc, sems)
    stP.close()
    stQ.close()

    stB = ExitStack()
    krows = stB.enter_context(nc.sbuf_tensor("krows", [8, 8 * 1024], BF16))
    k2pad = stB.enter_context(nc.sbuf_tensor("k2pad", [8, 1], F32))
    k2eps = stB.enter_context(nc.psum_tensor("k2eps", [1, 32], F32))
    btp = stB.enter_context(nc.psum_tensor("btp", [128, 32], F32))
    r = Rec(C)
    r.emit("gpsimd", lambda g: g.wait_ge(sems["cc"], 2))
    for rr in range(NCORES):
        r.emit("gpsimd", lambda g, rr=rr: g.dma_start(
            out=krows[0:8, rr * 1024:(rr + 1) * 1024],
            in_=kg.ap()[rr * 1032 + 1024:rr * 1032 + 1032, :]), inc="dG", inc_by=16)
    wd = snap("dG")
    r.emit("vector", lambda v: v.reduce_max(k2pad[0:8, 0:1], krows[0:8, :], axis=AX.X),
           waits=[wd], inc="sV")
    r.emit("tensor", lambda t: t.matmul(k2eps[0:1, :], k2pad[0:8, 0:1],
                                        exp8b[0:8, 0:32], start=True, stop=True),
           waits=[snap("sV")], inc="sPs")
    r.emit("vector", lambda v: v.tensor_mul(srow[0:1, :], q2row[0:1, :],
                                            k2eps[0:1, :]),
           waits=[snap("sPs")], inc="sV")

    r.emit("scalar", lambda sc: sc.activation(
        srow2[0:1, :], srow[0:1, :], AF.Ln, bias=epsA[0:1, 0:1],
        scale=1.0 / 128.0), waits=[snap("sV")], inc="sA")
    r.emit("scalar", lambda sc: sc.activation(
        srow[0:1, :], srow2[0:1, :], AF.Exp, bias=0.0, scale=0.5),
        waits=[snap("sA")], inc="sA")
    r.emit("vector", lambda v: v.tensor_scalar_mul(srow2[0:1, :], srow[0:1, :], -1.0),
           waits=[snap("sA")], inc="sV")
    r.emit("tensor", lambda t: t.matmul(btp[:, :], onesf[0:1, :], srow2[0:1, 0:32],
                                        start=True, stop=True),
           waits=[snap("sV")], inc="sPs")
    r.emit("scalar", lambda sc: sc.copy(btbl[:, :], btp[:, :]),
           waits=[snap("sPs")], inc="sA")
    r.realize(nc, sems)
    stB.close()

    stT = ExitStack()
    kbuf = stT.enter_context(nc.sbuf_tensor("kbuf", [128, 2 * 1024], BF16))
    vbuf = stT.enter_context(nc.sbuf_tensor("vbuf", [128, 2 * 1024], BF16))
    ptb = stT.enter_context(nc.sbuf_tensor("ptb", [128, 2 * 1024], BF16))
    ascr = stT.enter_context(nc.sbuf_tensor("ascr", [128, 1024], F32))
    dacc = stT.enter_context(nc.sbuf_tensor("dacc", [128, 2 * 1024], F32))
    atile = stT.enter_context(nc.sbuf_tensor("atile", [128, 2 * 1024], BF16))
    drow = stT.enter_context(nc.sbuf_tensor("drow", [1, 1024], F32))
    drow2 = stT.enter_context(nc.sbuf_tensor("drow2", [128, 1024], F32))
    sps = [stT.enter_context(nc.psum_tensor(f"sps{i}", [128, 1024], F32)) for i in range(2)]
    outps = stT.enter_context(nc.psum_tensor("outps", [128, 1024], F32))
    denps = stT.enter_context(nc.psum_tensor("denps", [128, 1024], F32))

    r = Rec(C)
    guard = {}
    r.emit("sync", lambda s: s.wait_ge(sems["ccd"], 16))
    eg = 0
    pair = 0
    pending = None

    def emit_pv(pg):
        waits = [pg["wA"]] + ([pg["wV"]] if pg["diag"] else []) + pg["whead"]

        def pv(t, pg=pg):
            ins = None
            for ti, tt in enumerate(pg["tiles"]):
                rhs = ptb[:, (pg["eg"] % 2) * 1024 + ti * pg["slot"]:
                          (pg["eg"] % 2) * 1024 + ti * pg["slot"] + pg["N"]]
                st = pg["first"] and ti == 0
                ins = t.matmul(
                    outps[:, pg["jlo"] * 128:pg["jlo"] * 128 + pg["N"]],
                    vbuf[:, pg["pp"] * 1024 + tt * 128:pg["pp"] * 1024 + tt * 128 + 128],
                    rhs, start=st, stop=False, skip_group_check=True)
            return ins
        r.emit("tensor", pv, waits=waits, inc="sPv")
        guard[("ptb", pg["eg"] % 2)] = [snap("sPv"), pg["dsum"]]
        if pg["tiles"][-1] == 7:
            guard[("kv", pg["pp"])] = [snap("sPv")]

    headguard = [snap("sA")]
    pending_store = None

    def emit_store(ps):
        hs, dfree = ps
        wl = "dG0" if hs % 2 == 0 else "dG1"
        r.emit("gpsimd", lambda g, hs=hs: g.dma_start(
            out=atnd.ap()[hs * 128:(hs + 1) * 128, :],
            in_=atile[:, (hs % 2) * 1024:(hs % 2) * 1024 + 1024]),
            waits=[dfree], inc=wl, inc_by=16)
        guard[("atile", hs % 2)] = [snap(wl)]

    for h in range(H):
        first_pv_of_head = True
        for (g, b) in attn_pairs():
            pp = pair % 2
            pw = guard.pop(("kv", pp), [])
            kv = h // 4
            wl = "dS0" if pp == 0 else "dS1"
            r.emit("sync", lambda s, b=b, kv=kv, pp=pp: s.dma_start(
                out=kbuf[:, pp * 1024:pp * 1024 + 1024],
                in_=kg.ap()[b * 1032 + kv * 128:b * 1032 + (kv + 1) * 128, :]),
                waits=pw, inc=wl, inc_by=16)
            r.emit("sync", lambda s, b=b, kv=kv, pp=pp: s.dma_start(
                out=vbuf[:, pp * 1024:(pp + 1) * 1024]
                    .rearrange("p (t f) -> p t f", f=128),
                in_=vg.ap()[b * 1024:(b + 1) * 1024, kv * 128:(kv + 1) * 128]
                    .rearrange("(t p) f -> p t f", p=128)), inc=wl, inc_by=16)
            kvdone = snap(wl)
            diag = (g == 0) or (b >= 4)
            for gi, gr in enumerate(attn_groups_of_pair(g, b)):
                tiles, slot, span, N, jlo, jhi = (gr["tiles"], gr["slot"], gr["span"],
                                                  gr["N"], gr["jlo"], gr["jhi"])
                sw = ([kvdone] + guard.pop(("sps", eg % 2), [])
                      + ([snap("sA")] if eg == 0 else []))

                def sgrp(t, tiles=tiles, slot=slot, N=N, jlo=jlo, h=h, pp=pp, eg=eg):
                    ins = None
                    for ti, tt in enumerate(tiles):
                        ins = t.matmul(
                            sps[eg % 2][:, ti * slot:ti * slot + N],
                            kbuf[:, pp * 1024 + tt * 128:pp * 1024 + tt * 128 + 128],
                            qfull[:, h * 1024 + jlo * 128:h * 1024 + jlo * 128 + N],
                            start=True, stop=True)
                    return ins
                r.emit("tensor", sgrp, waits=sw, inc="sPs")
                sdone = snap("sPs")
                ew = [sdone] + guard.pop(("ptb", eg % 2), [])
                r.emit("scalar", lambda sc, eg=eg, span=span, h=h: sc.activation(
                    ptb[:, (eg % 2) * 1024:(eg % 2) * 1024 + span],
                    sps[eg % 2][:, 0:span], AF.Exp,
                    bias=btbl[:, h:h + 1], scale=ISQ), waits=ew, inc="sA")
                guard[("sps", eg % 2)] = [snap("sA")]
                wA = snap("sA")
                wV = None
                if diag:
                    def mgrp(v, tiles=tiles, slot=slot, eg=eg):
                        ins = None
                        for ti, tt in enumerate(tiles):
                            sl = ptb[:, (eg % 2) * 1024 + ti * slot:
                                     (eg % 2) * 1024 + ti * slot + 128]
                            ins = v.tensor_mul(sl, sl,
                                               masks[:, tt * 128:(tt + 1) * 128])
                        return ins
                    r.emit("vector", mgrp, waits=[wA], inc="sV")
                    wV = snap("sV")
                    if pending_store is not None:
                        emit_store(pending_store)
                        pending_store = None
                dw = [wA] + ([wV] if diag else []) + guard.pop(("dacc", h % 2), [])

                def dadd(v, tiles=tiles, slot=slot, N=N, jlo=jlo, eg=eg, h=h,
                         first=(b == 0 and gi == 0)):
                    ins = None
                    dst = dacc[:, (h % 2) * 1024 + jlo * 128:
                               (h % 2) * 1024 + jlo * 128 + N]
                    for ti in range(len(tiles)):
                        sl = ptb[:, (eg % 2) * 1024 + ti * slot:
                                 (eg % 2) * 1024 + ti * slot + N]
                        if first and ti == 0:
                            ins = v.tensor_copy(dst, sl)
                        else:
                            ins = v.tensor_add(dst, dst, sl)
                    return ins
                r.emit("vector", dadd, waits=dw, inc="sV")
                dsum = snap("sV")
                if pending is not None:
                    emit_pv(pending)
                pending = dict(tiles=tiles, slot=slot, N=N, jlo=jlo,
                               diag=diag, pp=pp, eg=eg, wA=wA, wV=wV,
                               first=(b == 0 and gi == 0), dsum=dsum,
                               whead=headguard if first_pv_of_head else [])
                first_pv_of_head = False
                eg += 1
            pair += 1
        emit_pv(pending)
        pending = None
        pvall = snap("sPv")
        dnw = [snap("sV")] + guard.pop("denps", [])

        def denmm(t, h=h):
            ins = None
            for half in range(2):
                ins = t.matmul(denps[0:1, half * 512:half * 512 + 512],
                               onesf[:, 0:1],
                               dacc[:, (h % 2) * 1024 + half * 512:
                                    (h % 2) * 1024 + half * 512 + 512],
                               start=True, stop=True, skip_group_check=True)
            return ins
        r.emit("tensor", denmm, waits=dnw, inc="sPs")
        guard[("dacc", h % 2)] = [snap("sPs")]
        dmmdone = snap("sPs")
        r.emit("scalar", lambda sc: sc.copy(ascr[:, :], outps[:, :]),
               waits=[pvall] + guard.pop("ascr", []), inc="sA")
        outfree = snap("sA")
        r.emit("scalar", lambda sc: sc.copy(drow[0:1, :], denps[0:1, :]),
               waits=[dmmdone], inc="sA")
        rw = snap("sA")

        def bcast(t):
            ins = None
            for half in range(2):
                ins = t.matmul(denps[:, half * 512:half * 512 + 512],
                               onesf[0:1, 0:128],
                               drow[0:1, half * 512:half * 512 + 512],
                               start=True, stop=True, skip_group_check=True)
            return ins
        r.emit("tensor", bcast, waits=[rw], inc="sPs")
        r.emit("vector", lambda v: v.reciprocal(drow2[:, :], denps[:, :]),
               waits=[snap("sPs")] + guard.pop("drow2", []), inc="sV")
        rcpw = snap("sV")
        fw = ([rcpw, outfree] + guard.pop(("atile", h % 2), []))
        r.emit("vector", lambda v, h=h: v.tensor_mul(
            atile[:, (h % 2) * 1024:(h % 2) * 1024 + 1024],
            ascr[:, :], drow2[:, :]), waits=fw, inc="sV")
        denfree = snap("sV")
        guard["drow2"] = [denfree]
        guard["ascr"] = [denfree]
        pending_store = (h, denfree)
        guard["denps"] = [denfree]
        headguard = [outfree]
    emit_store(pending_store)
    pending_store = None
    g0, g1 = snap("dG0"), snap("dG1")
    r.emit("gpsimd", lambda g: g.wait_ge(sems["dG0"], g0[1]))
    r.emit("gpsimd", lambda g: g.wait_ge(sems["dG1"], g1[1]))
    r.realize(nc, sems)
    stT.close()
    stQA.close()

    stL1 = ExitStack()
    ln1full = stL1.enter_context(nc.sbuf_tensor("ln1full", [128, 8 * 4096], BF16))
    stO = ExitStack()
    oacc = stO.enter_context(nc.sbuf_tensor("oacc", [128, 4 * 4096], F32))
    atb = stO.enter_context(nc.sbuf_tensor("atb", [128, 2 * 512], BF16))
    wob = stO.enter_context(nc.sbuf_tensor("wob", [128, 2 * 1024], BF16))
    xqq = stO.enter_context(nc.sbuf_tensor("xqq", [128, 2 * 1024], F32))
    sqf = stO.enter_context(nc.sbuf_tensor("sqf", [128, 4096], F32))
    ln1f = stO.enter_context(nc.sbuf_tensor("ln1f", [128, 4096], F32))
    ostat = stO.enter_context(nc.sbuf_tensor("ostat", [128, 28], F32))
    psO = [stO.enter_context(nc.psum_tensor(f"psO{i}", [128, 1024], F32)) for i in range(4)]

    r = Rec(C)
    guard = {}
    n = 0
    for ttg in range(2):
        for hq in range(4):
            for oc in range(NCH):
                wl = "dG0" if n % 2 == 0 else "dG1"
                r.emit("gpsimd", lambda g, hq=hq, oc=oc, n=n: g.dma_start(
                    out=wob[:, (n % 2) * 1024:(n % 2) * 1024 + 1024],
                    in_=woT[oc * 128:(oc + 1) * 128, hq * 1024:(hq + 1) * 1024]),
                    waits=guard.pop(("wob", n % 2), []), inc=wl, inc_by=16)
                wG = snap(wl)
                sl = "dS0" if n % 2 == 0 else "dS1"
                r.emit("sync", lambda s, ttg=ttg, oc=oc, n=n: s.dma_start(
                    out=atb[:, (n % 2) * 512:(n % 2) * 512 + 512],
                    in_=atnd.ap()[oc * 128:(oc + 1) * 128, ttg * 512:(ttg + 1) * 512]),
                    waits=guard.pop(("atb", n % 2), []), inc=sl, inc_by=16)
                wS = snap(sl)
                mmw = [wG, wS] + (guard.pop("psO", []) if oc == 0 else [])

                def mmo(t, oc=oc, n=n):
                    ins = None
                    for tt in range(4):
                        for half in range(2):
                            ins = t.matmul(
                                psO[tt][:, half * 512:half * 512 + 512],
                                atb[:, (n % 2) * 512 + tt * 128:(n % 2) * 512 + tt * 128 + 128],
                                wob[:, (n % 2) * 1024 + half * 512:
                                    (n % 2) * 1024 + half * 512 + 512],
                                start=(oc == 0), stop=(oc == NCH - 1))
                    return ins
                r.emit("tensor", mmo, waits=mmw, inc="sPs")
                guard[("wob", n % 2)] = [snap("sPs")]
                guard[("atb", n % 2)] = [snap("sPs")]
                n += 1
            mmdone = snap("sPs")
            addsnap = None
            for tt in range(4):
                m = (ttg * 4 + hq) * 4 + tt
                sl = "dS2" if m % 2 == 0 else "dS3"
                r.emit("sync", lambda s, ttg=ttg, tt=tt, hq=hq, m=m: s.dma_start(
                    out=xqq[:, (m % 2) * 1024:(m % 2) * 1024 + 1024],
                    in_=xq[(ttg * 4 + tt) * 128:(ttg * 4 + tt + 1) * 128,
                           hq * 1024:(hq + 1) * 1024]),
                    waits=guard.pop(("xqq", m % 2), []), inc=sl, inc_by=16)
                wS = snap(sl)
                r.emit("vector", lambda v, tt=tt, hq=hq, m=m: v.tensor_add(
                    oacc[:, tt * 4096 + hq * 1024:tt * 4096 + hq * 1024 + 1024],
                    psO[tt][:, :], xqq[:, (m % 2) * 1024:(m % 2) * 1024 + 1024]),
                    waits=[mmdone, wS], inc="sV")
                guard[("xqq", m % 2)] = [snap("sV")]
            guard["psO"] = [snap("sV")]
        SR, SF, wln = ln_chain(r, ostat, 4,
                               [oacc[:, tt * 4096:(tt + 1) * 4096]
                                for tt in range(4)], sqf)
        for tt in range(4):
            k = ttg * 4 + tt
            a = oacc[:, tt * 4096:(tt + 1) * 4096]
            r.emit("vector", lambda v, a=a, tt=tt, SR=SR, SF=SF: v.tensor_scalar(
                ln1f[:, :], a, SR[:, tt:tt + 1], SF[:, tt:tt + 1],
                ALU.mult, ALU.add),
                waits=[wln] + guard.pop("ln1f", []), inc="sV")
            nrm = snap("sV")
            r.emit("sync", lambda s, k=k: s.dma_start(
                out=r2d.ap()[k * 128:(k + 1) * 128, :],
                in_=ln1f[:, :]),
                waits=[nrm], inc="dO", inc_by=16)
            r.emit("vector", lambda v, k=k: v.tensor_copy(
                ln1full[:, k * 4096:(k + 1) * 4096],
                ln1f[:, :]), waits=[nrm], inc="sV")
            guard["ln1f"] = [snap("dO"), snap("sV")]
    r.emit("sync", lambda s: s.wait_ge(sems["dO"], C["dO"]))
    r.realize(nc, sems)
    stO.close()

    stL = ExitStack()
    xt2st = stL.enter_context(nc.sbuf_tensor("xt2st", [128, 2 * 4096], BF16))
    ptL = [stL.enter_context(nc.psum_tensor(f"ptL{i}", [128, 128], BF16)) for i in range(2)]
    r = Rec(C)
    guard = {}
    for tt in range(8):
        for c in range(NCH):
            nn = tt * NCH + c
            r.emit("tensor", lambda t, tt=tt, c=c, nn=nn: t.transpose(
                ptL[nn % 2][:, :],
                ln1full[:, tt * 4096 + c * 128:tt * 4096 + c * 128 + 128], ident[:, :]),
                waits=guard.pop(("ptL", nn % 2), []) + ([snap("sV")] if nn == 0 else []),
                inc="sT")
            r.emit("vector", lambda v, tt=tt, c=c, nn=nn: v.tensor_copy(
                xt2st[:, (tt % 2) * 4096 + c * 128:(tt % 2) * 4096 + c * 128 + 128],
                ptL[nn % 2][:, :]),
                waits=[snap("sT")] + (guard.pop(("xt2st", tt % 2), []) if c == 0 else []),
                inc="sV")
            guard[("ptL", nn % 2)] = [snap("sV")]
        sl = "dS2" if tt % 2 == 0 else "dS3"
        r.emit("sync", lambda s, tt=tt: s.dma_start(
            out=xt2d.ap().rearrange("(c p) t -> p c t", p=128)[:, :, tt * 128:(tt + 1) * 128],
            in_=xt2st[:, (tt % 2) * 4096:(tt % 2) * 4096 + 4096]
                .rearrange("p (c f) -> p c f", f=128)),
            waits=[snap("sV")], inc=sl, inc_by=16)
        guard[("xt2st", tt % 2)] = [snap(sl)]
    r.realize(nc, sems)
    stL.close()
    stL1.close()

    stMid = ExitStack()
    midsb = stMid.enter_context(nc.sbuf_tensor("midsb", [128, FT * 512], BF16))
    stF = ExitStack()
    xt2b = stF.enter_context(nc.sbuf_tensor("xt2b", [128, NCH * 1024], BF16))
    gwb = stF.enter_context(nc.sbuf_tensor("gwb", [128, 2 * 4096], BF16))
    uwb = stF.enter_context(nc.sbuf_tensor("uwb", [128, 2 * 4096], BF16))
    gsil = stF.enter_context(nc.sbuf_tensor("gsil", [128, 2 * 1024], F32))
    mid1 = stF.enter_context(nc.sbuf_tensor("mid1", [128, 1024], F32))
    midb2 = stF.enter_context(nc.sbuf_tensor("midb2", [128, 2 * 512], BF16))
    gps = [stF.enter_context(nc.psum_tensor(f"gps{i}", [128, 1024], F32)) for i in range(2)]
    ups = [stF.enter_context(nc.psum_tensor(f"ups{i}", [128, 1024], F32)) for i in range(2)]

    r = Rec(C)
    guard = {}
    r.emit("gpsimd", lambda g: g.dma_start(
        out=xt2b[:, :].rearrange("p (c t) -> p c t", t=TOK),
        in_=xt2d.ap().rearrange("(c p) t -> p c t", p=128)),
        waits=[snap("dS2"), snap("dS3")], inc="dG", inc_by=16)
    xload = snap("dG")
    for ft in range(FT):
        gw = guard.pop(("wf", ft % 2), []) + ([xload] if ft < 2 else [])
        wl = "dG0" if ft % 2 == 0 else "dG1"
        r.emit("gpsimd", lambda g, ft=ft: g.dma_start(
            out=gwb[:, (ft % 2) * 4096:(ft % 2) * 4096 + 4096]
                .rearrange("p (c f) -> p c f", f=128),
            in_=gwT.rearrange("(c p) f -> p c f", p=128)[:, :, ft * 128:(ft + 1) * 128]),
            waits=gw, inc=wl, inc_by=16)
        r.emit("gpsimd", lambda g, ft=ft: g.dma_start(
            out=uwb[:, (ft % 2) * 4096:(ft % 2) * 4096 + 4096]
                .rearrange("p (c f) -> p c f", f=128),
            in_=uwT.rearrange("(c p) f -> p c f", p=128)[:, :, ft * 128:(ft + 1) * 128]),
            inc=wl, inc_by=16)
        wG = snap(wl)
        mmw = [wG] + guard.pop(("gps", ft % 2), []) + guard.pop(("ups", ft % 2), [])

        def mmf(t, ft=ft):
            ins = None
            for c in range(NCH):
                for (ps, wb) in ((gps, gwb), (ups, uwb)):
                    lt = wb[:, (ft % 2) * 4096 + c * 128:(ft % 2) * 4096 + c * 128 + 128]
                    for half in range(2):
                        ins = t.matmul(
                            ps[ft % 2][:, half * 512:half * 512 + 512], lt,
                            xt2b[:, c * 1024 + half * 512:c * 1024 + half * 512 + 512],
                            start=(c == 0), stop=(c == NCH - 1))
            return ins
        r.emit("tensor", mmf, waits=mmw, inc="sPs")
        guard[("wf", ft % 2)] = [snap("sPs")]
        mmdone = snap("sPs")
        r.emit("scalar", lambda sc, ft=ft: sc.activation(
            gsil[:, (ft % 2) * 1024:(ft % 2) * 1024 + 1024], gps[ft % 2][:, :], AF.Silu,
            bias=0.0, scale=gsb[:, ft:ft + 1]),
            waits=[mmdone] + guard.pop(("gsil", ft % 2), []), inc="sA")
        guard[("gps", ft % 2)] = [snap("sA")]
        wA = snap("sA")

        def midop(v, ft=ft):
            v.tensor_mul(mid1[:, :], ups[ft % 2][:, :],
                         gsil[:, (ft % 2) * 1024:(ft % 2) * 1024 + 1024])
            v.tensor_scalar(midsb[:, ft * 512:(ft + 1) * 512],
                            mid1[:, 0:512], usb[:, ft:ft + 1], None, ALU.mult)
            return v.tensor_scalar(midb2[:, (ft % 2) * 512:(ft % 2) * 512 + 512],
                                   mid1[:, 512:1024], usb[:, ft:ft + 1], None, ALU.mult)
        r.emit("vector", midop, waits=[wA] + guard.pop(("midb", ft % 2), []), inc="sV")
        guard[("ups", ft % 2)] = [snap("sV")]
        guard[("gsil", ft % 2)] = [snap("sV")]
        sl = "dS0" if ft % 2 == 0 else "dS1"
        r.emit("sync", lambda s, ft=ft: s.dma_start(
            out=midd.ap()[ft * 128:(ft + 1) * 128, :],
            in_=midb2[:, (ft % 2) * 512:(ft % 2) * 512 + 512]),
            waits=[snap("sV")], inc=sl, inc_by=16)
        guard[("midb", ft % 2)] = [snap(sl)]
    r.emit("sync", lambda s: s.wait_ge(sems["dS0"], C["dS0"]))
    r.emit("sync", lambda s: s.wait_ge(sems["dS1"], C["dS1"]))
    r.realize(nc, sems)
    stF.close()

    dstage = stMid.enter_context(nc.sbuf_tensor("dstage", [128, NCH * 1024], BF16))
    dwb = stMid.enter_context(nc.sbuf_tensor("dwb", [128, 2 * 512], BF16))
    midb = stMid.enter_context(nc.sbuf_tensor("midb", [128, 2 * 512], BF16))
    stDp = ExitStack()
    dps = [stDp.enter_context(nc.psum_tensor(f"dps{i}", [128, 1024], F32)) for i in range(4)]

    r = Rec(C)
    guard = {}
    msv = snap("sV")
    for dg in range(8):
        for fc in range(FT):
            nn = dg * FT + fc
            wl = "dG0" if nn % 2 == 0 else "dG1"
            r.emit("gpsimd", lambda g, dg=dg, fc=fc, nn=nn: g.dma_start(
                out=dwb[:, (nn % 2) * 512:(nn % 2) * 512 + 512],
                in_=dwT[fc * 128:(fc + 1) * 128, dg * 512:(dg + 1) * 512]),
                waits=guard.pop(("dwb", nn % 2), []), inc=wl, inc_by=16)
            wG = snap(wl)
            sl = "dS2" if nn % 2 == 0 else "dS3"
            r.emit("sync", lambda s, fc=fc, nn=nn: s.dma_start(
                out=midb[:, (nn % 2) * 512:(nn % 2) * 512 + 512],
                in_=midd.ap()[fc * 128:(fc + 1) * 128, :]),
                waits=guard.pop(("midb", nn % 2), []), inc=sl, inc_by=16)
            wS = snap(sl)
            mmw = ([wG, wS] + (guard.pop("dps", []) if fc == 0 else [])
                   + ([msv] if nn == 0 else []))

            def mmd(t, fc=fc, nn=nn):
                ins = None
                for dt in range(4):
                    lt = dwb[:, (nn % 2) * 512 + dt * 128:(nn % 2) * 512 + dt * 128 + 128]
                    ins = t.matmul(
                        dps[dt][:, 0:512], lt, midsb[:, fc * 512:(fc + 1) * 512],
                        start=(fc == 0), stop=(fc == FT - 1))
                    ins = t.matmul(
                        dps[dt][:, 512:1024], lt,
                        midb[:, (nn % 2) * 512:(nn % 2) * 512 + 512],
                        start=(fc == 0), stop=(fc == FT - 1))
                return ins
            r.emit("tensor", mmd, waits=mmw, inc="sPs")
            guard[("dwb", nn % 2)] = [snap("sPs")]
            guard[("midb", nn % 2)] = [snap("sPs")]
        mmdone = snap("sPs")
        last = None
        for dt in range(4):
            last = r.emit("scalar", lambda sc, dg=dg, dt=dt: sc.activation(
                dstage[:, (dg * 4 + dt) * 1024:(dg * 4 + dt) * 1024 + 1024],
                dps[dt][:, :], AF.Copy, bias=0.0, scale=dsb[:, dg * 4 + dt:dg * 4 + dt + 1]),
                waits=[mmdone] if dt == 0 else (), inc="sA")
        guard["dps"] = [snap("sA")]
    r.realize(nc, sems)
    stDp.close()

    stZ = ExitStack()
    ftok = stZ.enter_context(nc.sbuf_tensor("ftok", [128, 4096], F32))
    r2b = stZ.enter_context(nc.sbuf_tensor("r2b", [128, 4096], F32))
    zstat = stZ.enter_context(nc.sbuf_tensor("zstat", [128, 8], F32))
    ptZ = [stZ.enter_context(nc.psum_tensor(f"ptZ{i}", [128, 128], BF16)) for i in range(2)]

    r = Rec(C)
    guard = {}
    for tt in range(8):
        sl = "dS0" if tt % 2 == 0 else "dS1"
        r.emit("sync", lambda s, tt=tt: s.dma_start(
            out=r2b[:, :], in_=r2d.ap()[tt * 128:(tt + 1) * 128, :]),
            waits=guard.pop("r2b", []), inc=sl, inc_by=16)
        wS = snap(sl)
        for c in range(NCH):
            nn = tt * NCH + c
            r.emit("tensor", lambda t, tt=tt, c=c, nn=nn: t.transpose(
                ptZ[nn % 2][:, :],
                dstage[:, c * 1024 + tt * 128:c * 1024 + tt * 128 + 128], ident[:, :]),
                waits=guard.pop(("ptZ", nn % 2), []) + ([snap("sA")] if nn == 0 else []),
                inc="sT")
            r.emit("scalar", lambda sc, tt=tt, c=c, nn=nn: sc.copy(
                ftok[:, c * 128:c * 128 + 128],
                ptZ[nn % 2][:, :]),
                waits=[snap("sT")] + (guard.pop("ftok", []) if c == 0 else []),
                inc="sA")
            guard[("ptZ", nn % 2)] = [snap("sA")]
        evdone = snap("sA")
        fa = ftok[:, :]
        r.emit("vector", lambda v, fa=fa: v.tensor_add(fa, fa, r2b[:, :]),
               waits=[wS, evdone], inc="sV")
        SR, SF, wln = ln_chain(r, zstat, 1, [fa], r2b,
                               first_waits=[snap("sV")])
        guard["r2b"] = [wln]
        r.emit("vector", lambda v, fa=fa, SR=SR, SF=SF: v.tensor_scalar(
            fa, fa, SR[:, 0:1], SF[:, 0:1], ALU.mult, ALU.add),
            waits=[wln], inc="sV")
        so = "dS2" if tt % 2 == 0 else "dS3"
        r.emit("sync", lambda s, tt=tt: s.dma_start(
            out=out[tt * 128:(tt + 1) * 128, :],
            in_=ftok[:, :]),
            waits=[snap("sV")], inc=so, inc_by=16)
        guard["ftok"] = [snap(so)]
    r.emit("sync", lambda s: s.wait_ge(sems["dS2"], C["dS2"]))
    r.emit("sync", lambda s: s.wait_ge(sems["dS3"], C["dS3"]))
    r.realize(nc, sems)
    stZ.close()
    stMid.close()
    glob.close()
    return nc



_ROPE_THETA = 10000.0


def _rope_tables(pos):
    freqs = np.exp(-np.arange(0, 128, 2, dtype=np.float32)
                   * np.float32(math.log(_ROPE_THETA) / 128.0))
    ang = pos.astype(np.float32)[None, :] * freqs[:, None]
    cos = np.cos(ang).astype(np.float32)
    sin = np.sin(ang).astype(np.float32)
    return (np.ascontiguousarray(np.vstack([cos, cos])),
            np.ascontiguousarray(np.vstack([sin, sin])))


def _rot_matrix():
    m = np.zeros((128, 128), dtype=BF16NP)
    for d in range(64):
        m[64 + d, d] = -1
        m[d, 64 + d] = 1
    return m


def _stage_inputs(inputs):
    x = np.asarray(inputs["L0_attn_hidden_states"], np.float32)[0]
    Wq = np.asarray(inputs["L0_attn_Wq"], np.float32)
    Wk = np.asarray(inputs["L0_attn_Wk"], np.float32)
    Wv = np.asarray(inputs["L0_attn_Wv"], np.float32)
    Wo = np.asarray(inputs["L0_attn_Wo"], np.float32)
    gw = np.asarray(inputs["L0_ffn_gate_w"], np.int32)
    uw = np.asarray(inputs["L0_ffn_up_w"], np.int32)
    dw = np.asarray(inputs["L0_ffn_down_w"], np.int32)
    gs = np.asarray(inputs["L0_ffn_gate_s"], np.float32)
    us = np.asarray(inputs["L0_ffn_up_s"], np.float32)
    ds = np.asarray(inputs["L0_ffn_down_s"], np.float32)

    shared = {
        "wqT": np.ascontiguousarray(Wq.T),
        "wkT": np.ascontiguousarray(Wk.T),
        "wvT": np.ascontiguousarray(Wv.T),
        "woT": np.ascontiguousarray(Wo.T),
        "gwT": np.ascontiguousarray(gw.T),
        "uwT": np.ascontiguousarray(uw.T),
        "dwT": np.ascontiguousarray(dw.T),
        "gs": np.ascontiguousarray(gs.reshape(FT, 128).T),
        "us": np.ascontiguousarray(us.reshape(FT, 128).T),
        "ds": np.ascontiguousarray(ds.reshape(NCH, 128).T),
        "ident": np.eye(128, dtype=BF16NP),
        "identf": np.eye(128, dtype=np.float32),
        "onesi": np.ones((128, 128), np.float32),
        "rotm": _rot_matrix(),
        "exp8": np.ascontiguousarray(
            (np.arange(32)[None, :] // 4 == np.arange(8)[:, None]).astype(np.float32)),
    }
    xr = x.reshape(64, 128, D)
    in_maps = []
    for i in range(NCORES):
        xqi = np.ascontiguousarray(xr[i::8].reshape(TOK, D))
        xkvi = x[i * TOK:(i + 1) * TOK]
        pos_q = (np.arange(8)[:, None] * 1024 + i * 128 + np.arange(128)[None, :]).reshape(-1)
        pos_kv = np.arange(i * TOK, (i + 1) * TOK)
        cosq, sinq = _rope_tables(pos_q)
        coskv, sinkv = _rope_tables(pos_kv)
        m = np.zeros((128, 1024), dtype=BF16NP)
        for t in range(8):
            if t < i:
                m[:, t * 128:(t + 1) * 128] = 1
            elif t == i:
                m[:, t * 128:(t + 1) * 128] = np.triu(np.ones((128, 128))).astype(BF16NP)
        d = dict(shared)
        d.update({
            "xq": xqi,
            "xqT": np.ascontiguousarray(xqi.T),
            "xkvT": np.ascontiguousarray(xkvi.T),
            "cosq": cosq, "sinq": sinq, "coskv": coskv, "sinkv": sinkv,
            "masks": m,
        })
        in_maps.append(d)
    return in_maps


_NC_CACHE = None


def kernel(**inputs):
    global _NC_CACHE
    in_maps = _stage_inputs(inputs)
    if _NC_CACHE is None:
        _NC_CACHE = build_program()
    res = run_bass_kernel_spmd(_NC_CACHE, in_maps, core_ids=list(range(NCORES)))
    arr = np.stack([res.results[i]["out"] for i in range(NCORES)])
    arr = arr.reshape(NCORES, 8, 128, D).transpose(1, 0, 2, 3).reshape(1, S, D)
    return np.ascontiguousarray(arr)


if __name__ == "__main__":
    import time
    t0 = time.time()
    nc = build_program()
    print(f"build ok in {time.time() - t0:.1f}s; instructions: {nc.n_instructions()}")


# version 11
# speedup vs baseline: 1.0732x; 1.0141x over previous
import math
from contextlib import ExitStack

import numpy as np
import ml_dtypes
import concourse.bass as bass
import concourse.mybir as mybir
from concourse.bass_utils import run_bass_kernel_spmd

F32 = mybir.dt.float32
BF16 = mybir.dt.bfloat16
I32 = mybir.dt.int32
AF = mybir.ActivationFunctionType
ALU = mybir.AluOpType
AX = mybir.AxisListType

D = 4096
S = 8192
NCORES = 8
H = 32
KVH = 8
DFF = 11008
FT = DFF // 128
TOK = 1024
NCH = D // 128
ISQ = 1.0 / math.sqrt(128.0)
BF16NP = ml_dtypes.bfloat16
ENGS = ["sync", "gpsimd", "tensor", "scalar", "vector"]



def attn_pairs():
    return [(0, b) for b in range(4)] + [(1, b) for b in range(8)]


def attn_groups_of_pair(g, b):
    jlo = b if g == 0 else max(b, 4)
    jhi = 4 * g + 3
    N = (jhi - jlo + 1) * 128
    tpg = {512: 2, 384: 2, 256: 4, 128: 8}[N]
    slot = 512 if N >= 384 else N
    return [dict(tiles=list(range(s0, s0 + tpg)), slot=slot,
                 span=(tpg - 1) * slot + N, N=N, jlo=jlo, jhi=jhi)
            for s0 in range(0, 8, tpg)]


class Rec:

    def __init__(self, C):
        self.ops = {e: [] for e in ENGS}
        self.C = C

    def emit(self, eng, fn, waits=(), inc=None, inc_by=1):
        ws = [(s, v) for (s, v) in waits if v > 0]
        self.ops[eng].append((fn, ws, inc, inc_by))
        if inc is not None:
            self.C[inc] += inc_by
        return None if inc is None else (inc, self.C[inc])

    def realize(self, nc, sems):
        with nc.Block() as block:
            for ename in ENGS:
                if not self.ops[ename]:
                    continue

                def mk(ename):
                    def body(eng):
                        for fn, ws, inc, inc_by in self.ops[ename]:
                            for (s, v) in ws:
                                eng.wait_ge(sems[s], v)
                            ins = fn(eng)
                            if inc is not None:
                                ins.then_inc(sems[inc], inc_by)
                    return body
                getattr(block, ename)(mk(ename))


def build_program():
    nc = bass.Bass("TRN2", target_bir_lowering=False)

    xqT = nc.dram_tensor("xqT", [D, TOK], F32, kind="ExternalInput").ap()
    xq = nc.dram_tensor("xq", [TOK, D], F32, kind="ExternalInput").ap()
    xkvT = nc.dram_tensor("xkvT", [D, TOK], F32, kind="ExternalInput").ap()
    wqT = nc.dram_tensor("wqT", [D, D], F32, kind="ExternalInput").ap()
    wkT = nc.dram_tensor("wkT", [D, KVH * 128], F32, kind="ExternalInput").ap()
    wvT = nc.dram_tensor("wvT", [D, KVH * 128], F32, kind="ExternalInput").ap()
    woT = nc.dram_tensor("woT", [D, D], F32, kind="ExternalInput").ap()
    gwT = nc.dram_tensor("gwT", [D, DFF], I32, kind="ExternalInput").ap()
    uwT = nc.dram_tensor("uwT", [D, DFF], I32, kind="ExternalInput").ap()
    dwT = nc.dram_tensor("dwT", [DFF, D], I32, kind="ExternalInput").ap()
    gs_in = nc.dram_tensor("gs", [128, FT], F32, kind="ExternalInput").ap()
    us_in = nc.dram_tensor("us", [128, FT], F32, kind="ExternalInput").ap()
    ds_in = nc.dram_tensor("ds", [128, NCH], F32, kind="ExternalInput").ap()
    cosq_in = nc.dram_tensor("cosq", [128, TOK], F32, kind="ExternalInput").ap()
    sinq_in = nc.dram_tensor("sinq", [128, TOK], F32, kind="ExternalInput").ap()
    coskv_in = nc.dram_tensor("coskv", [128, TOK], F32, kind="ExternalInput").ap()
    sinkv_in = nc.dram_tensor("sinkv", [128, TOK], F32, kind="ExternalInput").ap()
    rotm_in = nc.dram_tensor("rotm", [128, 128], BF16, kind="ExternalInput").ap()
    masks_in = nc.dram_tensor("masks", [128, 1024], BF16, kind="ExternalInput").ap()
    exp8_in = nc.dram_tensor("exp8", [8, 32], F32, kind="ExternalInput").ap()
    ident_in = nc.dram_tensor("ident", [128, 128], BF16, kind="ExternalInput").ap()
    identf_in = nc.dram_tensor("identf", [128, 128], F32, kind="ExternalInput").ap()
    ones_in = nc.dram_tensor("onesi", [128, 128], F32, kind="ExternalInput").ap()
    out = nc.dram_tensor("out", [TOK, D], F32, kind="ExternalOutput").ap()

    kc = nc.dram_tensor("kc", [1032, 1024], BF16)
    vc = nc.dram_tensor("vc", [1024, 1024], BF16)
    kg = nc.dram_tensor("kg", [NCORES * 1032, 1024], BF16)
    vg = nc.dram_tensor("vg", [NCORES * 1024, 1024], BF16)
    atnd = nc.dram_tensor("atnd", [D, TOK], BF16)
    r2d = nc.dram_tensor("r2d", [TOK, D], F32)
    xt2d = nc.dram_tensor("xt2d", [D, TOK], BF16)
    midd = nc.dram_tensor("midd", [DFF, 512], BF16)

    glob = ExitStack()

    def sb(name, shape, dtype):
        return glob.enter_context(nc.sbuf_tensor(name, shape, dtype))

    ident = sb("identt", [128, 128], BF16)
    rotmb = sb("rotmbt", [128, 128], BF16)
    identf = sb("identft", [128, 128], F32)
    onesf = sb("onesft", [128, 128], F32)
    onesb = sb("onesbt", [128, 1], BF16)
    masks = sb("maskst", [128, 1024], BF16)
    gsb = sb("gsbt", [128, FT], F32)
    usb = sb("usbt", [128, FT], F32)
    dsb = sb("dsbt", [128, NCH], F32)
    btbl = sb("btblt", [128, 32], F32)
    exp8b = sb("exp8bt", [8, 32], F32)
    nqrow = sb("nqrowt", [1, 1024], F32)
    q2row = sb("q2rowt", [1, 32], F32)
    k2row = sb("k2rowt", [1, 128], F32)
    srow = sb("srowt", [1, 32], F32)
    srow2 = sb("srow2t", [1, 32], F32)
    epsA = sb("epsAt", [128, 1], F32)

    mkr = sb("mkrt", [1, 16], F32)

    sems = {}
    for name in ["dG", "dG0", "dG1", "dS", "dS0", "dS1", "dS2", "dS3", "dO",
                 "cc", "ccd", "sPs", "sPv", "sA", "sV", "sF1", "sT", "sF2"]:
        sems[name] = glob.enter_context(nc.semaphore(name))
    C = {k: 0 for k in sems}

    def snap(sem):
        return (sem, C[sem])

    def ln_chain(r, stat, B, a_list, sqscr, first_waits=()):
        SA = stat[:, 0 * B:1 * B]
        SB = stat[:, 1 * B:2 * B]
        SC = stat[:, 2 * B:3 * B]
        SD = stat[:, 3 * B:4 * B]
        SE = stat[:, 4 * B:5 * B]
        SR = stat[:, 5 * B:6 * B]
        SF = stat[:, 6 * B:7 * B]

        def v1(v):
            ins = None
            for t, a in enumerate(a_list):
                v.reduce_sum(SA[:, t:t + 1], a, axis=AX.X)
                ins = v.scalar_tensor_tensor(
                    sqscr[:, :], a, 1.0, a, ALU.mult, ALU.mult,
                    accum_out=SB[:, t:t + 1])
            return ins
        r.emit("vector", v1, waits=list(first_waits), inc="sV")
        r.emit("scalar", lambda sc: sc.activation(
            SC, SA, AF.Square, bias=0.0, scale=1.0 / 4096.0),
            waits=[snap("sV")], inc="sA")
        r.emit("vector", lambda v: v.scalar_tensor_tensor(
            SD, SB, 1.0 / 4096.0, SC, ALU.mult, ALU.subtract),
            waits=[snap("sA")], inc="sV")
        r.emit("scalar", lambda sc: sc.activation(
            SE, SD, AF.Ln, bias=epsA[:, 0:1], scale=1.0),
            waits=[snap("sV")], inc="sA")
        r.emit("scalar", lambda sc: sc.activation(
            SR, SE, AF.Exp, bias=0.0, scale=-0.5),
            waits=[snap("sA")], inc="sA")
        r.emit("vector", lambda v: v.scalar_tensor_tensor(
            SF, SA, -1.0 / 4096.0, SR, ALU.mult, ALU.mult),
            waits=[snap("sA")], inc="sV")
        return SR, SF, snap("sV")

    r = Rec(C)
    for dst, src in [(ident, ident_in), (identf, identf_in), (onesf, ones_in),
                     (masks, masks_in), (gsb, gs_in), (usb, us_in), (dsb, ds_in),
                     (rotmb, rotm_in), (exp8b, exp8_in)]:
        r.emit("gpsimd", lambda g, d=dst, s=src: g.dma_start(out=d[:, :], in_=s),
               inc="dG", inc_by=16)
    r.emit("gpsimd", lambda g: g.wait_ge(sems["dG"], C["dG"]))
    r.emit("vector", lambda v: v.memset(onesb[:, :], 1.0))
    r.emit("vector", lambda v: v.memset(epsA[:, :], 1e-5))
    r.realize(nc, sems)

    stA = ExitStack()
    statrow = stA.enter_context(nc.sbuf_tensor("statrowt", [1, 8192], BF16))
    xkvb = stA.enter_context(nc.sbuf_tensor("xkvb", [128, NCH * 1024], BF16))
    ckv = stA.enter_context(nc.sbuf_tensor("ckv", [128, 2 * TOK], F32))
    qraw = stA.enter_context(nc.sbuf_tensor("qraw", [128, 4 * 1024], BF16))
    rtmp = stA.enter_context(nc.sbuf_tensor("rtmp", [128, 1024], F32))
    kfull = stA.enter_context(nc.sbuf_tensor("kfull", [128, KVH * 1024], BF16))
    vfull = stA.enter_context(nc.sbuf_tensor("vfull", [128, KVH * 1024], BF16))
    vtb = stA.enter_context(nc.sbuf_tensor("vtb", [128, 8 * 1024], BF16))
    wbufA = stA.enter_context(nc.sbuf_tensor("wbufA", [128, 2 * 512], BF16))
    sqbA = stA.enter_context(nc.sbuf_tensor("sqbA", [128, 1024], BF16))

    r = Rec(C)
    r.emit("gpsimd", lambda g: g.dma_start(
        out=xkvb[:, :].rearrange("p (c t) -> p c t", t=TOK),
        in_=xkvT.rearrange("(c p) t -> p c t", p=128)), inc="dG", inc_by=16)
    r.emit("gpsimd", lambda g: g.dma_start(out=ckv[:, 0:TOK], in_=coskv_in),
           inc="dG", inc_by=16)
    r.emit("gpsimd", lambda g: g.dma_start(out=ckv[:, TOK:2 * TOK], in_=sinkv_in),
           inc="dG", inc_by=16)
    r.emit("gpsimd", lambda g: g.wait_ge(sems["dG"], C["dG"]))
    r.realize(nc, sems)

    def proj_block(which, og, wT, dst, cos_sin, xb):
        stP = ExitStack()
        kps = [stP.enter_context(nc.psum_tensor(f"ps{which}{og}_{i}", [128, 1024], F32))
               for i in range(4)]
        r = Rec(C)
        guard = {}
        xload = snap("dG")
        prevdone = [snap("sV"), snap("sA")]
        for c in range(NCH):
            w = [xload] + guard.pop(("w", c % 2), [])
            wl = "dG0" if c % 2 == 0 else "dG1"
            r.emit("gpsimd", lambda g, c=c: g.dma_start(
                out=wbufA[:, (c % 2) * 512:(c % 2) * 512 + 512],
                in_=wT[c * 128:(c + 1) * 128, og * 512:(og + 1) * 512]),
                waits=w, inc=wl, inc_by=16)
            wdone = snap(wl)

            def mmgrp(t, c=c):
                ins = None
                for hh in range(4):
                    lt = wbufA[:, (c % 2) * 512 + hh * 128:(c % 2) * 512 + hh * 128 + 128]
                    for half in range(2):
                        ins = t.matmul(
                            kps[hh][:, half * 512:half * 512 + 512], lt,
                            xb[:, c * 1024 + half * 512:c * 1024 + half * 512 + 512],
                            start=(c == 0), stop=(c == NCH - 1))
                return ins
            r.emit("tensor", mmgrp,
                   waits=[wdone] + (prevdone if c == 0 else []), inc="sPs")
            guard[("w", c % 2)] = [snap("sPs")]
        mmdone = snap("sPs")
        if which in ("k", "q"):
            cos = cos_sin[:, 0:TOK]
            sin = cos_sin[:, TOK:2 * TOK]
            for hh in range(4):
                r.emit("scalar", lambda sc, hh=hh: sc.copy(
                    qraw[:, hh * 1024:(hh + 1) * 1024], kps[hh][:, :]),
                    waits=[mmdone] if hh == 0 else (), inc="sA")
            evdone = snap("sA")
            for hh in range(4):
                dcol = (og * 4 + hh) * 1024

                def rotmm(t, hh=hh):
                    ins = None
                    for half in range(2):
                        ins = t.matmul(kps[hh][:, half * 512:half * 512 + 512],
                                       rotmb[:, :],
                                       qraw[:, hh * 1024 + half * 512:
                                            hh * 1024 + half * 512 + 512],
                                       start=True, stop=True)
                    return ins
                r.emit("tensor", rotmm, waits=[evdone] if hh == 0 else (), inc="sPs")
                rotdone = snap("sPs")

                def rope(v, hh=hh, dcol=dcol):
                    v.tensor_mul(rtmp[:, :], qraw[:, hh * 1024:(hh + 1) * 1024], cos)
                    v.tensor_mul(sqbA[:, :], kps[hh][:, :], sin)
                    return v.tensor_add(dst[:, dcol:dcol + 1024], rtmp[:, :], sqbA[:, :])
                r.emit("vector", rope, waits=[rotdone], inc="sV")
        else:
            for hh in range(4):
                dcol = (og * 4 + hh) * 1024
                r.emit("scalar",
                       lambda sc, hh=hh, dcol=dcol: sc.copy(
                           dst[:, dcol:dcol + 1024], kps[hh][:, :]),
                       waits=[mmdone] if hh == 0 else (), inc="sA")
        r.realize(nc, sems)
        stP.close()

    for og in range(2):
        proj_block("k", og, wkT, kfull, ckv, xkvb)
    for og in range(2):
        proj_block("v", og, wvT, vfull, ckv, xkvb)

    stP = ExitStack()
    vtp = [stP.enter_context(nc.psum_tensor(f"vtp{i}", [128, 128], BF16)) for i in range(2)]
    stat_ps = stP.enter_context(nc.psum_tensor("statps", [1, 1024], F32))
    r = Rec(C)
    guard = {}
    for n in range(64):
        h, tt = divmod(n, 8)
        r.emit("tensor", lambda t, h=h, tt=tt, n=n: t.transpose(
            vtp[n % 2][:, :], vfull[:, h * 1024 + tt * 128:h * 1024 + tt * 128 + 128],
            ident[:, :]),
            waits=guard.pop(("vtp", n % 2), []) + ([snap("sA")] if n == 0 else []),
            inc="sPs")
        w = snap("sPs")
        r.emit("vector", lambda v, h=h, tt=tt, n=n: v.tensor_copy(
            vtb[:, tt * 1024 + h * 128:tt * 1024 + h * 128 + 128], vtp[n % 2][:, :]),
            waits=[w], inc="sV")
        guard[("vtp", n % 2)] = [snap("sV")]
    for h in range(KVH):
        r.emit("vector", lambda v, h=h: v.tensor_mul(
            sqbA[:, :], kfull[:, h * 1024:(h + 1) * 1024], kfull[:, h * 1024:(h + 1) * 1024]),
            waits=guard.pop("sqfree", []), inc="sV")
        w = snap("sV")
        last = None
        for half in range(2):
            last = r.emit("tensor", lambda t, half=half: t.matmul(
                stat_ps[0:1, half * 512:half * 512 + 512], onesb[:, 0:1],
                sqbA[:, half * 512:half * 512 + 512], start=True, stop=True),
                waits=[w] + (guard.pop("statps", []) if half == 0 else []), inc="sPs")
        guard["sqfree"] = [snap("sPs")]
        r.emit("scalar", lambda sc, h=h: sc.copy(
            statrow[0:1, h * 1024:(h + 1) * 1024], stat_ps[0:1, :]),
            waits=[snap("sPs")], inc="sA")
        guard["statps"] = [snap("sA")]
    r.realize(nc, sems)
    stP.close()

    r = Rec(C)
    r.emit("sync", lambda s: s.dma_start(
        out=kc.ap()[0:1024, :].rearrange("(h p) t -> p h t", p=128),
        in_=kfull[:, :].rearrange("p (h t) -> p h t", t=1024)),
        waits=[snap("sV"), snap("sA")], inc="dS", inc_by=16)
    r.emit("sync", lambda s: s.dma_start(
        out=kc.ap()[1024:1032, :].rearrange("(o h) t -> o h t", o=1),
        in_=statrow[0:1, :].rearrange("o (h t) -> o h t", t=1024)), inc="dS", inc_by=16)
    r.emit("sync", lambda s: s.dma_start(
        out=vc.ap()[:, :].rearrange("(tt p) o -> p tt o", p=128),
        in_=vtb[:, :].rearrange("p (tt o) -> p tt o", o=1024)), inc="dS", inc_by=16)
    r.emit("sync", lambda s: s.wait_ge(sems["dS"], C["dS"]))
    r.realize(nc, sems)
    stA.close()

    stQA = ExitStack()
    qfull = stQA.enter_context(nc.sbuf_tensor("qfull", [128, H * 1024], BF16))
    stQ = ExitStack()
    xqb = stQ.enter_context(nc.sbuf_tensor("xqb", [128, NCH * 1024], BF16))
    cq = stQ.enter_context(nc.sbuf_tensor("cq", [128, 2 * TOK], F32))
    qrawQ = stQ.enter_context(nc.sbuf_tensor("qrawQ", [128, 4 * 1024], BF16))
    rtmpQ = stQ.enter_context(nc.sbuf_tensor("rtmpQ", [128, 1024], F32))
    sq2 = stQ.enter_context(nc.sbuf_tensor("sq2", [128, 1024], BF16))
    wbufQ = stQ.enter_context(nc.sbuf_tensor("wbufQ", [128, 2 * 512], BF16))
    wstgQ = stQ.enter_context(nc.sbuf_tensor("wstgQ", [128, 2 * 512], F32))

    r = Rec(C)
    r.emit("gpsimd", lambda g: g.dma_start(
        out=xqb[:, :].rearrange("p (c t) -> p c t", t=TOK),
        in_=xqT.rearrange("(c p) t -> p c t", p=128)), inc="dG", inc_by=16)
    r.emit("gpsimd", lambda g: g.dma_start(out=cq[:, 0:TOK], in_=cosq_in),
           inc="dG", inc_by=16)
    r.emit("gpsimd", lambda g: g.dma_start(out=cq[:, TOK:2 * TOK], in_=sinq_in),
           inc="dG", inc_by=16)
    r.realize(nc, sems)

    r = Rec(C)
    r.emit("gpsimd", lambda g: g.wait_ge(sems["dS"], C["dS"]))
    r.emit("gpsimd", lambda g: g.collective_compute(
        "AllGather", ALU.bypass, replica_groups=[list(range(NCORES))],
        ins=[kc.ap().opt()], outs=[kg.ap().opt()]), inc="cc")
    r.emit("gpsimd", lambda g: g.collective_compute(
        "AllGather", ALU.bypass, replica_groups=[list(range(NCORES))],
        ins=[vc.ap().opt()], outs=[vg.ap().opt()]), inc="cc")
    r.emit("gpsimd", lambda g: g.dma_start(out=mkr[0:1, :], in_=onesf[0:1, 0:16]),
           inc="ccd", inc_by=16)
    r.realize(nc, sems)

    qguard = {}

    def q_block(og):
        stP = ExitStack()
        kps = [stP.enter_context(nc.psum_tensor(f"qps{og}_{i}", [128, 1024], F32))
               for i in range(4)]
        r = Rec(C)
        guard = qguard
        xload = snap("dG")
        prevdone = [snap("sV"), snap("sA")]
        for c in range(NCH):
            w = guard.pop(("ws", c % 2), [])
            wl = "dS0" if c % 2 == 0 else "dS1"
            r.emit("sync", lambda s, c=c: s.dma_start(
                out=wstgQ[:, (c % 2) * 512:(c % 2) * 512 + 512],
                in_=wqT[c * 128:(c + 1) * 128, og * 512:(og + 1) * 512]),
                waits=w, inc=wl, inc_by=16)
            r.emit("vector", lambda v, c=c: v.tensor_copy(
                wbufQ[:, (c % 2) * 512:(c % 2) * 512 + 512],
                wstgQ[:, (c % 2) * 512:(c % 2) * 512 + 512]),
                waits=[snap(wl)] + guard.pop(("w", c % 2), []), inc="sV")
            guard[("ws", c % 2)] = [snap("sV")]
            wdone = snap("sV")

            def mmgrp(t, c=c):
                ins = None
                for hh in range(4):
                    lt = wbufQ[:, (c % 2) * 512 + hh * 128:(c % 2) * 512 + hh * 128 + 128]
                    for half in range(2):
                        ins = t.matmul(
                            kps[hh][:, half * 512:half * 512 + 512], lt,
                            xqb[:, c * 1024 + half * 512:c * 1024 + half * 512 + 512],
                            start=(c == 0), stop=(c == NCH - 1))
                return ins
            r.emit("tensor", mmgrp,
                   waits=[wdone] + ([xload] if c == 0 else []) + (prevdone if c == 0 else []),
                   inc="sPs")
            guard[("w", c % 2)] = [snap("sPs")]
        mmdone = snap("sPs")
        cos = cq[:, 0:TOK]
        sin = cq[:, TOK:2 * TOK]
        for hh in range(4):
            r.emit("scalar", lambda sc, hh=hh: sc.copy(
                qrawQ[:, hh * 1024:(hh + 1) * 1024], kps[hh][:, :]),
                waits=[mmdone] if hh == 0 else (), inc="sA")
        evdone = snap("sA")
        for hh in range(4):
            dcol = (og * 4 + hh) * 1024

            def rotmm(t, hh=hh):
                ins = None
                for half in range(2):
                    ins = t.matmul(kps[hh][:, half * 512:half * 512 + 512],
                                   rotmb[:, :],
                                   qrawQ[:, hh * 1024 + half * 512:
                                        hh * 1024 + half * 512 + 512],
                                   start=True, stop=True)
                return ins
            r.emit("tensor", rotmm, waits=[evdone] if hh == 0 else (), inc="sPs")
            rotdone = snap("sPs")

            def rope(v, hh=hh, dcol=dcol):
                v.tensor_mul(rtmpQ[:, :], qrawQ[:, hh * 1024:(hh + 1) * 1024], cos)
                v.tensor_mul(sq2[:, :], kps[hh][:, :], sin)
                return v.tensor_add(qfull[:, dcol:dcol + 1024], rtmpQ[:, :], sq2[:, :])
            r.emit("vector", rope, waits=[rotdone], inc="sV")
        r.realize(nc, sems)
        stP.close()

    for og in range(8):
        q_block(og)

    stP = ExitStack()
    statq = stP.enter_context(nc.psum_tensor("statq", [1, 1024], F32))
    r = Rec(C)
    guard = {}
    for h in range(H):
        r.emit("vector", lambda v, h=h: v.tensor_mul(
            sq2[:, :], qfull[:, h * 1024:(h + 1) * 1024], qfull[:, h * 1024:(h + 1) * 1024]),
            waits=guard.pop("sqfree", []), inc="sV")
        w = snap("sV")
        for half in range(2):
            r.emit("tensor", lambda t, half=half: t.matmul(
                statq[0:1, half * 512:half * 512 + 512], onesb[:, 0:1],
                sq2[:, half * 512:half * 512 + 512], start=True, stop=True),
                waits=[w] + (guard.pop("statps", []) if half == 0 else []), inc="sPs")
        guard["sqfree"] = [snap("sPs")]
        r.emit("scalar", lambda sc: sc.copy(nqrow[0:1, :], statq[0:1, :]),
               waits=[snap("sPs")] + guard.pop("nqfree", []), inc="sA")
        guard["statps"] = [snap("sA")]
        r.emit("vector", lambda v, h=h: v.reduce_max(
            q2row[0:1, h:h + 1], nqrow[0:1, :], axis=AX.X), waits=[snap("sA")], inc="sV")
        guard["nqfree"] = [snap("sV")]
    r.realize(nc, sems)
    stP.close()
    stQ.close()

    stB = ExitStack()
    krows = stB.enter_context(nc.sbuf_tensor("krows", [8, 8 * 1024], BF16))
    k2pad = stB.enter_context(nc.sbuf_tensor("k2pad", [8, 1], F32))
    k2eps = stB.enter_context(nc.psum_tensor("k2eps", [1, 32], F32))
    btp = stB.enter_context(nc.psum_tensor("btp", [128, 32], F32))
    r = Rec(C)
    r.emit("gpsimd", lambda g: g.wait_ge(sems["cc"], 2))
    for rr in range(NCORES):
        r.emit("gpsimd", lambda g, rr=rr: g.dma_start(
            out=krows[0:8, rr * 1024:(rr + 1) * 1024],
            in_=kg.ap()[rr * 1032 + 1024:rr * 1032 + 1032, :]), inc="dG", inc_by=16)
    wd = snap("dG")
    r.emit("vector", lambda v: v.reduce_max(k2pad[0:8, 0:1], krows[0:8, :], axis=AX.X),
           waits=[wd], inc="sV")
    r.emit("tensor", lambda t: t.matmul(k2eps[0:1, :], k2pad[0:8, 0:1],
                                        exp8b[0:8, 0:32], start=True, stop=True),
           waits=[snap("sV")], inc="sPs")
    r.emit("vector", lambda v: v.tensor_mul(srow[0:1, :], q2row[0:1, :],
                                            k2eps[0:1, :]),
           waits=[snap("sPs")], inc="sV")

    r.emit("scalar", lambda sc: sc.activation(
        srow2[0:1, :], srow[0:1, :], AF.Ln, bias=epsA[0:1, 0:1],
        scale=1.0 / 128.0), waits=[snap("sV")], inc="sA")
    r.emit("scalar", lambda sc: sc.activation(
        srow[0:1, :], srow2[0:1, :], AF.Exp, bias=0.0, scale=0.5),
        waits=[snap("sA")], inc="sA")
    r.emit("vector", lambda v: v.tensor_scalar_mul(srow2[0:1, :], srow[0:1, :], -1.0),
           waits=[snap("sA")], inc="sV")
    r.emit("tensor", lambda t: t.matmul(btp[:, :], onesf[0:1, :], srow2[0:1, 0:32],
                                        start=True, stop=True),
           waits=[snap("sV")], inc="sPs")
    r.emit("scalar", lambda sc: sc.copy(btbl[:, :], btp[:, :]),
           waits=[snap("sPs")], inc="sA")
    r.realize(nc, sems)
    stB.close()

    stT = ExitStack()
    kbuf = stT.enter_context(nc.sbuf_tensor("kbuf", [128, 2 * 1024], BF16))
    vbuf = stT.enter_context(nc.sbuf_tensor("vbuf", [128, 2 * 1024], BF16))
    ptb = stT.enter_context(nc.sbuf_tensor("ptb", [128, 2 * 1024], BF16))
    ascr = stT.enter_context(nc.sbuf_tensor("ascr", [128, 1024], F32))
    dacc = stT.enter_context(nc.sbuf_tensor("dacc", [128, 2 * 1024], F32))
    atile = stT.enter_context(nc.sbuf_tensor("atile", [128, 2 * 1024], BF16))
    drow = stT.enter_context(nc.sbuf_tensor("drow", [1, 1024], F32))
    drow2 = stT.enter_context(nc.sbuf_tensor("drow2", [128, 1024], F32))
    sps = [stT.enter_context(nc.psum_tensor(f"sps{i}", [128, 1024], F32)) for i in range(2)]
    outps = stT.enter_context(nc.psum_tensor("outps", [128, 1024], F32))
    denps = stT.enter_context(nc.psum_tensor("denps", [128, 1024], F32))

    r = Rec(C)
    guard = {}
    r.emit("sync", lambda s: s.wait_ge(sems["ccd"], 16))
    eg = 0
    pair = 0
    pending = None

    def emit_pv(pg):
        waits = [pg["wA"]] + ([pg["wV"]] if pg["diag"] else []) + pg["whead"]

        def pv(t, pg=pg):
            ins = None
            for ti, tt in enumerate(pg["tiles"]):
                rhs = ptb[:, (pg["eg"] % 2) * 1024 + ti * pg["slot"]:
                          (pg["eg"] % 2) * 1024 + ti * pg["slot"] + pg["N"]]
                st = pg["first"] and ti == 0
                ins = t.matmul(
                    outps[:, pg["jlo"] * 128:pg["jlo"] * 128 + pg["N"]],
                    vbuf[:, pg["pp"] * 1024 + tt * 128:pg["pp"] * 1024 + tt * 128 + 128],
                    rhs, start=st, stop=False, skip_group_check=True)
            return ins
        r.emit("tensor", pv, waits=waits, inc="sPv")
        guard[("ptb", pg["eg"] % 2)] = [snap("sPv"), pg["dsum"]]
        if pg["tiles"][-1] == 7:
            guard[("kv", pg["pp"])] = [snap("sPv")]

    headguard = [snap("sA")]
    pending_store = None

    def emit_store(ps):
        hs, dfree = ps
        wl = "dG0" if hs % 2 == 0 else "dG1"
        r.emit("gpsimd", lambda g, hs=hs: g.dma_start(
            out=atnd.ap()[hs * 128:(hs + 1) * 128, :],
            in_=atile[:, (hs % 2) * 1024:(hs % 2) * 1024 + 1024]),
            waits=[dfree], inc=wl, inc_by=16)
        guard[("atile", hs % 2)] = [snap(wl)]

    for h in range(H):
        first_pv_of_head = True
        for (g, b) in attn_pairs():
            pp = pair % 2
            pw = guard.pop(("kv", pp), [])
            kv = h // 4
            wl = "dS0" if pp == 0 else "dS1"
            r.emit("sync", lambda s, b=b, kv=kv, pp=pp: s.dma_start(
                out=kbuf[:, pp * 1024:pp * 1024 + 1024],
                in_=kg.ap()[b * 1032 + kv * 128:b * 1032 + (kv + 1) * 128, :]),
                waits=pw, inc=wl, inc_by=16)
            r.emit("sync", lambda s, b=b, kv=kv, pp=pp: s.dma_start(
                out=vbuf[:, pp * 1024:(pp + 1) * 1024]
                    .rearrange("p (t f) -> p t f", f=128),
                in_=vg.ap()[b * 1024:(b + 1) * 1024, kv * 128:(kv + 1) * 128]
                    .rearrange("(t p) f -> p t f", p=128)), inc=wl, inc_by=16)
            kvdone = snap(wl)
            diag = (g == 0) or (b >= 4)
            for gi, gr in enumerate(attn_groups_of_pair(g, b)):
                tiles, slot, span, N, jlo, jhi = (gr["tiles"], gr["slot"], gr["span"],
                                                  gr["N"], gr["jlo"], gr["jhi"])
                sw = ([kvdone] + guard.pop(("sps", eg % 2), [])
                      + ([snap("sA")] if eg == 0 else []))

                def sgrp(t, tiles=tiles, slot=slot, N=N, jlo=jlo, h=h, pp=pp, eg=eg):
                    ins = None
                    for ti, tt in enumerate(tiles):
                        ins = t.matmul(
                            sps[eg % 2][:, ti * slot:ti * slot + N],
                            kbuf[:, pp * 1024 + tt * 128:pp * 1024 + tt * 128 + 128],
                            qfull[:, h * 1024 + jlo * 128:h * 1024 + jlo * 128 + N],
                            start=True, stop=True)
                    return ins
                r.emit("tensor", sgrp, waits=sw, inc="sPs")
                sdone = snap("sPs")
                ew = [sdone] + guard.pop(("ptb", eg % 2), [])
                r.emit("scalar", lambda sc, eg=eg, span=span, h=h: sc.activation(
                    ptb[:, (eg % 2) * 1024:(eg % 2) * 1024 + span],
                    sps[eg % 2][:, 0:span], AF.Exp,
                    bias=btbl[:, h:h + 1], scale=ISQ), waits=ew, inc="sA")
                guard[("sps", eg % 2)] = [snap("sA")]
                wA = snap("sA")
                wV = None
                if diag:
                    def mgrp(v, tiles=tiles, slot=slot, eg=eg):
                        ins = None
                        for ti, tt in enumerate(tiles):
                            sl = ptb[:, (eg % 2) * 1024 + ti * slot:
                                     (eg % 2) * 1024 + ti * slot + 128]
                            ins = v.tensor_mul(sl, sl,
                                               masks[:, tt * 128:(tt + 1) * 128])
                        return ins
                    r.emit("vector", mgrp, waits=[wA], inc="sV")
                    wV = snap("sV")
                    if pending_store is not None:
                        emit_store(pending_store)
                        pending_store = None
                dw = [wA] + ([wV] if diag else []) + guard.pop(("dacc", h % 2), [])

                def dadd(v, tiles=tiles, slot=slot, N=N, jlo=jlo, eg=eg, h=h,
                         first=(b == 0 and gi == 0)):
                    ins = None
                    dst = dacc[:, (h % 2) * 1024 + jlo * 128:
                               (h % 2) * 1024 + jlo * 128 + N]
                    for ti in range(len(tiles)):
                        sl = ptb[:, (eg % 2) * 1024 + ti * slot:
                                 (eg % 2) * 1024 + ti * slot + N]
                        if first and ti == 0:
                            ins = v.tensor_copy(dst, sl)
                        else:
                            ins = v.tensor_add(dst, dst, sl)
                    return ins
                r.emit("vector", dadd, waits=dw, inc="sV")
                dsum = snap("sV")
                if pending is not None:
                    emit_pv(pending)
                pending = dict(tiles=tiles, slot=slot, N=N, jlo=jlo,
                               diag=diag, pp=pp, eg=eg, wA=wA, wV=wV,
                               first=(b == 0 and gi == 0), dsum=dsum,
                               whead=headguard if first_pv_of_head else [])
                first_pv_of_head = False
                eg += 1
            pair += 1
        emit_pv(pending)
        pending = None
        pvall = snap("sPv")
        dnw = [snap("sV")] + guard.pop("denps", [])

        def denmm(t, h=h):
            ins = None
            for half in range(2):
                ins = t.matmul(denps[0:1, half * 512:half * 512 + 512],
                               onesf[:, 0:1],
                               dacc[:, (h % 2) * 1024 + half * 512:
                                    (h % 2) * 1024 + half * 512 + 512],
                               start=True, stop=True, skip_group_check=True)
            return ins
        r.emit("tensor", denmm, waits=dnw, inc="sPs")
        guard[("dacc", h % 2)] = [snap("sPs")]
        dmmdone = snap("sPs")
        r.emit("scalar", lambda sc: sc.copy(ascr[:, :], outps[:, :]),
               waits=[pvall] + guard.pop("ascr", []), inc="sA")
        outfree = snap("sA")
        r.emit("scalar", lambda sc: sc.copy(drow[0:1, :], denps[0:1, :]),
               waits=[dmmdone], inc="sA")
        rw = snap("sA")

        def bcast(t):
            ins = None
            for half in range(2):
                ins = t.matmul(denps[:, half * 512:half * 512 + 512],
                               onesf[0:1, 0:128],
                               drow[0:1, half * 512:half * 512 + 512],
                               start=True, stop=True, skip_group_check=True)
            return ins
        r.emit("tensor", bcast, waits=[rw], inc="sPs")
        r.emit("vector", lambda v: v.reciprocal(drow2[:, :], denps[:, :]),
               waits=[snap("sPs")] + guard.pop("drow2", []), inc="sV")
        rcpw = snap("sV")
        fw = ([rcpw, outfree] + guard.pop(("atile", h % 2), []))
        r.emit("vector", lambda v, h=h: v.tensor_mul(
            atile[:, (h % 2) * 1024:(h % 2) * 1024 + 1024],
            ascr[:, :], drow2[:, :]), waits=fw, inc="sV")
        denfree = snap("sV")
        guard["drow2"] = [denfree]
        guard["ascr"] = [denfree]
        pending_store = (h, denfree)
        guard["denps"] = [denfree]
        headguard = [outfree]
    emit_store(pending_store)
    pending_store = None
    g0, g1 = snap("dG0"), snap("dG1")
    r.emit("gpsimd", lambda g: g.wait_ge(sems["dG0"], g0[1]))
    r.emit("gpsimd", lambda g: g.wait_ge(sems["dG1"], g1[1]))
    r.realize(nc, sems)
    stT.close()
    stQA.close()

    stL1 = ExitStack()
    ln1full = stL1.enter_context(nc.sbuf_tensor("ln1full", [128, 8 * 4096], BF16))
    stO = ExitStack()
    oacc = stO.enter_context(nc.sbuf_tensor("oacc", [128, 4 * 4096], F32))
    atb = stO.enter_context(nc.sbuf_tensor("atb", [128, 4 * 512], BF16))
    wob = stO.enter_context(nc.sbuf_tensor("wob", [128, 4 * 1024], BF16))
    xqq = stO.enter_context(nc.sbuf_tensor("xqq", [128, 2 * 1024], F32))
    sqf = stO.enter_context(nc.sbuf_tensor("sqf", [128, 4096], F32))
    ln1f = stO.enter_context(nc.sbuf_tensor("ln1f", [128, 4096], F32))
    ostat = stO.enter_context(nc.sbuf_tensor("ostat", [128, 28], F32))
    psO = [stO.enter_context(nc.psum_tensor(f"psO{i}", [128, 1024], F32)) for i in range(4)]

    r = Rec(C)
    guard = {}
    n = 0
    for ttg in range(2):
        for hq in range(4):
            for ocg in range(NCH // 2):
                oc0 = ocg * 2
                wl = "dG0" if n % 2 == 0 else "dG1"
                r.emit("gpsimd", lambda g, hq=hq, oc0=oc0, n=n: g.dma_start(
                    out=wob[:, (n % 2) * 2048:(n % 2) * 2048 + 2048]
                        .rearrange("p (k f) -> p k f", f=1024),
                    in_=woT[oc0 * 128:(oc0 + 2) * 128, hq * 1024:(hq + 1) * 1024]
                        .rearrange("(k p) f -> p k f", p=128)),
                    waits=guard.pop(("wob", n % 2), []), inc=wl, inc_by=16)
                wG = snap(wl)
                sl = "dS0" if n % 2 == 0 else "dS1"
                r.emit("sync", lambda s, ttg=ttg, oc0=oc0, n=n: s.dma_start(
                    out=atb[:, (n % 2) * 1024:(n % 2) * 1024 + 1024]
                        .rearrange("p (k f) -> p k f", f=512),
                    in_=atnd.ap()[oc0 * 128:(oc0 + 2) * 128, ttg * 512:(ttg + 1) * 512]
                        .rearrange("(k p) f -> p k f", p=128)),
                    waits=guard.pop(("atb", n % 2), []), inc=sl, inc_by=16)
                wS = snap(sl)
                mmw = [wG, wS] + (guard.pop("psO", []) if oc0 == 0 else [])

                def mmo(t, oc0=oc0, n=n):
                    ins = None
                    for k in range(2):
                        oc = oc0 + k
                        for tt in range(4):
                            for half in range(2):
                                ins = t.matmul(
                                    psO[tt][:, half * 512:half * 512 + 512],
                                    atb[:, (n % 2) * 1024 + k * 512 + tt * 128:
                                        (n % 2) * 1024 + k * 512 + tt * 128 + 128],
                                    wob[:, (n % 2) * 2048 + k * 1024 + half * 512:
                                        (n % 2) * 2048 + k * 1024 + half * 512 + 512],
                                    start=(oc == 0), stop=(oc == NCH - 1))
                    return ins
                r.emit("tensor", mmo, waits=mmw, inc="sPs")
                guard[("wob", n % 2)] = [snap("sPs")]
                guard[("atb", n % 2)] = [snap("sPs")]
                n += 1
            mmdone = snap("sPs")
            addsnap = None
            for tt in range(4):
                m = (ttg * 4 + hq) * 4 + tt
                sl = "dS2" if m % 2 == 0 else "dS3"
                r.emit("sync", lambda s, ttg=ttg, tt=tt, hq=hq, m=m: s.dma_start(
                    out=xqq[:, (m % 2) * 1024:(m % 2) * 1024 + 1024],
                    in_=xq[(ttg * 4 + tt) * 128:(ttg * 4 + tt + 1) * 128,
                           hq * 1024:(hq + 1) * 1024]),
                    waits=guard.pop(("xqq", m % 2), []), inc=sl, inc_by=16)
                wS = snap(sl)
                r.emit("vector", lambda v, tt=tt, hq=hq, m=m: v.tensor_add(
                    oacc[:, tt * 4096 + hq * 1024:tt * 4096 + hq * 1024 + 1024],
                    psO[tt][:, :], xqq[:, (m % 2) * 1024:(m % 2) * 1024 + 1024]),
                    waits=[mmdone, wS], inc="sV")
                guard[("xqq", m % 2)] = [snap("sV")]
            guard["psO"] = [snap("sV")]
        SR, SF, wln = ln_chain(r, ostat, 4,
                               [oacc[:, tt * 4096:(tt + 1) * 4096]
                                for tt in range(4)], sqf)
        for tt in range(4):
            k = ttg * 4 + tt
            a = oacc[:, tt * 4096:(tt + 1) * 4096]
            r.emit("vector", lambda v, a=a, tt=tt, SR=SR, SF=SF: v.tensor_scalar(
                ln1f[:, :], a, SR[:, tt:tt + 1], SF[:, tt:tt + 1],
                ALU.mult, ALU.add),
                waits=[wln] + guard.pop("ln1f", []), inc="sV")
            nrm = snap("sV")
            r.emit("sync", lambda s, k=k: s.dma_start(
                out=r2d.ap()[k * 128:(k + 1) * 128, :],
                in_=ln1f[:, :]),
                waits=[nrm], inc="dO", inc_by=16)
            r.emit("vector", lambda v, k=k: v.tensor_copy(
                ln1full[:, k * 4096:(k + 1) * 4096],
                ln1f[:, :]), waits=[nrm], inc="sV")
            guard["ln1f"] = [snap("dO"), snap("sV")]
    r.emit("sync", lambda s: s.wait_ge(sems["dO"], C["dO"]))
    r.realize(nc, sems)
    stO.close()

    stL = ExitStack()
    xt2st = stL.enter_context(nc.sbuf_tensor("xt2st", [128, 2 * 4096], BF16))
    ptL = [stL.enter_context(nc.psum_tensor(f"ptL{i}", [128, 128], BF16)) for i in range(2)]
    r = Rec(C)
    guard = {}
    for tt in range(8):
        for c in range(NCH):
            nn = tt * NCH + c
            r.emit("tensor", lambda t, tt=tt, c=c, nn=nn: t.transpose(
                ptL[nn % 2][:, :],
                ln1full[:, tt * 4096 + c * 128:tt * 4096 + c * 128 + 128], ident[:, :]),
                waits=guard.pop(("ptL", nn % 2), []) + ([snap("sV")] if nn == 0 else []),
                inc="sT")
            r.emit("vector", lambda v, tt=tt, c=c, nn=nn: v.tensor_copy(
                xt2st[:, (tt % 2) * 4096 + c * 128:(tt % 2) * 4096 + c * 128 + 128],
                ptL[nn % 2][:, :]),
                waits=[snap("sT")] + (guard.pop(("xt2st", tt % 2), []) if c == 0 else []),
                inc="sV")
            guard[("ptL", nn % 2)] = [snap("sV")]
        sl = "dS2" if tt % 2 == 0 else "dS3"
        r.emit("sync", lambda s, tt=tt: s.dma_start(
            out=xt2d.ap().rearrange("(c p) t -> p c t", p=128)[:, :, tt * 128:(tt + 1) * 128],
            in_=xt2st[:, (tt % 2) * 4096:(tt % 2) * 4096 + 4096]
                .rearrange("p (c f) -> p c f", f=128)),
            waits=[snap("sV")], inc=sl, inc_by=16)
        guard[("xt2st", tt % 2)] = [snap(sl)]
    r.realize(nc, sems)
    stL.close()
    stL1.close()

    stMid = ExitStack()
    midsb = stMid.enter_context(nc.sbuf_tensor("midsb", [128, FT * 512], BF16))
    stF = ExitStack()
    xt2b = stF.enter_context(nc.sbuf_tensor("xt2b", [128, NCH * 1024], BF16))
    gwb = stF.enter_context(nc.sbuf_tensor("gwb", [128, 2 * 4096], BF16))
    uwb = stF.enter_context(nc.sbuf_tensor("uwb", [128, 2 * 4096], BF16))
    gsil = stF.enter_context(nc.sbuf_tensor("gsil", [128, 2 * 1024], F32))
    mid1 = stF.enter_context(nc.sbuf_tensor("mid1", [128, 1024], F32))
    midb2 = stF.enter_context(nc.sbuf_tensor("midb2", [128, 2 * 512], BF16))
    gps = [stF.enter_context(nc.psum_tensor(f"gps{i}", [128, 1024], F32)) for i in range(2)]
    ups = [stF.enter_context(nc.psum_tensor(f"ups{i}", [128, 1024], F32)) for i in range(2)]

    r = Rec(C)
    guard = {}
    r.emit("gpsimd", lambda g: g.dma_start(
        out=xt2b[:, :].rearrange("p (c t) -> p c t", t=TOK),
        in_=xt2d.ap().rearrange("(c p) t -> p c t", p=128)),
        waits=[snap("dS2"), snap("dS3")], inc="dG", inc_by=16)
    xload = snap("dG")
    for ft in range(FT):
        gw = guard.pop(("wf", ft % 2), []) + ([xload] if ft < 2 else [])
        wl = "dG0" if ft % 2 == 0 else "dG1"
        r.emit("gpsimd", lambda g, ft=ft: g.dma_start(
            out=gwb[:, (ft % 2) * 4096:(ft % 2) * 4096 + 4096]
                .rearrange("p (c f) -> p c f", f=128),
            in_=gwT.rearrange("(c p) f -> p c f", p=128)[:, :, ft * 128:(ft + 1) * 128]),
            waits=gw, inc=wl, inc_by=16)
        r.emit("gpsimd", lambda g, ft=ft: g.dma_start(
            out=uwb[:, (ft % 2) * 4096:(ft % 2) * 4096 + 4096]
                .rearrange("p (c f) -> p c f", f=128),
            in_=uwT.rearrange("(c p) f -> p c f", p=128)[:, :, ft * 128:(ft + 1) * 128]),
            inc=wl, inc_by=16)
        wG = snap(wl)
        mmw = [wG] + guard.pop(("gps", ft % 2), []) + guard.pop(("ups", ft % 2), [])

        def mmf(t, ft=ft):
            ins = None
            for c in range(NCH):
                for (ps, wb) in ((gps, gwb), (ups, uwb)):
                    lt = wb[:, (ft % 2) * 4096 + c * 128:(ft % 2) * 4096 + c * 128 + 128]
                    for half in range(2):
                        ins = t.matmul(
                            ps[ft % 2][:, half * 512:half * 512 + 512], lt,
                            xt2b[:, c * 1024 + half * 512:c * 1024 + half * 512 + 512],
                            start=(c == 0), stop=(c == NCH - 1))
            return ins
        r.emit("tensor", mmf, waits=mmw, inc="sPs")
        guard[("wf", ft % 2)] = [snap("sPs")]
        mmdone = snap("sPs")
        r.emit("scalar", lambda sc, ft=ft: sc.activation(
            gsil[:, (ft % 2) * 1024:(ft % 2) * 1024 + 1024], gps[ft % 2][:, :], AF.Silu,
            bias=0.0, scale=gsb[:, ft:ft + 1]),
            waits=[mmdone] + guard.pop(("gsil", ft % 2), []), inc="sA")
        guard[("gps", ft % 2)] = [snap("sA")]
        wA = snap("sA")

        def midop(v, ft=ft):
            v.tensor_mul(mid1[:, :], ups[ft % 2][:, :],
                         gsil[:, (ft % 2) * 1024:(ft % 2) * 1024 + 1024])
            v.tensor_scalar(midsb[:, ft * 512:(ft + 1) * 512],
                            mid1[:, 0:512], usb[:, ft:ft + 1], None, ALU.mult)
            return v.tensor_scalar(midb2[:, (ft % 2) * 512:(ft % 2) * 512 + 512],
                                   mid1[:, 512:1024], usb[:, ft:ft + 1], None, ALU.mult)
        r.emit("vector", midop, waits=[wA] + guard.pop(("midb", ft % 2), []), inc="sV")
        guard[("ups", ft % 2)] = [snap("sV")]
        guard[("gsil", ft % 2)] = [snap("sV")]
        sl = "dS0" if ft % 2 == 0 else "dS1"
        r.emit("sync", lambda s, ft=ft: s.dma_start(
            out=midd.ap()[ft * 128:(ft + 1) * 128, :],
            in_=midb2[:, (ft % 2) * 512:(ft % 2) * 512 + 512]),
            waits=[snap("sV")], inc=sl, inc_by=16)
        guard[("midb", ft % 2)] = [snap(sl)]
    r.emit("sync", lambda s: s.wait_ge(sems["dS0"], C["dS0"]))
    r.emit("sync", lambda s: s.wait_ge(sems["dS1"], C["dS1"]))
    r.realize(nc, sems)
    stF.close()

    dstage = stMid.enter_context(nc.sbuf_tensor("dstage", [128, NCH * 1024], BF16))
    dwb = stMid.enter_context(nc.sbuf_tensor("dwb", [128, 4 * 512], BF16))
    midb = stMid.enter_context(nc.sbuf_tensor("midb", [128, 4 * 512], BF16))
    stDp = ExitStack()
    dps = [stDp.enter_context(nc.psum_tensor(f"dps{i}", [128, 1024], F32)) for i in range(4)]

    r = Rec(C)
    guard = {}
    msv = snap("sV")
    for dg in range(8):
        for fcg in range(FT // 2):
            fc0 = fcg * 2
            nn = dg * (FT // 2) + fcg
            wl = "dG0" if nn % 2 == 0 else "dG1"
            r.emit("gpsimd", lambda g, dg=dg, fc0=fc0, nn=nn: g.dma_start(
                out=dwb[:, (nn % 2) * 1024:(nn % 2) * 1024 + 1024]
                    .rearrange("p (k f) -> p k f", f=512),
                in_=dwT[fc0 * 128:(fc0 + 2) * 128, dg * 512:(dg + 1) * 512]
                    .rearrange("(k p) f -> p k f", p=128)),
                waits=guard.pop(("dwb", nn % 2), []), inc=wl, inc_by=16)
            wG = snap(wl)
            sl = "dS2" if nn % 2 == 0 else "dS3"
            r.emit("sync", lambda s, fc0=fc0, nn=nn: s.dma_start(
                out=midb[:, (nn % 2) * 1024:(nn % 2) * 1024 + 1024]
                    .rearrange("p (k f) -> p k f", f=512),
                in_=midd.ap()[fc0 * 128:(fc0 + 2) * 128, :]
                    .rearrange("(k p) f -> p k f", p=128)),
                waits=guard.pop(("midb", nn % 2), []), inc=sl, inc_by=16)
            wS = snap(sl)
            mmw = ([wG, wS] + (guard.pop("dps", []) if fc0 == 0 else [])
                   + ([msv] if nn == 0 else []))

            def mmd(t, fc0=fc0, nn=nn):
                ins = None
                for k in range(2):
                    fc = fc0 + k
                    for dt in range(4):
                        lt = dwb[:, (nn % 2) * 1024 + k * 512 + dt * 128:
                                 (nn % 2) * 1024 + k * 512 + dt * 128 + 128]
                        ins = t.matmul(
                            dps[dt][:, 0:512], lt, midsb[:, fc * 512:(fc + 1) * 512],
                            start=(fc == 0), stop=(fc == FT - 1))
                        ins = t.matmul(
                            dps[dt][:, 512:1024], lt,
                            midb[:, (nn % 2) * 1024 + k * 512:
                                 (nn % 2) * 1024 + k * 512 + 512],
                            start=(fc == 0), stop=(fc == FT - 1))
                return ins
            r.emit("tensor", mmd, waits=mmw, inc="sPs")
            guard[("dwb", nn % 2)] = [snap("sPs")]
            guard[("midb", nn % 2)] = [snap("sPs")]
        mmdone = snap("sPs")
        last = None
        for dt in range(4):
            last = r.emit("scalar", lambda sc, dg=dg, dt=dt: sc.activation(
                dstage[:, (dg * 4 + dt) * 1024:(dg * 4 + dt) * 1024 + 1024],
                dps[dt][:, :], AF.Copy, bias=0.0, scale=dsb[:, dg * 4 + dt:dg * 4 + dt + 1]),
                waits=[mmdone] if dt == 0 else (), inc="sA")
        guard["dps"] = [snap("sA")]
    r.realize(nc, sems)
    stDp.close()

    stZ = ExitStack()
    ftok = stZ.enter_context(nc.sbuf_tensor("ftok", [128, 4096], F32))
    r2b = stZ.enter_context(nc.sbuf_tensor("r2b", [128, 4096], F32))
    zstat = stZ.enter_context(nc.sbuf_tensor("zstat", [128, 8], F32))
    ptZ = [stZ.enter_context(nc.psum_tensor(f"ptZ{i}", [128, 128], BF16)) for i in range(2)]

    r = Rec(C)
    guard = {}
    for tt in range(8):
        sl = "dS0" if tt % 2 == 0 else "dS1"
        r.emit("sync", lambda s, tt=tt: s.dma_start(
            out=r2b[:, :], in_=r2d.ap()[tt * 128:(tt + 1) * 128, :]),
            waits=guard.pop("r2b", []), inc=sl, inc_by=16)
        wS = snap(sl)
        for c in range(NCH):
            nn = tt * NCH + c
            r.emit("tensor", lambda t, tt=tt, c=c, nn=nn: t.transpose(
                ptZ[nn % 2][:, :],
                dstage[:, c * 1024 + tt * 128:c * 1024 + tt * 128 + 128], ident[:, :]),
                waits=guard.pop(("ptZ", nn % 2), []) + ([snap("sA")] if nn == 0 else []),
                inc="sT")
            r.emit("scalar", lambda sc, tt=tt, c=c, nn=nn: sc.copy(
                ftok[:, c * 128:c * 128 + 128],
                ptZ[nn % 2][:, :]),
                waits=[snap("sT")] + (guard.pop("ftok", []) if c == 0 else []),
                inc="sA")
            guard[("ptZ", nn % 2)] = [snap("sA")]
        evdone = snap("sA")
        fa = ftok[:, :]
        r.emit("vector", lambda v, fa=fa: v.tensor_add(fa, fa, r2b[:, :]),
               waits=[wS, evdone], inc="sV")
        SR, SF, wln = ln_chain(r, zstat, 1, [fa], r2b,
                               first_waits=[snap("sV")])
        guard["r2b"] = [wln]
        r.emit("vector", lambda v, fa=fa, SR=SR, SF=SF: v.tensor_scalar(
            fa, fa, SR[:, 0:1], SF[:, 0:1], ALU.mult, ALU.add),
            waits=[wln], inc="sV")
        so = "dS2" if tt % 2 == 0 else "dS3"
        r.emit("sync", lambda s, tt=tt: s.dma_start(
            out=out[tt * 128:(tt + 1) * 128, :],
            in_=ftok[:, :]),
            waits=[snap("sV")], inc=so, inc_by=16)
        guard["ftok"] = [snap(so)]
    r.emit("sync", lambda s: s.wait_ge(sems["dS2"], C["dS2"]))
    r.emit("sync", lambda s: s.wait_ge(sems["dS3"], C["dS3"]))
    r.realize(nc, sems)
    stZ.close()
    stMid.close()
    glob.close()
    return nc



_ROPE_THETA = 10000.0


def _rope_tables(pos):
    freqs = np.exp(-np.arange(0, 128, 2, dtype=np.float32)
                   * np.float32(math.log(_ROPE_THETA) / 128.0))
    ang = pos.astype(np.float32)[None, :] * freqs[:, None]
    cos = np.cos(ang).astype(np.float32)
    sin = np.sin(ang).astype(np.float32)
    return (np.ascontiguousarray(np.vstack([cos, cos])),
            np.ascontiguousarray(np.vstack([sin, sin])))


def _rot_matrix():
    m = np.zeros((128, 128), dtype=BF16NP)
    for d in range(64):
        m[64 + d, d] = -1
        m[d, 64 + d] = 1
    return m


def _stage_inputs(inputs):
    x = np.asarray(inputs["L0_attn_hidden_states"], np.float32)[0]
    Wq = np.asarray(inputs["L0_attn_Wq"], np.float32)
    Wk = np.asarray(inputs["L0_attn_Wk"], np.float32)
    Wv = np.asarray(inputs["L0_attn_Wv"], np.float32)
    Wo = np.asarray(inputs["L0_attn_Wo"], np.float32)
    gw = np.asarray(inputs["L0_ffn_gate_w"], np.int32)
    uw = np.asarray(inputs["L0_ffn_up_w"], np.int32)
    dw = np.asarray(inputs["L0_ffn_down_w"], np.int32)
    gs = np.asarray(inputs["L0_ffn_gate_s"], np.float32)
    us = np.asarray(inputs["L0_ffn_up_s"], np.float32)
    ds = np.asarray(inputs["L0_ffn_down_s"], np.float32)

    shared = {
        "wqT": np.ascontiguousarray(Wq.T),
        "wkT": np.ascontiguousarray(Wk.T),
        "wvT": np.ascontiguousarray(Wv.T),
        "woT": np.ascontiguousarray(Wo.T),
        "gwT": np.ascontiguousarray(gw.T),
        "uwT": np.ascontiguousarray(uw.T),
        "dwT": np.ascontiguousarray(dw.T),
        "gs": np.ascontiguousarray(gs.reshape(FT, 128).T),
        "us": np.ascontiguousarray(us.reshape(FT, 128).T),
        "ds": np.ascontiguousarray(ds.reshape(NCH, 128).T),
        "ident": np.eye(128, dtype=BF16NP),
        "identf": np.eye(128, dtype=np.float32),
        "onesi": np.ones((128, 128), np.float32),
        "rotm": _rot_matrix(),
        "exp8": np.ascontiguousarray(
            (np.arange(32)[None, :] // 4 == np.arange(8)[:, None]).astype(np.float32)),
    }
    xr = x.reshape(64, 128, D)
    in_maps = []
    for i in range(NCORES):
        xqi = np.ascontiguousarray(xr[i::8].reshape(TOK, D))
        xkvi = x[i * TOK:(i + 1) * TOK]
        pos_q = (np.arange(8)[:, None] * 1024 + i * 128 + np.arange(128)[None, :]).reshape(-1)
        pos_kv = np.arange(i * TOK, (i + 1) * TOK)
        cosq, sinq = _rope_tables(pos_q)
        coskv, sinkv = _rope_tables(pos_kv)
        m = np.zeros((128, 1024), dtype=BF16NP)
        for t in range(8):
            if t < i:
                m[:, t * 128:(t + 1) * 128] = 1
            elif t == i:
                m[:, t * 128:(t + 1) * 128] = np.triu(np.ones((128, 128))).astype(BF16NP)
        d = dict(shared)
        d.update({
            "xq": xqi,
            "xqT": np.ascontiguousarray(xqi.T),
            "xkvT": np.ascontiguousarray(xkvi.T),
            "cosq": cosq, "sinq": sinq, "coskv": coskv, "sinkv": sinkv,
            "masks": m,
        })
        in_maps.append(d)
    return in_maps


_NC_CACHE = None


def kernel(**inputs):
    global _NC_CACHE
    in_maps = _stage_inputs(inputs)
    if _NC_CACHE is None:
        _NC_CACHE = build_program()
    res = run_bass_kernel_spmd(_NC_CACHE, in_maps, core_ids=list(range(NCORES)))
    arr = np.stack([res.results[i]["out"] for i in range(NCORES)])
    arr = arr.reshape(NCORES, 8, 128, D).transpose(1, 0, 2, 3).reshape(1, S, D)
    return np.ascontiguousarray(arr)


if __name__ == "__main__":
    import time
    t0 = time.time()
    nc = build_program()
    print(f"build ok in {time.time() - t0:.1f}s; instructions: {nc.n_instructions()}")


# version 13
# speedup vs baseline: 1.1910x; 1.1098x over previous
import math
from contextlib import ExitStack

import numpy as np
import ml_dtypes
import concourse.bass as bass
import concourse.mybir as mybir
from concourse.bass_utils import run_bass_kernel_spmd

F32 = mybir.dt.float32
BF16 = mybir.dt.bfloat16
I32 = mybir.dt.int32
AF = mybir.ActivationFunctionType
ALU = mybir.AluOpType
AX = mybir.AxisListType

D = 4096
S = 8192
NCORES = 8
H = 32
KVH = 8
DFF = 11008
FT = DFF // 128
TOK = 1024
NCH = D // 128
ISQ = 1.0 / math.sqrt(128.0)
BF16NP = ml_dtypes.bfloat16
ENGS = ["sync", "gpsimd", "tensor", "scalar", "vector"]



def attn_pairs():
    return [(0, b) for b in range(4)] + [(1, b) for b in range(8)]


def attn_groups_of_pair(g, b):
    jlo = b if g == 0 else max(b, 4)
    jhi = 4 * g + 3
    N = (jhi - jlo + 1) * 128
    tpg = {512: 2, 384: 2, 256: 4, 128: 8}[N]
    slot = 512 if N >= 384 else N
    return [dict(tiles=list(range(s0, s0 + tpg)), slot=slot,
                 span=(tpg - 1) * slot + N, N=N, jlo=jlo, jhi=jhi)
            for s0 in range(0, 8, tpg)]


class Rec:

    def __init__(self, C):
        self.ops = {e: [] for e in ENGS}
        self.C = C

    def emit(self, eng, fn, waits=(), inc=None, inc_by=1):
        ws = [(s, v) for (s, v) in waits if v > 0]
        self.ops[eng].append((fn, ws, inc, inc_by))
        if inc is not None:
            self.C[inc] += inc_by
        return None if inc is None else (inc, self.C[inc])

    def realize(self, nc, sems):
        with nc.Block() as block:
            for ename in ENGS:
                if not self.ops[ename]:
                    continue

                def mk(ename):
                    def body(eng):
                        for fn, ws, inc, inc_by in self.ops[ename]:
                            for (s, v) in ws:
                                eng.wait_ge(sems[s], v)
                            ins = fn(eng)
                            if inc is not None:
                                ins.then_inc(sems[inc], inc_by)
                    return body
                getattr(block, ename)(mk(ename))


def build_program():
    nc = bass.Bass("TRN2", target_bir_lowering=False)

    xqT = nc.dram_tensor("xqT", [D, TOK], F32, kind="ExternalInput").ap()
    xq = nc.dram_tensor("xq", [TOK, D], F32, kind="ExternalInput").ap()
    xkvT = nc.dram_tensor("xkvT", [D, TOK], F32, kind="ExternalInput").ap()
    wqT = nc.dram_tensor("wqT", [D, D], F32, kind="ExternalInput").ap()
    wkT = nc.dram_tensor("wkT", [D, KVH * 128], F32, kind="ExternalInput").ap()
    wvT = nc.dram_tensor("wvT", [D, KVH * 128], F32, kind="ExternalInput").ap()
    woT = nc.dram_tensor("woT", [D, D], F32, kind="ExternalInput").ap()
    gwT = nc.dram_tensor("gwT", [D, DFF], I32, kind="ExternalInput").ap()
    uwT = nc.dram_tensor("uwT", [D, DFF], I32, kind="ExternalInput").ap()
    dwT = nc.dram_tensor("dwT", [DFF, D], I32, kind="ExternalInput").ap()
    gs_in = nc.dram_tensor("gs", [128, FT], F32, kind="ExternalInput").ap()
    us_in = nc.dram_tensor("us", [128, FT], F32, kind="ExternalInput").ap()
    ds_in = nc.dram_tensor("ds", [128, NCH], F32, kind="ExternalInput").ap()
    cosq_in = nc.dram_tensor("cosq", [128, TOK], F32, kind="ExternalInput").ap()
    sinq_in = nc.dram_tensor("sinq", [128, TOK], F32, kind="ExternalInput").ap()
    coskv_in = nc.dram_tensor("coskv", [128, TOK], F32, kind="ExternalInput").ap()
    sinkv_in = nc.dram_tensor("sinkv", [128, TOK], F32, kind="ExternalInput").ap()
    rotm_in = nc.dram_tensor("rotm", [128, 128], BF16, kind="ExternalInput").ap()
    masks_in = nc.dram_tensor("masks", [128, 1024], BF16, kind="ExternalInput").ap()
    exp8_in = nc.dram_tensor("exp8", [8, 32], F32, kind="ExternalInput").ap()
    ident_in = nc.dram_tensor("ident", [128, 128], BF16, kind="ExternalInput").ap()
    identf_in = nc.dram_tensor("identf", [128, 128], F32, kind="ExternalInput").ap()
    ones_in = nc.dram_tensor("onesi", [128, 128], F32, kind="ExternalInput").ap()
    out = nc.dram_tensor("out", [TOK, D], F32, kind="ExternalOutput").ap()

    kc = nc.dram_tensor("kc", [1032, 1024], BF16)
    vc = nc.dram_tensor("vc", [1024, 1024], BF16)
    kg = nc.dram_tensor("kg", [NCORES * 1032, 1024], BF16)
    vg = nc.dram_tensor("vg", [NCORES * 1024, 1024], BF16)
    atnd = nc.dram_tensor("atnd", [D, TOK], BF16)
    r2d = nc.dram_tensor("r2d", [TOK, D], F32)
    xt2d = nc.dram_tensor("xt2d", [D, TOK], BF16)
    midd = nc.dram_tensor("midd", [DFF, 512], BF16)

    glob = ExitStack()

    def sb(name, shape, dtype):
        return glob.enter_context(nc.sbuf_tensor(name, shape, dtype))

    ident = sb("identt", [128, 128], BF16)
    rotmb = sb("rotmbt", [128, 128], BF16)
    identf = sb("identft", [128, 128], F32)
    onesf = sb("onesft", [128, 128], F32)
    onesb = sb("onesbt", [128, 1], BF16)
    masks = sb("maskst", [128, 1024], BF16)
    gsb = sb("gsbt", [128, FT], F32)
    usb = sb("usbt", [128, FT], F32)
    dsb = sb("dsbt", [128, NCH], F32)
    btbl = sb("btblt", [128, 32], F32)
    exp8b = sb("exp8bt", [8, 32], F32)
    nqrow = sb("nqrowt", [1, 1024], F32)
    q2row = sb("q2rowt", [1, 32], F32)
    k2row = sb("k2rowt", [1, 128], F32)
    srow = sb("srowt", [1, 32], F32)
    srow2 = sb("srow2t", [1, 32], F32)
    epsA = sb("epsAt", [128, 1], F32)

    mkr = sb("mkrt", [1, 16], F32)

    sems = {}
    for name in ["dG", "dG0", "dG1", "dS", "dS0", "dS1", "dS2", "dS3", "dO",
                 "cc", "ccd", "sPs", "sPv", "sA", "sV", "sF1", "sT", "sF2"]:
        sems[name] = glob.enter_context(nc.semaphore(name))
    C = {k: 0 for k in sems}

    def snap(sem):
        return (sem, C[sem])

    def ln_chain(r, stat, B, a_list, sqscr, first_waits=()):
        SA = stat[:, 0 * B:1 * B]
        SB = stat[:, 1 * B:2 * B]
        SC = stat[:, 2 * B:3 * B]
        SD = stat[:, 3 * B:4 * B]
        SE = stat[:, 4 * B:5 * B]
        SR = stat[:, 5 * B:6 * B]
        SF = stat[:, 6 * B:7 * B]

        def v1(v):
            ins = None
            for t, a in enumerate(a_list):
                v.reduce_sum(SA[:, t:t + 1], a, axis=AX.X)
                ins = v.scalar_tensor_tensor(
                    sqscr[:, :], a, 1.0, a, ALU.mult, ALU.mult,
                    accum_out=SB[:, t:t + 1])
            return ins
        r.emit("vector", v1, waits=list(first_waits), inc="sV")
        r.emit("scalar", lambda sc: sc.activation(
            SC, SA, AF.Square, bias=0.0, scale=1.0 / 4096.0),
            waits=[snap("sV")], inc="sA")
        r.emit("vector", lambda v: v.scalar_tensor_tensor(
            SD, SB, 1.0 / 4096.0, SC, ALU.mult, ALU.subtract),
            waits=[snap("sA")], inc="sV")
        r.emit("scalar", lambda sc: sc.activation(
            SE, SD, AF.Ln, bias=epsA[:, 0:1], scale=1.0),
            waits=[snap("sV")], inc="sA")
        r.emit("scalar", lambda sc: sc.activation(
            SR, SE, AF.Exp, bias=0.0, scale=-0.5),
            waits=[snap("sA")], inc="sA")
        r.emit("vector", lambda v: v.scalar_tensor_tensor(
            SF, SA, -1.0 / 4096.0, SR, ALU.mult, ALU.mult),
            waits=[snap("sA")], inc="sV")
        return SR, SF, snap("sV")

    r = Rec(C)
    for dst, src in [(ident, ident_in), (identf, identf_in), (onesf, ones_in),
                     (masks, masks_in), (gsb, gs_in), (usb, us_in), (dsb, ds_in),
                     (rotmb, rotm_in), (exp8b, exp8_in)]:
        r.emit("gpsimd", lambda g, d=dst, s=src: g.dma_start(out=d[:, :], in_=s),
               inc="dG", inc_by=16)
    r.emit("gpsimd", lambda g: g.wait_ge(sems["dG"], C["dG"]))
    r.emit("vector", lambda v: v.memset(onesb[:, :], 1.0))
    r.emit("vector", lambda v: v.memset(epsA[:, :], 1e-5))
    r.realize(nc, sems)

    stA = ExitStack()
    statrow = stA.enter_context(nc.sbuf_tensor("statrowt", [1, 8192], BF16))
    xkvb = stA.enter_context(nc.sbuf_tensor("xkvb", [128, NCH * 1024], BF16))
    ckv = stA.enter_context(nc.sbuf_tensor("ckv", [128, 2 * TOK], F32))
    qraw = stA.enter_context(nc.sbuf_tensor("qraw", [128, 4 * 1024], BF16))
    rtmp = stA.enter_context(nc.sbuf_tensor("rtmp", [128, 1024], F32))
    kfull = stA.enter_context(nc.sbuf_tensor("kfull", [128, KVH * 1024], BF16))
    vfull = stA.enter_context(nc.sbuf_tensor("vfull", [128, KVH * 1024], BF16))
    vtb = stA.enter_context(nc.sbuf_tensor("vtb", [128, 8 * 1024], BF16))
    wbufA = stA.enter_context(nc.sbuf_tensor("wbufA", [128, 2 * 1024], BF16))
    sqbA = stA.enter_context(nc.sbuf_tensor("sqbA", [128, 1024], BF16))

    r = Rec(C)
    r.emit("gpsimd", lambda g: g.dma_start(
        out=xkvb[:, :].rearrange("p (c t) -> p c t", t=TOK),
        in_=xkvT.rearrange("(c p) t -> p c t", p=128)), inc="dG", inc_by=16)
    r.emit("gpsimd", lambda g: g.dma_start(out=ckv[:, 0:TOK], in_=coskv_in),
           inc="dG", inc_by=16)
    r.emit("gpsimd", lambda g: g.dma_start(out=ckv[:, TOK:2 * TOK], in_=sinkv_in),
           inc="dG", inc_by=16)
    r.emit("gpsimd", lambda g: g.wait_ge(sems["dG"], C["dG"]))
    r.realize(nc, sems)

    def proj_block(which, og, wT, dst, cos_sin, xb):
        stP = ExitStack()
        kps = [stP.enter_context(nc.psum_tensor(f"ps{which}{og}_{i}", [128, 1024], F32))
               for i in range(4)]
        r = Rec(C)
        guard = {}
        xload = snap("dG")
        prevdone = [snap("sV"), snap("sA")]
        for cg in range(NCH // 2):
            c0 = cg * 2
            w = [xload] + guard.pop(("w", cg % 2), [])
            wl = "dG0" if cg % 2 == 0 else "dG1"
            r.emit("gpsimd", lambda g, c0=c0, cg=cg: g.dma_start(
                out=wbufA[:, (cg % 2) * 1024:(cg % 2) * 1024 + 1024]
                    .rearrange("p (k f) -> p k f", f=512),
                in_=wT[c0 * 128:(c0 + 2) * 128, og * 512:(og + 1) * 512]
                    .rearrange("(k p) f -> p k f", p=128)),
                waits=w, inc=wl, inc_by=16)
            wdone = snap(wl)

            def mmgrp(t, c0=c0, cg=cg):
                ins = None
                for k in range(2):
                    c = c0 + k
                    for hh in range(4):
                        lt = wbufA[:, (cg % 2) * 1024 + k * 512 + hh * 128:
                                   (cg % 2) * 1024 + k * 512 + hh * 128 + 128]
                        for half in range(2):
                            ins = t.matmul(
                                kps[hh][:, half * 512:half * 512 + 512], lt,
                                xb[:, c * 1024 + half * 512:c * 1024 + half * 512 + 512],
                                start=(c == 0), stop=(c == NCH - 1))
                return ins
            r.emit("tensor", mmgrp,
                   waits=[wdone] + (prevdone if c0 == 0 else []), inc="sPs")
            guard[("w", cg % 2)] = [snap("sPs")]
        mmdone = snap("sPs")
        if which in ("k", "q"):
            cos = cos_sin[:, 0:TOK]
            sin = cos_sin[:, TOK:2 * TOK]
            for hh in range(4):
                r.emit("scalar", lambda sc, hh=hh: sc.copy(
                    qraw[:, hh * 1024:(hh + 1) * 1024], kps[hh][:, :]),
                    waits=[mmdone] if hh == 0 else (), inc="sA")
            evdone = snap("sA")
            for hh in range(4):
                dcol = (og * 4 + hh) * 1024

                def rotmm(t, hh=hh):
                    ins = None
                    for half in range(2):
                        ins = t.matmul(kps[hh][:, half * 512:half * 512 + 512],
                                       rotmb[:, :],
                                       qraw[:, hh * 1024 + half * 512:
                                            hh * 1024 + half * 512 + 512],
                                       start=True, stop=True)
                    return ins
                r.emit("tensor", rotmm, waits=[evdone] if hh == 0 else (), inc="sPs")
                rotdone = snap("sPs")

                def rope(v, hh=hh, dcol=dcol):
                    v.tensor_mul(rtmp[:, :], qraw[:, hh * 1024:(hh + 1) * 1024], cos)
                    v.tensor_mul(sqbA[:, :], kps[hh][:, :], sin)
                    return v.tensor_add(dst[:, dcol:dcol + 1024], rtmp[:, :], sqbA[:, :])
                r.emit("vector", rope, waits=[rotdone], inc="sV")
        else:
            for hh in range(4):
                dcol = (og * 4 + hh) * 1024
                r.emit("scalar",
                       lambda sc, hh=hh, dcol=dcol: sc.copy(
                           dst[:, dcol:dcol + 1024], kps[hh][:, :]),
                       waits=[mmdone] if hh == 0 else (), inc="sA")
        r.realize(nc, sems)
        stP.close()

    for og in range(2):
        proj_block("k", og, wkT, kfull, ckv, xkvb)
    for og in range(2):
        proj_block("v", og, wvT, vfull, ckv, xkvb)

    stP = ExitStack()
    vtp = [stP.enter_context(nc.psum_tensor(f"vtp{i}", [128, 128], BF16)) for i in range(2)]
    stat_ps = stP.enter_context(nc.psum_tensor("statps", [1, 1024], F32))
    r = Rec(C)
    guard = {}
    for n in range(64):
        h, tt = divmod(n, 8)
        r.emit("tensor", lambda t, h=h, tt=tt, n=n: t.transpose(
            vtp[n % 2][:, :], vfull[:, h * 1024 + tt * 128:h * 1024 + tt * 128 + 128],
            ident[:, :]),
            waits=guard.pop(("vtp", n % 2), []) + ([snap("sA")] if n == 0 else []),
            inc="sPs")
        w = snap("sPs")
        r.emit("vector", lambda v, h=h, tt=tt, n=n: v.tensor_copy(
            vtb[:, tt * 1024 + h * 128:tt * 1024 + h * 128 + 128], vtp[n % 2][:, :]),
            waits=[w], inc="sV")
        guard[("vtp", n % 2)] = [snap("sV")]
    for h in range(KVH):
        r.emit("vector", lambda v, h=h: v.tensor_mul(
            sqbA[:, :], kfull[:, h * 1024:(h + 1) * 1024], kfull[:, h * 1024:(h + 1) * 1024]),
            waits=guard.pop("sqfree", []), inc="sV")
        w = snap("sV")
        last = None
        for half in range(2):
            last = r.emit("tensor", lambda t, half=half: t.matmul(
                stat_ps[0:1, half * 512:half * 512 + 512], onesb[:, 0:1],
                sqbA[:, half * 512:half * 512 + 512], start=True, stop=True),
                waits=[w] + (guard.pop("statps", []) if half == 0 else []), inc="sPs")
        guard["sqfree"] = [snap("sPs")]
        r.emit("scalar", lambda sc, h=h: sc.copy(
            statrow[0:1, h * 1024:(h + 1) * 1024], stat_ps[0:1, :]),
            waits=[snap("sPs")], inc="sA")
        guard["statps"] = [snap("sA")]
    r.realize(nc, sems)
    stP.close()

    r = Rec(C)
    r.emit("sync", lambda s: s.dma_start(
        out=kc.ap()[0:1024, :].rearrange("(h p) t -> p h t", p=128),
        in_=kfull[:, :].rearrange("p (h t) -> p h t", t=1024)),
        waits=[snap("sV"), snap("sA")], inc="dS", inc_by=16)
    r.emit("sync", lambda s: s.dma_start(
        out=kc.ap()[1024:1032, :].rearrange("(o h) t -> o h t", o=1),
        in_=statrow[0:1, :].rearrange("o (h t) -> o h t", t=1024)), inc="dS", inc_by=16)
    r.emit("sync", lambda s: s.dma_start(
        out=vc.ap()[:, :].rearrange("(tt p) o -> p tt o", p=128),
        in_=vtb[:, :].rearrange("p (tt o) -> p tt o", o=1024)), inc="dS", inc_by=16)
    r.emit("sync", lambda s: s.wait_ge(sems["dS"], C["dS"]))
    r.realize(nc, sems)
    stA.close()

    stQA = ExitStack()
    qfull = stQA.enter_context(nc.sbuf_tensor("qfull", [128, H * 1024], BF16))
    stQ = ExitStack()
    xqb = stQ.enter_context(nc.sbuf_tensor("xqb", [128, NCH * 1024], BF16))
    cq = stQ.enter_context(nc.sbuf_tensor("cq", [128, 2 * TOK], F32))
    qrawQ = stQ.enter_context(nc.sbuf_tensor("qrawQ", [128, 4 * 1024], BF16))
    rtmpQ = stQ.enter_context(nc.sbuf_tensor("rtmpQ", [128, 1024], F32))
    sq2 = stQ.enter_context(nc.sbuf_tensor("sq2", [128, 1024], BF16))
    wbufQ = stQ.enter_context(nc.sbuf_tensor("wbufQ", [128, 2 * 1024], BF16))
    wstgQ = stQ.enter_context(nc.sbuf_tensor("wstgQ", [128, 2 * 1024], F32))

    r = Rec(C)
    r.emit("gpsimd", lambda g: g.dma_start(
        out=xqb[:, :].rearrange("p (c t) -> p c t", t=TOK),
        in_=xqT.rearrange("(c p) t -> p c t", p=128)), inc="dG", inc_by=16)
    r.emit("gpsimd", lambda g: g.dma_start(out=cq[:, 0:TOK], in_=cosq_in),
           inc="dG", inc_by=16)
    r.emit("gpsimd", lambda g: g.dma_start(out=cq[:, TOK:2 * TOK], in_=sinq_in),
           inc="dG", inc_by=16)
    r.realize(nc, sems)

    r = Rec(C)
    r.emit("gpsimd", lambda g: g.wait_ge(sems["dS"], C["dS"]))
    r.emit("gpsimd", lambda g: g.collective_compute(
        "AllGather", ALU.bypass, replica_groups=[list(range(NCORES))],
        ins=[kc.ap().opt()], outs=[kg.ap().opt()]), inc="cc")
    r.emit("gpsimd", lambda g: g.collective_compute(
        "AllGather", ALU.bypass, replica_groups=[list(range(NCORES))],
        ins=[vc.ap().opt()], outs=[vg.ap().opt()]), inc="cc")
    r.emit("gpsimd", lambda g: g.dma_start(out=mkr[0:1, :], in_=onesf[0:1, 0:16]),
           inc="ccd", inc_by=16)
    r.realize(nc, sems)

    qguard = {}

    def q_block(og):
        stP = ExitStack()
        kps = [stP.enter_context(nc.psum_tensor(f"qps{og}_{i}", [128, 1024], F32))
               for i in range(4)]
        r = Rec(C)
        guard = qguard
        xload = snap("dG")
        prevdone = [snap("sV"), snap("sA")]
        for cg in range(NCH // 2):
            c0 = cg * 2
            w = guard.pop(("ws", cg % 2), [])
            wl = "dS0" if cg % 2 == 0 else "dS1"
            r.emit("sync", lambda s, c0=c0, cg=cg: s.dma_start(
                out=wstgQ[:, (cg % 2) * 1024:(cg % 2) * 1024 + 1024]
                    .rearrange("p (k f) -> p k f", f=512),
                in_=wqT[c0 * 128:(c0 + 2) * 128, og * 512:(og + 1) * 512]
                    .rearrange("(k p) f -> p k f", p=128)),
                waits=w, inc=wl, inc_by=16)
            r.emit("vector", lambda v, cg=cg: v.tensor_copy(
                wbufQ[:, (cg % 2) * 1024:(cg % 2) * 1024 + 1024],
                wstgQ[:, (cg % 2) * 1024:(cg % 2) * 1024 + 1024]),
                waits=[snap(wl)] + guard.pop(("w", cg % 2), []), inc="sV")
            guard[("ws", cg % 2)] = [snap("sV")]
            wdone = snap("sV")

            def mmgrp(t, c0=c0, cg=cg):
                ins = None
                for k in range(2):
                    c = c0 + k
                    for hh in range(4):
                        lt = wbufQ[:, (cg % 2) * 1024 + k * 512 + hh * 128:
                                   (cg % 2) * 1024 + k * 512 + hh * 128 + 128]
                        for half in range(2):
                            ins = t.matmul(
                                kps[hh][:, half * 512:half * 512 + 512], lt,
                                xqb[:, c * 1024 + half * 512:c * 1024 + half * 512 + 512],
                                start=(c == 0), stop=(c == NCH - 1))
                return ins
            r.emit("tensor", mmgrp,
                   waits=[wdone] + ([xload] if c0 == 0 else []) + (prevdone if c0 == 0 else []),
                   inc="sPs")
            guard[("w", cg % 2)] = [snap("sPs")]
        mmdone = snap("sPs")
        cos = cq[:, 0:TOK]
        sin = cq[:, TOK:2 * TOK]
        for hh in range(4):
            r.emit("scalar", lambda sc, hh=hh: sc.copy(
                qrawQ[:, hh * 1024:(hh + 1) * 1024], kps[hh][:, :]),
                waits=[mmdone] if hh == 0 else (), inc="sA")
        evdone = snap("sA")
        for hh in range(4):
            dcol = (og * 4 + hh) * 1024

            def rotmm(t, hh=hh):
                ins = None
                for half in range(2):
                    ins = t.matmul(kps[hh][:, half * 512:half * 512 + 512],
                                   rotmb[:, :],
                                   qrawQ[:, hh * 1024 + half * 512:
                                        hh * 1024 + half * 512 + 512],
                                   start=True, stop=True)
                return ins
            r.emit("tensor", rotmm, waits=[evdone] if hh == 0 else (), inc="sPs")
            rotdone = snap("sPs")

            def rope(v, hh=hh, dcol=dcol):
                v.tensor_mul(rtmpQ[:, :], qrawQ[:, hh * 1024:(hh + 1) * 1024], cos)
                v.tensor_mul(sq2[:, :], kps[hh][:, :], sin)
                return v.tensor_add(qfull[:, dcol:dcol + 1024], rtmpQ[:, :], sq2[:, :])
            r.emit("vector", rope, waits=[rotdone], inc="sV")
        r.realize(nc, sems)
        stP.close()

    for og in range(8):
        q_block(og)

    stP = ExitStack()
    statq = stP.enter_context(nc.psum_tensor("statq", [1, 1024], F32))
    r = Rec(C)
    guard = {}
    for h in range(H):
        r.emit("vector", lambda v, h=h: v.tensor_mul(
            sq2[:, :], qfull[:, h * 1024:(h + 1) * 1024], qfull[:, h * 1024:(h + 1) * 1024]),
            waits=guard.pop("sqfree", []), inc="sV")
        w = snap("sV")
        for half in range(2):
            r.emit("tensor", lambda t, half=half: t.matmul(
                statq[0:1, half * 512:half * 512 + 512], onesb[:, 0:1],
                sq2[:, half * 512:half * 512 + 512], start=True, stop=True),
                waits=[w] + (guard.pop("statps", []) if half == 0 else []), inc="sPs")
        guard["sqfree"] = [snap("sPs")]
        r.emit("scalar", lambda sc: sc.copy(nqrow[0:1, :], statq[0:1, :]),
               waits=[snap("sPs")] + guard.pop("nqfree", []), inc="sA")
        guard["statps"] = [snap("sA")]
        r.emit("vector", lambda v, h=h: v.reduce_max(
            q2row[0:1, h:h + 1], nqrow[0:1, :], axis=AX.X), waits=[snap("sA")], inc="sV")
        guard["nqfree"] = [snap("sV")]
    r.realize(nc, sems)
    stP.close()
    stQ.close()

    stB = ExitStack()
    krows = stB.enter_context(nc.sbuf_tensor("krows", [8, 8 * 1024], BF16))
    k2pad = stB.enter_context(nc.sbuf_tensor("k2pad", [8, 1], F32))
    k2eps = stB.enter_context(nc.psum_tensor("k2eps", [1, 32], F32))
    btp = stB.enter_context(nc.psum_tensor("btp", [128, 32], F32))
    r = Rec(C)
    r.emit("gpsimd", lambda g: g.wait_ge(sems["cc"], 2))
    for rr in range(NCORES):
        r.emit("gpsimd", lambda g, rr=rr: g.dma_start(
            out=krows[0:8, rr * 1024:(rr + 1) * 1024],
            in_=kg.ap()[rr * 1032 + 1024:rr * 1032 + 1032, :]), inc="dG", inc_by=16)
    wd = snap("dG")
    r.emit("vector", lambda v: v.reduce_max(k2pad[0:8, 0:1], krows[0:8, :], axis=AX.X),
           waits=[wd], inc="sV")
    r.emit("tensor", lambda t: t.matmul(k2eps[0:1, :], k2pad[0:8, 0:1],
                                        exp8b[0:8, 0:32], start=True, stop=True),
           waits=[snap("sV")], inc="sPs")
    r.emit("vector", lambda v: v.tensor_mul(srow[0:1, :], q2row[0:1, :],
                                            k2eps[0:1, :]),
           waits=[snap("sPs")], inc="sV")

    r.emit("scalar", lambda sc: sc.activation(
        srow2[0:1, :], srow[0:1, :], AF.Ln, bias=epsA[0:1, 0:1],
        scale=1.0 / 128.0), waits=[snap("sV")], inc="sA")
    r.emit("scalar", lambda sc: sc.activation(
        srow[0:1, :], srow2[0:1, :], AF.Exp, bias=0.0, scale=0.5),
        waits=[snap("sA")], inc="sA")
    r.emit("vector", lambda v: v.tensor_scalar_mul(srow2[0:1, :], srow[0:1, :], -1.0),
           waits=[snap("sA")], inc="sV")
    r.emit("tensor", lambda t: t.matmul(btp[:, :], onesf[0:1, :], srow2[0:1, 0:32],
                                        start=True, stop=True),
           waits=[snap("sV")], inc="sPs")
    r.emit("scalar", lambda sc: sc.copy(btbl[:, :], btp[:, :]),
           waits=[snap("sPs")], inc="sA")
    r.realize(nc, sems)
    stB.close()

    stT = ExitStack()
    kbuf = stT.enter_context(nc.sbuf_tensor("kbuf", [128, 2 * 1024], BF16))
    vbuf = stT.enter_context(nc.sbuf_tensor("vbuf", [128, 2 * 1024], BF16))
    ptb = stT.enter_context(nc.sbuf_tensor("ptb", [128, 2 * 1024], BF16))
    ascr = stT.enter_context(nc.sbuf_tensor("ascr", [128, 1024], F32))
    dacc = stT.enter_context(nc.sbuf_tensor("dacc", [128, 2 * 1024], F32))
    atile = stT.enter_context(nc.sbuf_tensor("atile", [128, 2 * 1024], BF16))
    drow = stT.enter_context(nc.sbuf_tensor("drow", [1, 1024], F32))
    drow2 = stT.enter_context(nc.sbuf_tensor("drow2", [128, 1024], F32))
    sps = [stT.enter_context(nc.psum_tensor(f"sps{i}", [128, 1024], F32)) for i in range(2)]
    outps = stT.enter_context(nc.psum_tensor("outps", [128, 1024], F32))
    denps = stT.enter_context(nc.psum_tensor("denps", [128, 1024], F32))

    r = Rec(C)
    guard = {}
    r.emit("sync", lambda s: s.wait_ge(sems["ccd"], 16))
    eg = 0
    pair = 0
    pending = None

    def emit_pv(pg):
        waits = [pg["wA"]] + ([pg["wV"]] if pg["diag"] else []) + pg["whead"]

        def pv(t, pg=pg):
            ins = None
            for ti, tt in enumerate(pg["tiles"]):
                rhs = ptb[:, (pg["eg"] % 2) * 1024 + ti * pg["slot"]:
                          (pg["eg"] % 2) * 1024 + ti * pg["slot"] + pg["N"]]
                st = pg["first"] and ti == 0
                ins = t.matmul(
                    outps[:, pg["jlo"] * 128:pg["jlo"] * 128 + pg["N"]],
                    vbuf[:, pg["pp"] * 1024 + tt * 128:pg["pp"] * 1024 + tt * 128 + 128],
                    rhs, start=st, stop=False, skip_group_check=True)
            return ins
        r.emit("tensor", pv, waits=waits, inc="sPv")
        guard[("ptb", pg["eg"] % 2)] = [snap("sPv"), pg["dsum"]]
        if pg["tiles"][-1] == 7:
            guard[("kv", pg["pp"])] = [snap("sPv")]

    headguard = [snap("sA")]
    pending_store = None

    def emit_store(ps):
        hs, dfree = ps
        wl = "dG0" if hs % 2 == 0 else "dG1"
        r.emit("gpsimd", lambda g, hs=hs: g.dma_start(
            out=atnd.ap()[hs * 128:(hs + 1) * 128, :],
            in_=atile[:, (hs % 2) * 1024:(hs % 2) * 1024 + 1024]),
            waits=[dfree], inc=wl, inc_by=16)
        guard[("atile", hs % 2)] = [snap(wl)]

    for h in range(H):
        first_pv_of_head = True
        for (g, b) in attn_pairs():
            pp = pair % 2
            pw = guard.pop(("kv", pp), [])
            kv = h // 4
            wl = "dS0" if pp == 0 else "dS1"
            r.emit("sync", lambda s, b=b, kv=kv, pp=pp: s.dma_start(
                out=kbuf[:, pp * 1024:pp * 1024 + 1024],
                in_=kg.ap()[b * 1032 + kv * 128:b * 1032 + (kv + 1) * 128, :]),
                waits=pw, inc=wl, inc_by=16)
            r.emit("sync", lambda s, b=b, kv=kv, pp=pp: s.dma_start(
                out=vbuf[:, pp * 1024:(pp + 1) * 1024]
                    .rearrange("p (t f) -> p t f", f=128),
                in_=vg.ap()[b * 1024:(b + 1) * 1024, kv * 128:(kv + 1) * 128]
                    .rearrange("(t p) f -> p t f", p=128)), inc=wl, inc_by=16)
            kvdone = snap(wl)
            diag = (g == 0) or (b >= 4)
            for gi, gr in enumerate(attn_groups_of_pair(g, b)):
                tiles, slot, span, N, jlo, jhi = (gr["tiles"], gr["slot"], gr["span"],
                                                  gr["N"], gr["jlo"], gr["jhi"])
                sw = ([kvdone] + guard.pop(("sps", eg % 2), [])
                      + ([snap("sA")] if eg == 0 else []))

                def sgrp(t, tiles=tiles, slot=slot, N=N, jlo=jlo, h=h, pp=pp, eg=eg):
                    ins = None
                    for ti, tt in enumerate(tiles):
                        ins = t.matmul(
                            sps[eg % 2][:, ti * slot:ti * slot + N],
                            kbuf[:, pp * 1024 + tt * 128:pp * 1024 + tt * 128 + 128],
                            qfull[:, h * 1024 + jlo * 128:h * 1024 + jlo * 128 + N],
                            start=True, stop=True)
                    return ins
                r.emit("tensor", sgrp, waits=sw, inc="sPs")
                sdone = snap("sPs")
                ew = [sdone] + guard.pop(("ptb", eg % 2), [])
                r.emit("scalar", lambda sc, eg=eg, span=span, h=h: sc.activation(
                    ptb[:, (eg % 2) * 1024:(eg % 2) * 1024 + span],
                    sps[eg % 2][:, 0:span], AF.Exp,
                    bias=btbl[:, h:h + 1], scale=ISQ), waits=ew, inc="sA")
                guard[("sps", eg % 2)] = [snap("sA")]
                wA = snap("sA")
                wV = None
                if diag:
                    def mgrp(v, tiles=tiles, slot=slot, eg=eg):
                        ins = None
                        for ti, tt in enumerate(tiles):
                            sl = ptb[:, (eg % 2) * 1024 + ti * slot:
                                     (eg % 2) * 1024 + ti * slot + 128]
                            ins = v.tensor_mul(sl, sl,
                                               masks[:, tt * 128:(tt + 1) * 128])
                        return ins
                    r.emit("vector", mgrp, waits=[wA], inc="sV")
                    wV = snap("sV")
                    if pending_store is not None:
                        emit_store(pending_store)
                        pending_store = None
                dw = [wA] + ([wV] if diag else []) + guard.pop(("dacc", h % 2), [])

                def dadd(v, tiles=tiles, slot=slot, N=N, jlo=jlo, eg=eg, h=h,
                         first=(b == 0 and gi == 0)):
                    ins = None
                    dst = dacc[:, (h % 2) * 1024 + jlo * 128:
                               (h % 2) * 1024 + jlo * 128 + N]
                    for ti in range(len(tiles)):
                        sl = ptb[:, (eg % 2) * 1024 + ti * slot:
                                 (eg % 2) * 1024 + ti * slot + N]
                        if first and ti == 0:
                            ins = v.tensor_copy(dst, sl)
                        else:
                            ins = v.tensor_add(dst, dst, sl)
                    return ins
                r.emit("vector", dadd, waits=dw, inc="sV")
                dsum = snap("sV")
                if pending is not None:
                    emit_pv(pending)
                pending = dict(tiles=tiles, slot=slot, N=N, jlo=jlo,
                               diag=diag, pp=pp, eg=eg, wA=wA, wV=wV,
                               first=(b == 0 and gi == 0), dsum=dsum,
                               whead=headguard if first_pv_of_head else [])
                first_pv_of_head = False
                eg += 1
            pair += 1
        emit_pv(pending)
        pending = None
        pvall = snap("sPv")
        dnw = [snap("sV")] + guard.pop("denps", [])

        def denmm(t, h=h):
            ins = None
            for half in range(2):
                ins = t.matmul(denps[0:1, half * 512:half * 512 + 512],
                               onesf[:, 0:1],
                               dacc[:, (h % 2) * 1024 + half * 512:
                                    (h % 2) * 1024 + half * 512 + 512],
                               start=True, stop=True, skip_group_check=True)
            return ins
        r.emit("tensor", denmm, waits=dnw, inc="sPs")
        guard[("dacc", h % 2)] = [snap("sPs")]
        dmmdone = snap("sPs")
        r.emit("scalar", lambda sc: sc.copy(ascr[:, :], outps[:, :]),
               waits=[pvall] + guard.pop("ascr", []), inc="sA")
        outfree = snap("sA")
        r.emit("scalar", lambda sc: sc.copy(drow[0:1, :], denps[0:1, :]),
               waits=[dmmdone], inc="sA")
        rw = snap("sA")

        def bcast(t):
            ins = None
            for half in range(2):
                ins = t.matmul(denps[:, half * 512:half * 512 + 512],
                               onesf[0:1, 0:128],
                               drow[0:1, half * 512:half * 512 + 512],
                               start=True, stop=True, skip_group_check=True)
            return ins
        r.emit("tensor", bcast, waits=[rw], inc="sPs")
        r.emit("vector", lambda v: v.reciprocal(drow2[:, :], denps[:, :]),
               waits=[snap("sPs")] + guard.pop("drow2", []), inc="sV")
        rcpw = snap("sV")
        fw = ([rcpw, outfree] + guard.pop(("atile", h % 2), []))
        r.emit("vector", lambda v, h=h: v.tensor_mul(
            atile[:, (h % 2) * 1024:(h % 2) * 1024 + 1024],
            ascr[:, :], drow2[:, :]), waits=fw, inc="sV")
        denfree = snap("sV")
        guard["drow2"] = [denfree]
        guard["ascr"] = [denfree]
        pending_store = (h, denfree)
        guard["denps"] = [denfree]
        headguard = [outfree]
    emit_store(pending_store)
    pending_store = None
    g0, g1 = snap("dG0"), snap("dG1")
    r.emit("gpsimd", lambda g: g.wait_ge(sems["dG0"], g0[1]))
    r.emit("gpsimd", lambda g: g.wait_ge(sems["dG1"], g1[1]))
    r.realize(nc, sems)
    stT.close()
    stQA.close()

    stL1 = ExitStack()
    ln1full = stL1.enter_context(nc.sbuf_tensor("ln1full", [128, 8 * 4096], BF16))
    stO = ExitStack()
    oacc = stO.enter_context(nc.sbuf_tensor("oacc", [128, 4 * 4096], F32))
    atb = stO.enter_context(nc.sbuf_tensor("atb", [128, 4 * 512], BF16))
    wob = stO.enter_context(nc.sbuf_tensor("wob", [128, 4 * 1024], BF16))
    xqq = stO.enter_context(nc.sbuf_tensor("xqq", [128, 2 * 1024], F32))
    sqf = stO.enter_context(nc.sbuf_tensor("sqf", [128, 4096], F32))
    ln1f = stO.enter_context(nc.sbuf_tensor("ln1f", [128, 4096], F32))
    ostat = stO.enter_context(nc.sbuf_tensor("ostat", [128, 28], F32))
    psO = [stO.enter_context(nc.psum_tensor(f"psO{i}", [128, 1024], F32)) for i in range(4)]

    r = Rec(C)
    guard = {}
    n = 0
    for ttg in range(2):
        for hq in range(4):
            for ocg in range(NCH // 2):
                oc0 = ocg * 2
                wl = "dG0" if n % 2 == 0 else "dG1"
                r.emit("gpsimd", lambda g, hq=hq, oc0=oc0, n=n: g.dma_start(
                    out=wob[:, (n % 2) * 2048:(n % 2) * 2048 + 2048]
                        .rearrange("p (k f) -> p k f", f=1024),
                    in_=woT[oc0 * 128:(oc0 + 2) * 128, hq * 1024:(hq + 1) * 1024]
                        .rearrange("(k p) f -> p k f", p=128)),
                    waits=guard.pop(("wob", n % 2), []), inc=wl, inc_by=16)
                wG = snap(wl)
                sl = "dS0" if n % 2 == 0 else "dS1"
                r.emit("sync", lambda s, ttg=ttg, oc0=oc0, n=n: s.dma_start(
                    out=atb[:, (n % 2) * 1024:(n % 2) * 1024 + 1024]
                        .rearrange("p (k f) -> p k f", f=512),
                    in_=atnd.ap()[oc0 * 128:(oc0 + 2) * 128, ttg * 512:(ttg + 1) * 512]
                        .rearrange("(k p) f -> p k f", p=128)),
                    waits=guard.pop(("atb", n % 2), []), inc=sl, inc_by=16)
                wS = snap(sl)
                mmw = [wG, wS] + (guard.pop("psO", []) if oc0 == 0 else [])

                def mmo(t, oc0=oc0, n=n):
                    ins = None
                    for k in range(2):
                        oc = oc0 + k
                        for tt in range(4):
                            for half in range(2):
                                ins = t.matmul(
                                    psO[tt][:, half * 512:half * 512 + 512],
                                    atb[:, (n % 2) * 1024 + k * 512 + tt * 128:
                                        (n % 2) * 1024 + k * 512 + tt * 128 + 128],
                                    wob[:, (n % 2) * 2048 + k * 1024 + half * 512:
                                        (n % 2) * 2048 + k * 1024 + half * 512 + 512],
                                    start=(oc == 0), stop=(oc == NCH - 1))
                    return ins
                r.emit("tensor", mmo, waits=mmw, inc="sPs")
                guard[("wob", n % 2)] = [snap("sPs")]
                guard[("atb", n % 2)] = [snap("sPs")]
                n += 1
            mmdone = snap("sPs")
            addsnap = None
            for tt in range(4):
                m = (ttg * 4 + hq) * 4 + tt
                sl = "dS2" if m % 2 == 0 else "dS3"
                r.emit("sync", lambda s, ttg=ttg, tt=tt, hq=hq, m=m: s.dma_start(
                    out=xqq[:, (m % 2) * 1024:(m % 2) * 1024 + 1024],
                    in_=xq[(ttg * 4 + tt) * 128:(ttg * 4 + tt + 1) * 128,
                           hq * 1024:(hq + 1) * 1024]),
                    waits=guard.pop(("xqq", m % 2), []), inc=sl, inc_by=16)
                wS = snap(sl)
                r.emit("vector", lambda v, tt=tt, hq=hq, m=m: v.tensor_add(
                    oacc[:, tt * 4096 + hq * 1024:tt * 4096 + hq * 1024 + 1024],
                    psO[tt][:, :], xqq[:, (m % 2) * 1024:(m % 2) * 1024 + 1024]),
                    waits=[mmdone, wS], inc="sV")
                guard[("xqq", m % 2)] = [snap("sV")]
            guard["psO"] = [snap("sV")]
        SR, SF, wln = ln_chain(r, ostat, 4,
                               [oacc[:, tt * 4096:(tt + 1) * 4096]
                                for tt in range(4)], sqf)
        for tt in range(4):
            k = ttg * 4 + tt
            a = oacc[:, tt * 4096:(tt + 1) * 4096]
            r.emit("vector", lambda v, a=a, tt=tt, SR=SR, SF=SF: v.tensor_scalar(
                ln1f[:, :], a, SR[:, tt:tt + 1], SF[:, tt:tt + 1],
                ALU.mult, ALU.add),
                waits=[wln] + guard.pop("ln1f", []), inc="sV")
            nrm = snap("sV")
            r.emit("sync", lambda s, k=k: s.dma_start(
                out=r2d.ap()[k * 128:(k + 1) * 128, :],
                in_=ln1f[:, :]),
                waits=[nrm], inc="dO", inc_by=16)
            r.emit("vector", lambda v, k=k: v.tensor_copy(
                ln1full[:, k * 4096:(k + 1) * 4096],
                ln1f[:, :]), waits=[nrm], inc="sV")
            guard["ln1f"] = [snap("dO"), snap("sV")]
    r.emit("sync", lambda s: s.wait_ge(sems["dO"], C["dO"]))
    r.realize(nc, sems)
    stO.close()

    stL = ExitStack()
    xt2st = stL.enter_context(nc.sbuf_tensor("xt2st", [128, 2 * 4096], BF16))
    ptL = [stL.enter_context(nc.psum_tensor(f"ptL{i}", [128, 128], BF16)) for i in range(2)]
    r = Rec(C)
    guard = {}
    for tt in range(8):
        for c in range(NCH):
            nn = tt * NCH + c
            r.emit("tensor", lambda t, tt=tt, c=c, nn=nn: t.transpose(
                ptL[nn % 2][:, :],
                ln1full[:, tt * 4096 + c * 128:tt * 4096 + c * 128 + 128], ident[:, :]),
                waits=guard.pop(("ptL", nn % 2), []) + ([snap("sV")] if nn == 0 else []),
                inc="sT")
            r.emit("vector", lambda v, tt=tt, c=c, nn=nn: v.tensor_copy(
                xt2st[:, (tt % 2) * 4096 + c * 128:(tt % 2) * 4096 + c * 128 + 128],
                ptL[nn % 2][:, :]),
                waits=[snap("sT")] + (guard.pop(("xt2st", tt % 2), []) if c == 0 else []),
                inc="sV")
            guard[("ptL", nn % 2)] = [snap("sV")]
        sl = "dS2" if tt % 2 == 0 else "dS3"
        r.emit("sync", lambda s, tt=tt: s.dma_start(
            out=xt2d.ap().rearrange("(c p) t -> p c t", p=128)[:, :, tt * 128:(tt + 1) * 128],
            in_=xt2st[:, (tt % 2) * 4096:(tt % 2) * 4096 + 4096]
                .rearrange("p (c f) -> p c f", f=128)),
            waits=[snap("sV")], inc=sl, inc_by=16)
        guard[("xt2st", tt % 2)] = [snap(sl)]
    r.realize(nc, sems)
    stL.close()
    stL1.close()

    stMid = ExitStack()
    midsb = stMid.enter_context(nc.sbuf_tensor("midsb", [128, FT * 512], BF16))
    stF = ExitStack()
    xt2b = stF.enter_context(nc.sbuf_tensor("xt2b", [128, NCH * 1024], BF16))
    gwb = stF.enter_context(nc.sbuf_tensor("gwb", [128, 2 * 4096], BF16))
    uwb = stF.enter_context(nc.sbuf_tensor("uwb", [128, 2 * 4096], BF16))
    gsil = stF.enter_context(nc.sbuf_tensor("gsil", [128, 2 * 1024], F32))
    mid1 = stF.enter_context(nc.sbuf_tensor("mid1", [128, 1024], F32))
    midb2 = stF.enter_context(nc.sbuf_tensor("midb2", [128, 2 * 512], BF16))
    gps = [stF.enter_context(nc.psum_tensor(f"gps{i}", [128, 1024], F32)) for i in range(2)]
    ups = [stF.enter_context(nc.psum_tensor(f"ups{i}", [128, 1024], F32)) for i in range(2)]

    r = Rec(C)
    guard = {}
    r.emit("gpsimd", lambda g: g.dma_start(
        out=xt2b[:, :].rearrange("p (c t) -> p c t", t=TOK),
        in_=xt2d.ap().rearrange("(c p) t -> p c t", p=128)),
        waits=[snap("dS2"), snap("dS3")], inc="dG", inc_by=16)
    xload = snap("dG")
    for ft in range(FT):
        gw = guard.pop(("wf", ft % 2), []) + ([xload] if ft < 2 else [])
        wl = "dG0" if ft % 2 == 0 else "dG1"
        r.emit("gpsimd", lambda g, ft=ft: g.dma_start(
            out=gwb[:, (ft % 2) * 4096:(ft % 2) * 4096 + 4096]
                .rearrange("p (c f) -> p c f", f=128),
            in_=gwT.rearrange("(c p) f -> p c f", p=128)[:, :, ft * 128:(ft + 1) * 128]),
            waits=gw, inc=wl, inc_by=16)
        r.emit("gpsimd", lambda g, ft=ft: g.dma_start(
            out=uwb[:, (ft % 2) * 4096:(ft % 2) * 4096 + 4096]
                .rearrange("p (c f) -> p c f", f=128),
            in_=uwT.rearrange("(c p) f -> p c f", p=128)[:, :, ft * 128:(ft + 1) * 128]),
            inc=wl, inc_by=16)
        wG = snap(wl)
        mmw = [wG] + guard.pop(("gps", ft % 2), []) + guard.pop(("ups", ft % 2), [])

        def mmf(t, ft=ft):
            ins = None
            for c in range(NCH):
                for (ps, wb) in ((gps, gwb), (ups, uwb)):
                    lt = wb[:, (ft % 2) * 4096 + c * 128:(ft % 2) * 4096 + c * 128 + 128]
                    for half in range(2):
                        ins = t.matmul(
                            ps[ft % 2][:, half * 512:half * 512 + 512], lt,
                            xt2b[:, c * 1024 + half * 512:c * 1024 + half * 512 + 512],
                            start=(c == 0), stop=(c == NCH - 1))
            return ins
        r.emit("tensor", mmf, waits=mmw, inc="sPs")
        guard[("wf", ft % 2)] = [snap("sPs")]
        mmdone = snap("sPs")
        r.emit("scalar", lambda sc, ft=ft: sc.activation(
            gsil[:, (ft % 2) * 1024:(ft % 2) * 1024 + 1024], gps[ft % 2][:, :], AF.Silu,
            bias=0.0, scale=gsb[:, ft:ft + 1]),
            waits=[mmdone] + guard.pop(("gsil", ft % 2), []), inc="sA")
        guard[("gps", ft % 2)] = [snap("sA")]
        wA = snap("sA")

        def midop(v, ft=ft):
            v.tensor_mul(mid1[:, :], ups[ft % 2][:, :],
                         gsil[:, (ft % 2) * 1024:(ft % 2) * 1024 + 1024])
            v.tensor_scalar(midsb[:, ft * 512:(ft + 1) * 512],
                            mid1[:, 0:512], usb[:, ft:ft + 1], None, ALU.mult)
            return v.tensor_scalar(midb2[:, (ft % 2) * 512:(ft % 2) * 512 + 512],
                                   mid1[:, 512:1024], usb[:, ft:ft + 1], None, ALU.mult)
        r.emit("vector", midop, waits=[wA] + guard.pop(("midb", ft % 2), []), inc="sV")
        guard[("ups", ft % 2)] = [snap("sV")]
        guard[("gsil", ft % 2)] = [snap("sV")]
        sl = "dS0" if ft % 2 == 0 else "dS1"
        r.emit("sync", lambda s, ft=ft: s.dma_start(
            out=midd.ap()[ft * 128:(ft + 1) * 128, :],
            in_=midb2[:, (ft % 2) * 512:(ft % 2) * 512 + 512]),
            waits=[snap("sV")], inc=sl, inc_by=16)
        guard[("midb", ft % 2)] = [snap(sl)]
    r.emit("sync", lambda s: s.wait_ge(sems["dS0"], C["dS0"]))
    r.emit("sync", lambda s: s.wait_ge(sems["dS1"], C["dS1"]))
    r.realize(nc, sems)
    stF.close()

    dstage = stMid.enter_context(nc.sbuf_tensor("dstage", [128, NCH * 1024], BF16))
    dwb = stMid.enter_context(nc.sbuf_tensor("dwb", [128, 2 * 2048], BF16))
    midb = stMid.enter_context(nc.sbuf_tensor("midb", [128, 2 * 2048], BF16))
    stDp = ExitStack()
    dps = [stDp.enter_context(nc.psum_tensor(f"dps{i}", [128, 1024], F32)) for i in range(4)]

    r = Rec(C)
    guard = {}
    msv = snap("sV")
    chunks = [4] * 21 + [2]
    for dg in range(8):
        fc0 = 0
        for nn0, ck in enumerate(chunks):
            fc0 = sum(chunks[:nn0])
            nn = dg * len(chunks) + nn0
            wl = "dG0" if nn % 2 == 0 else "dG1"
            r.emit("gpsimd", lambda g, dg=dg, fc0=fc0, nn=nn, ck=ck: g.dma_start(
                out=dwb[:, (nn % 2) * 2048:(nn % 2) * 2048 + ck * 512]
                    .rearrange("p (k f) -> p k f", f=512),
                in_=dwT[fc0 * 128:(fc0 + ck) * 128, dg * 512:(dg + 1) * 512]
                    .rearrange("(k p) f -> p k f", p=128)),
                waits=guard.pop(("dwb", nn % 2), []), inc=wl, inc_by=16)
            wG = snap(wl)
            sl = "dS2" if nn % 2 == 0 else "dS3"
            r.emit("sync", lambda s, fc0=fc0, nn=nn, ck=ck: s.dma_start(
                out=midb[:, (nn % 2) * 2048:(nn % 2) * 2048 + ck * 512]
                    .rearrange("p (k f) -> p k f", f=512),
                in_=midd.ap()[fc0 * 128:(fc0 + ck) * 128, :]
                    .rearrange("(k p) f -> p k f", p=128)),
                waits=guard.pop(("midb", nn % 2), []), inc=sl, inc_by=16)
            wS = snap(sl)
            mmw = ([wG, wS] + (guard.pop("dps", []) if fc0 == 0 else [])
                   + ([msv] if nn == 0 else []))

            def mmd(t, fc0=fc0, nn=nn, ck=ck):
                ins = None
                for k in range(ck):
                    fc = fc0 + k
                    for dt in range(4):
                        lt = dwb[:, (nn % 2) * 2048 + k * 512 + dt * 128:
                                 (nn % 2) * 2048 + k * 512 + dt * 128 + 128]
                        ins = t.matmul(
                            dps[dt][:, 0:512], lt, midsb[:, fc * 512:(fc + 1) * 512],
                            start=(fc == 0), stop=(fc == FT - 1))
                        ins = t.matmul(
                            dps[dt][:, 512:1024], lt,
                            midb[:, (nn % 2) * 2048 + k * 512:
                                 (nn % 2) * 2048 + k * 512 + 512],
                            start=(fc == 0), stop=(fc == FT - 1))
                return ins
            r.emit("tensor", mmd, waits=mmw, inc="sPs")
            guard[("dwb", nn % 2)] = [snap("sPs")]
            guard[("midb", nn % 2)] = [snap("sPs")]
        mmdone = snap("sPs")
        last = None
        for dt in range(4):
            last = r.emit("scalar", lambda sc, dg=dg, dt=dt: sc.activation(
                dstage[:, (dg * 4 + dt) * 1024:(dg * 4 + dt) * 1024 + 1024],
                dps[dt][:, :], AF.Copy, bias=0.0, scale=dsb[:, dg * 4 + dt:dg * 4 + dt + 1]),
                waits=[mmdone] if dt == 0 else (), inc="sA")
        guard["dps"] = [snap("sA")]
    r.realize(nc, sems)
    stDp.close()

    stZ = ExitStack()
    ftok = stZ.enter_context(nc.sbuf_tensor("ftok", [128, 4096], F32))
    r2b = stZ.enter_context(nc.sbuf_tensor("r2b", [128, 4096], F32))
    zstat = stZ.enter_context(nc.sbuf_tensor("zstat", [128, 8], F32))
    ptZ = [stZ.enter_context(nc.psum_tensor(f"ptZ{i}", [128, 128], BF16)) for i in range(2)]

    r = Rec(C)
    guard = {}
    for tt in range(8):
        sl = "dS0" if tt % 2 == 0 else "dS1"
        r.emit("sync", lambda s, tt=tt: s.dma_start(
            out=r2b[:, :], in_=r2d.ap()[tt * 128:(tt + 1) * 128, :]),
            waits=guard.pop("r2b", []), inc=sl, inc_by=16)
        wS = snap(sl)
        for c in range(NCH):
            nn = tt * NCH + c
            r.emit("tensor", lambda t, tt=tt, c=c, nn=nn: t.transpose(
                ptZ[nn % 2][:, :],
                dstage[:, c * 1024 + tt * 128:c * 1024 + tt * 128 + 128], ident[:, :]),
                waits=guard.pop(("ptZ", nn % 2), []) + ([snap("sA")] if nn == 0 else []),
                inc="sT")
            r.emit("scalar", lambda sc, tt=tt, c=c, nn=nn: sc.copy(
                ftok[:, c * 128:c * 128 + 128],
                ptZ[nn % 2][:, :]),
                waits=[snap("sT")] + (guard.pop("ftok", []) if c == 0 else []),
                inc="sA")
            guard[("ptZ", nn % 2)] = [snap("sA")]
        evdone = snap("sA")
        fa = ftok[:, :]
        r.emit("vector", lambda v, fa=fa: v.tensor_add(fa, fa, r2b[:, :]),
               waits=[wS, evdone], inc="sV")
        SR, SF, wln = ln_chain(r, zstat, 1, [fa], r2b,
                               first_waits=[snap("sV")])
        guard["r2b"] = [wln]
        r.emit("vector", lambda v, fa=fa, SR=SR, SF=SF: v.tensor_scalar(
            fa, fa, SR[:, 0:1], SF[:, 0:1], ALU.mult, ALU.add),
            waits=[wln], inc="sV")
        so = "dS2" if tt % 2 == 0 else "dS3"
        r.emit("sync", lambda s, tt=tt: s.dma_start(
            out=out[tt * 128:(tt + 1) * 128, :],
            in_=ftok[:, :]),
            waits=[snap("sV")], inc=so, inc_by=16)
        guard["ftok"] = [snap(so)]
    r.emit("sync", lambda s: s.wait_ge(sems["dS2"], C["dS2"]))
    r.emit("sync", lambda s: s.wait_ge(sems["dS3"], C["dS3"]))
    r.realize(nc, sems)
    stZ.close()
    stMid.close()
    glob.close()
    return nc



_ROPE_THETA = 10000.0


def _rope_tables(pos):
    freqs = np.exp(-np.arange(0, 128, 2, dtype=np.float32)
                   * np.float32(math.log(_ROPE_THETA) / 128.0))
    ang = pos.astype(np.float32)[None, :] * freqs[:, None]
    cos = np.cos(ang).astype(np.float32)
    sin = np.sin(ang).astype(np.float32)
    return (np.ascontiguousarray(np.vstack([cos, cos])),
            np.ascontiguousarray(np.vstack([sin, sin])))


def _rot_matrix():
    m = np.zeros((128, 128), dtype=BF16NP)
    for d in range(64):
        m[64 + d, d] = -1
        m[d, 64 + d] = 1
    return m


def _stage_inputs(inputs):
    x = np.asarray(inputs["L0_attn_hidden_states"], np.float32)[0]
    Wq = np.asarray(inputs["L0_attn_Wq"], np.float32)
    Wk = np.asarray(inputs["L0_attn_Wk"], np.float32)
    Wv = np.asarray(inputs["L0_attn_Wv"], np.float32)
    Wo = np.asarray(inputs["L0_attn_Wo"], np.float32)
    gw = np.asarray(inputs["L0_ffn_gate_w"], np.int32)
    uw = np.asarray(inputs["L0_ffn_up_w"], np.int32)
    dw = np.asarray(inputs["L0_ffn_down_w"], np.int32)
    gs = np.asarray(inputs["L0_ffn_gate_s"], np.float32)
    us = np.asarray(inputs["L0_ffn_up_s"], np.float32)
    ds = np.asarray(inputs["L0_ffn_down_s"], np.float32)

    shared = {
        "wqT": np.ascontiguousarray(Wq.T),
        "wkT": np.ascontiguousarray(Wk.T),
        "wvT": np.ascontiguousarray(Wv.T),
        "woT": np.ascontiguousarray(Wo.T),
        "gwT": np.ascontiguousarray(gw.T),
        "uwT": np.ascontiguousarray(uw.T),
        "dwT": np.ascontiguousarray(dw.T),
        "gs": np.ascontiguousarray(gs.reshape(FT, 128).T),
        "us": np.ascontiguousarray(us.reshape(FT, 128).T),
        "ds": np.ascontiguousarray(ds.reshape(NCH, 128).T),
        "ident": np.eye(128, dtype=BF16NP),
        "identf": np.eye(128, dtype=np.float32),
        "onesi": np.ones((128, 128), np.float32),
        "rotm": _rot_matrix(),
        "exp8": np.ascontiguousarray(
            (np.arange(32)[None, :] // 4 == np.arange(8)[:, None]).astype(np.float32)),
    }
    xr = x.reshape(64, 128, D)
    in_maps = []
    for i in range(NCORES):
        xqi = np.ascontiguousarray(xr[i::8].reshape(TOK, D))
        xkvi = x[i * TOK:(i + 1) * TOK]
        pos_q = (np.arange(8)[:, None] * 1024 + i * 128 + np.arange(128)[None, :]).reshape(-1)
        pos_kv = np.arange(i * TOK, (i + 1) * TOK)
        cosq, sinq = _rope_tables(pos_q)
        coskv, sinkv = _rope_tables(pos_kv)
        m = np.zeros((128, 1024), dtype=BF16NP)
        for t in range(8):
            if t < i:
                m[:, t * 128:(t + 1) * 128] = 1
            elif t == i:
                m[:, t * 128:(t + 1) * 128] = np.triu(np.ones((128, 128))).astype(BF16NP)
        d = dict(shared)
        d.update({
            "xq": xqi,
            "xqT": np.ascontiguousarray(xqi.T),
            "xkvT": np.ascontiguousarray(xkvi.T),
            "cosq": cosq, "sinq": sinq, "coskv": coskv, "sinkv": sinkv,
            "masks": m,
        })
        in_maps.append(d)
    return in_maps


_NC_CACHE = None


def kernel(**inputs):
    global _NC_CACHE
    in_maps = _stage_inputs(inputs)
    if _NC_CACHE is None:
        _NC_CACHE = build_program()
    res = run_bass_kernel_spmd(_NC_CACHE, in_maps, core_ids=list(range(NCORES)))
    arr = np.stack([res.results[i]["out"] for i in range(NCORES)])
    arr = arr.reshape(NCORES, 8, 128, D).transpose(1, 0, 2, 3).reshape(1, S, D)
    return np.ascontiguousarray(arr)


if __name__ == "__main__":
    import time
    t0 = time.time()
    nc = build_program()
    print(f"build ok in {time.time() - t0:.1f}s; instructions: {nc.n_instructions()}")
